# version 03
# baseline (speedup 1.0000x reference)
import math
import numpy as np
import ml_dtypes
from contextlib import ExitStack

import concourse.bass as bass
import concourse.mybir as mybir
from concourse.alu_op_type import AluOpType
from concourse.bass_utils import run_bass_kernel_spmd

F32 = mybir.dt.float32
BF16 = mybir.dt.bfloat16
AF = mybir.ActivationFunctionType
BF = ml_dtypes.bfloat16

NCORES = 8
T = 8192
D = 2048
NH = 16
HDIM = 128
DH = 8192
NCH = 16
TPC = 1024
SPAN = 512
QSCALE = 1.0 / math.sqrt(HDIM)
LN_EPS = 1e-5

SLOPES = [2.0 ** (-0.5 * (i + 1)) for i in range(NH)]


def span_starts(c):
    return 512 * c, 7680 - 512 * c


class Sem:
    def __init__(self, handle):
        self.h = handle
        self.n = 0

    def bump(self, k=1):
        self.n += k
        return self.n


class Plan:
    ENGS = ("tensor", "vector", "scalar", "gpsimd", "sync")

    def __init__(self):
        self.ops = {e: [] for e in self.ENGS}

    def add(self, eng, fn):
        self.ops[eng].append(fn)

    def emit(self, block):
        plan = self

        @block.sync
        def _(e):
            for fn in plan.ops["sync"]:
                fn(e)

        @block.tensor
        def _(e):
            for fn in plan.ops["tensor"]:
                fn(e)

        @block.vector
        def _(e):
            for fn in plan.ops["vector"]:
                fn(e)

        @block.scalar
        def _(e):
            for fn in plan.ops["scalar"]:
                fn(e)

        @block.gpsimd
        def _(e):
            for fn in plan.ops["gpsimd"]:
                fn(e)


def chunked_colmajor(ap, nch, per):
    return ap.rearrange("(c p) t -> p (c t)", c=nch)



def emit_layernorm(p, sems, bufs, nch, t_stride, t0, span, src_X, dst_XN, dst_dtype_note=""):
    s_v, s_a, s_t, s_g, s_gx = sems["v"], sems["a"], sems["t"], sems["g"], sems["gx"]
    XBF, XSQ = bufs["XBF"], bufs["XSQ"]
    ONESC, ONESR, EPS1 = bufs["ONESC"], bufs["ONESR"], bufs["EPS1"]
    STATR, AB, ABC, TMP = bufs["STATR"], bufs["AB"], bufs["ABC"], bufs["TMP"]
    PS_MU, PS_SQ, PS_AB = bufs["PS_MU"], bufs["PS_SQ"], bufs["PS_AB"]
    xbf_ready = bufs["xbf_ready"]
    consts_ready = bufs["consts_ready"]
    x_ready = bufs["x_ready"]

    def sq_f(e):
        e.wait_ge(s_gx.h, xbf_ready)
        for i in range(nch):
            inst = e.tensor_tensor(XSQ[:, i * span:(i + 1) * span], XBF[:, i * span:(i + 1) * span],
                                   XBF[:, i * span:(i + 1) * span], AluOpType.mult)
        inst.then_inc(s_v.h, 1)
    p.add("vector", sq_f)
    sq_done = s_v.bump()

    def stats_mm(e):
        e.wait_ge(s_v.h, sq_done)
        e.wait_ge(s_g.h, consts_ready)
        for i in range(nch):
            e.matmul(PS_MU[0:1, :], ONESC[:], XBF[:, i * span:(i + 1) * span],
                     start=(i == 0), stop=(i == nch - 1))
        for i in range(nch):
            inst = e.matmul(PS_SQ[0:1, :], ONESC[:], XSQ[:, i * span:(i + 1) * span],
                            start=(i == 0), stop=(i == nch - 1))
        inst.then_inc(s_t.h, 1)
    p.add("tensor", stats_mm)
    stats_done = s_t.bump()
    bufs["stats_done_out"] = stats_done

    ch_v = sems["chv"]

    def stats_fin(e):
        e.wait_ge(s_t.h, stats_done)
        e.tensor_scalar(STATR[0:1, 0:span], PS_MU[0:1, :], 1.0 / D, None, AluOpType.mult)
        e.tensor_scalar(STATR[0:1, span:2 * span], PS_SQ[0:1, :], 1.0 / D, None,
                        AluOpType.mult).then_inc(ch_v.h, 1)
        e.wait_ge(ch_v.h, stats_fin.c1)
        e.tensor_tensor(AB[0:1, span:2 * span], STATR[0:1, 0:span], STATR[0:1, 0:span],
                        AluOpType.mult).then_inc(ch_v.h, 1)
        e.wait_ge(ch_v.h, stats_fin.c2)
        e.tensor_tensor(AB[0:1, span:2 * span], STATR[0:1, span:2 * span], AB[0:1, span:2 * span],
                        AluOpType.subtract).then_inc(s_v.h, 1)
    stats_fin.c1 = ch_v.bump()
    stats_fin.c2 = ch_v.bump()
    p.add("vector", stats_fin)
    var_done = s_v.bump()

    ch_a = sems["cha"]

    def rstd_f(e):
        e.wait_ge(s_v.h, var_done)
        e.activation(AB[0:1, 0:span], AB[0:1, span:2 * span], AF.Ln, bias=EPS1[0:1, 0:1],
                     scale=1.0).then_inc(ch_a.h, 1)
        e.wait_ge(ch_a.h, rstd_f.c1)
        e.activation(AB[0:1, 0:span], AB[0:1, 0:span], AF.Exp, scale=-0.5).then_inc(s_a.h, 1)
    rstd_f.c1 = ch_a.bump()
    p.add("scalar", rstd_f)
    rstd_done = s_a.bump()

    def brow_f(e):
        e.wait_ge(s_a.h, rstd_done)
        e.tensor_tensor(AB[0:1, span:2 * span], STATR[0:1, 0:span], AB[0:1, 0:span],
                        AluOpType.mult).then_inc(ch_v.h, 1)
        e.wait_ge(ch_v.h, brow_f.c1)
        e.tensor_scalar(AB[0:1, span:2 * span], AB[0:1, span:2 * span], -1.0, None,
                        AluOpType.mult).then_inc(s_v.h, 1)
    brow_f.c1 = ch_v.bump()
    p.add("vector", brow_f)
    brow_done = s_v.bump()

    def bcast_mm(e):
        e.wait_ge(s_v.h, brow_done)
        e.matmul(PS_AB[:, 0:span], ONESR[:], AB[0:1, 0:span], start=True, stop=True)
        e.matmul(PS_AB[:, span:2 * span], ONESR[:], AB[0:1, span:2 * span],
                 start=True, stop=True).then_inc(s_t.h, 1)
    p.add("tensor", bcast_mm)
    bcast_done = s_t.bump()

    def evac_bcast(e):
        e.wait_ge(s_t.h, bcast_done)
        e.tensor_copy(ABC[:, 0:span], PS_AB[:, 0:span])
        e.tensor_copy(ABC[:, span:2 * span], PS_AB[:, span:2 * span]).then_inc(s_v.h, 1)
    p.add("vector", evac_bcast)
    abc_done = s_v.bump()

    def apply_f(e):
        e.wait_ge(s_v.h, abc_done)
        for i in range(nch):
            if i:
                e.wait_ge(ch_v.h, apply_f.c0 + 2 * i)
            e.tensor_tensor(TMP[:, 0:span], src_X[:, i * t_stride + t0:i * t_stride + t0 + span],
                            ABC[:, 0:span], AluOpType.mult).then_inc(ch_v.h, 1)
            e.wait_ge(ch_v.h, apply_f.c0 + 2 * i + 1)
            e.tensor_tensor(dst_XN[:, i * t_stride + t0:i * t_stride + t0 + span], TMP[:, 0:span],
                            ABC[:, span:2 * span], AluOpType.add).then_inc(ch_v.h, 1)
    apply_f.c0 = ch_v.n
    done = ch_v.bump(2 * nch)
    p.add("vector", apply_f)
    return done



def build_l1():
    nc = bass.Bass("TRN2", target_bir_lowering=False)
    xT = nc.dram_tensor("xT", [D, TPC], F32, kind="ExternalInput").ap()
    Ws = {n: nc.dram_tensor(n, [D, D], F32, kind="ExternalInput").ap() for n in ("Wq", "Wk", "Wv")}
    bTs = {n: nc.dram_tensor(n, [128, NCH], F32, kind="ExternalInput").ap()
           for n in ("bqT", "bkT", "bvT")}
    OUTs = {n: nc.dram_tensor(n, [D, TPC], BF16, kind="ExternalOutput").ap()
            for n in ("qT", "kT", "vT")}
    UTd = nc.dram_tensor("UT", [D, DH], F32, kind="ExternalInput").ap()
    DTd = nc.dram_tensor("DT", [DH, D], F32, kind="ExternalInput").ap()
    UTSd = nc.dram_tensor("UTS", [D, DH], BF16, kind="ExternalOutput").ap()
    DTSd = nc.dram_tensor("DTS", [DH, D], BF16, kind="ExternalOutput").ap()

    es = ExitStack()
    sb = lambda name, shape, dt: es.enter_context(nc.sbuf_tensor(name, shape, dt))
    ps_ = lambda name, shape, dt=F32: es.enter_context(nc.psum_tensor(name, shape, dt))

    X = sb("X", [128, NCH * TPC], F32)
    XN = sb("XN", [128, NCH * TPC], BF16)
    XBF = sb("XBF", [128, NCH * SPAN], BF16)
    XSQ = sb("XSQ", [128, NCH * SPAN], BF16)
    ONESC = sb("ONESC", [128, 1], BF16)
    EPS1 = sb("EPS1", [1, 1], F32)
    ONESR = sb("ONESR", [1, 128], F32)
    STATR = sb("STATR", [1, 2 * SPAN], F32)
    AB = sb("AB", [1, 2 * SPAN], F32)
    ABC = sb("ABC", [128, 2 * SPAN], F32)
    TMP = sb("TMP", [128, SPAN], F32)
    WBF = [sb(f"WBF{i}", [128, NCH * 512], BF16) for i in range(2)]
    BIA = sb("BIA", [128, 3 * NCH], F32)
    OUTB = [sb(f"OUTB{i}", [128, SPAN], BF16) for i in range(2)]
    SGB = [sb(f"SGB{i}", [128, 4096], BF16) for i in range(2)]

    PS_MU = ps_("PS_MU", [1, SPAN])
    PS_SQ = ps_("PS_SQ", [1, SPAN])
    PS_AB = ps_("PS_AB", [128, 2 * SPAN])
    PS_O = [ps_(f"PS_O{i}", [128, SPAN]) for i in range(2)]

    sems = {k: Sem(es.enter_context(nc.semaphore("s_" + k)))
            for k in ("ld", "v", "a", "t", "g", "gx", "st0", "st1", "wl0", "wl1", "ev",
                      "chv", "cha", "sg0", "sg1", "so0", "so1", "sa")}
    s_ld, s_v, s_a, s_t, s_g = sems["ld"], sems["v"], sems["a"], sems["t"], sems["g"]
    s_st = [sems["st0"], sems["st1"]]
    s_wl = [sems["wl0"], sems["wl1"]]
    s_ev, s_gx = sems["ev"], sems["gx"]
    s_sg = [sems["sg0"], sems["sg1"]]
    s_so = [sems["so0"], sems["so1"]]
    s_sa = sems["sa"]

    p = Plan()

    def loads(e):
        e.dma_start(out=X[:].rearrange("p (c t) -> p c t", c=NCH),
                    in_=xT.rearrange("(c p) t -> p c t", c=NCH)).then_inc(s_ld.h, 16)
        for j, n in enumerate(("bqT", "bkT", "bvT")):
            e.dma_start(out=BIA[:, j * NCH:(j + 1) * NCH], in_=bTs[n]).then_inc(s_ld.h, 16)
    p.add("sync", loads)
    s_ld.bump(16 * 4)
    x_loaded = 16
    bias_loaded = s_ld.n

    def consts(e):
        e.memset(ONESC[:], 1.0)
        e.memset(ONESR[:], 1.0)
        e.memset(EPS1[:], LN_EPS).then_inc(s_g.h, 1)
    p.add("gpsimd", consts)
    consts_ready = s_g.bump()

    def bias_prep(e):
        e.wait_ge(s_ld.h, bias_loaded)
        e.tensor_scalar(BIA[:, 0:NCH], BIA[:, 0:NCH], QSCALE, None, AluOpType.mult).then_inc(s_v.h, 1)
    p.add("vector", bias_prep)
    bias_ready = s_v.bump()

    ln_done = []
    ln_milestones = []
    for sp in range(2):
        t0 = sp * SPAN

        prev_stats = ln_milestones[-1]["stats_done_out"] if ln_milestones else 0

        def xbf_load(e, t0=t0, prev_stats=prev_stats):
            if prev_stats:
                e.wait_ge(s_t.h, prev_stats)
            src = xT.rearrange("(c p) t -> p c t", c=NCH)[:, :, t0:t0 + SPAN]
            e.dma_start(out=XBF[:].rearrange("p (c t) -> p c t", c=NCH), in_=src).then_inc(s_gx.h, 16)
        p.add("gpsimd", xbf_load)
        xbf_ready = s_gx.bump(16)

        bufs = dict(XBF=XBF, XSQ=XSQ, ONESC=ONESC, ONESR=ONESR, STATR=STATR, AB=AB, ABC=ABC,
                    TMP=TMP, PS_MU=PS_MU, PS_SQ=PS_SQ, PS_AB=PS_AB, EPS1=EPS1, xbf_ready=xbf_ready,
                    consts_ready=consts_ready, x_ready=x_loaded and x_loaded)
        bufs["x_ready"] = 0

        def wait_x(e):
            e.wait_ge(s_ld.h, s_ld.n)
            e.nop().then_inc(s_v.h, 1)
        p.add("vector", wait_x)
        s_v.bump()

        done = emit_layernorm(p, {"v": s_v, "a": s_a, "t": s_t, "g": s_g, "gx": s_gx,
                                  "chv": sems["chv"], "cha": sems["cha"]}, bufs,
                              NCH, TPC, t0, SPAN, X, XN)
        ln_done.append(done)
        ln_milestones.append(bufs)

    evac_cnt = 0
    mm_groups = 0
    wl_sets = 0
    projs = [("Wq", "qT", 0, QSCALE), ("Wk", "kT", 1, 1.0), ("Wv", "vT", 2, 1.0)]
    set_mm_done = {}
    for oi, (wn, on, bidx, scal) in enumerate(projs):
        W, OUT = Ws[wn], OUTs[on]
        for espan in range(4):
            step = oi * 4 + espan
            buf = step % 2
            guard = set_mm_done.get(step - 2, 0)

            def wload(e, W=W, espan=espan, buf=buf, guard=guard):
                if guard:
                    e.wait_ge(s_t.h, guard)
                src = W.rearrange("(c p) e -> p c e", c=NCH)[:, :, espan * 512:(espan + 1) * 512]
                e.dma_start(out=WBF[buf][:].rearrange("p (c e) -> p c e", c=NCH),
                            in_=src).then_inc(s_wl[buf].h, 16)
            p.add("gpsimd", wload)
            set_loaded = s_wl[buf].bump(16)

            chv_h = sems["chv"].h

            def mms(e, buf=buf, set_loaded=set_loaded, step=step, ln_all=tuple(ln_done),
                    chv_h=chv_h):
                e.wait_ge(s_wl[buf].h, set_loaded)
                if step == 0:
                    e.wait_ge(chv_h, max(ln_all))
                for eb in range(4):
                    for tsp in range(2):
                        j = (step * 8 + eb * 2 + tsp)
                        bank = j % 2
                        if j >= 2:
                            e.wait_ge(s_ev.h, j - 1)
                        for i in range(NCH):
                            inst = e.matmul(PS_O[bank][:],
                                            WBF[buf][:, i * 512 + eb * 128:i * 512 + (eb + 1) * 128],
                                            XN[:, i * TPC + tsp * SPAN:i * TPC + (tsp + 1) * SPAN],
                                            start=(i == 0), stop=(i == NCH - 1))
                        inst.then_inc(s_t.h, 1)
            p.add("tensor", mms)
            first_tile = s_t.n + 1
            set_mm_done[step] = s_t.bump(8)

            def evacs(e, bidx=bidx, scal=scal, first_tile=first_tile, step=step):
                for eb in range(4):
                    for tsp in range(2):
                        j = step * 8 + eb * 2 + tsp
                        bank = j % 2
                        e.wait_ge(s_t.h, first_tile + eb * 2 + tsp)
                        if j >= 2:
                            e.wait_ge(s_st[j % 2].h, 16 * (j // 2))
                        col = bidx * NCH + (step % 4) * 4 + eb
                        e.tensor_scalar(OUTB[bank][:], PS_O[bank][:], scal,
                                        BIA[:, col:col + 1], AluOpType.mult, AluOpType.add
                                        ).then_inc(s_ev.h, 1)
            p.add("vector", evacs)
            first_ev = s_ev.n + 1
            s_ev.bump(8)

            def stores(e, OUT=OUT, espan=espan, first_ev=first_ev, step=step):
                for eb in range(4):
                    for tsp in range(2):
                        j = step * 8 + eb * 2 + tsp
                        bank = j % 2
                        e.wait_ge(s_ev.h, first_ev + eb * 2 + tsp)
                        r0 = espan * 512 + eb * 128
                        e.dma_start(out=OUT[r0:r0 + 128, tsp * SPAN:(tsp + 1) * SPAN],
                                    in_=OUTB[bank][:]).then_inc(s_st[bank].h, 16)
            p.add("sync", stores)
            s_st[0].bump(16 * 4)
            s_st[1].bump(16 * 4)

    sign_units = []
    for u in range(32):
        rc, half = u // 2, u % 2
        sign_units.append(("U", UTd[rc * 128:(rc + 1) * 128, half * 4096:(half + 1) * 4096],
                           UTSd[rc * 128:(rc + 1) * 128, half * 4096:(half + 1) * 4096], None))
    for u in range(32):
        src_ap = DTd[u * 256:(u + 1) * 256, :].rearrange("(c p) dd -> p c dd", c=2)
        dst_ap = DTSd[u * 256:(u + 1) * 256, :].rearrange("(c p) dd -> p c dd", c=2)
        sign_units.append(("D", src_ap, dst_ap, 2))

    for ui, (kind_, sap, dap, csplit) in enumerate(sign_units):
        par = ui % 2
        ld_val = 16 * (ui // 2 + 1)

        def sg_load(e, sap=sap, par=par, csplit=csplit, ui=ui):
            if ui >= 2:
                e.wait_ge(s_so[par].h, 16 * (ui // 2))
            dst = SGB[par][:].rearrange("p (c dd) -> p c dd", c=2) if csplit else SGB[par][:]
            e.dma_start(out=dst, in_=sap).then_inc(s_sg[par].h, 16)
        p.add("gpsimd", sg_load)
        s_sg[par].bump(16)
        ld_val = s_sg[par].n

        def sg_sign(e, par=par, ld_val=ld_val):
            e.wait_ge(s_sg[par].h, ld_val)
            e.activation(SGB[par][:], SGB[par][:], AF.Sign).then_inc(s_sa.h, 1)
        p.add("scalar", sg_sign)
        sa_val = s_sa.bump(1)

        def sg_store(e, dap=dap, par=par, csplit=csplit, sa_val=sa_val):
            e.wait_ge(s_sa.h, sa_val)
            src_ = SGB[par][:].rearrange("p (c dd) -> p c dd", c=2) if csplit else SGB[par][:]
            e.dma_start(out=dap, in_=src_).then_inc(s_so[par].h, 16)
        p.add("sync", sg_store)
        s_so[par].bump(16)

    def fin(e):
        e.wait_ge(s_st[0].h, s_st[0].n)
        e.wait_ge(s_st[1].h, s_st[1].n)
        e.wait_ge(s_so[0].h, s_so[0].n)
        e.wait_ge(s_so[1].h, s_so[1].n)
    p.add("sync", fin)

    p.emit(es.enter_context(nc.Block()))
    es.close()
    return nc



def stage_l1(inputs):
    x = np.asarray(inputs["L0_attn_a_x"])[0]
    Wq = np.asarray(inputs["L0_attn_a_Wq"])
    Wk = np.asarray(inputs["L0_attn_a_Wk"])
    Wv = np.asarray(inputs["L0_attn_a_Wv"])
    bq = np.asarray(inputs["L0_attn_a_bq"])
    bk = np.asarray(inputs["L0_attn_a_bk"])
    bv = np.asarray(inputs["L0_attn_a_bv"])
    UT_T = np.ascontiguousarray(np.asarray(inputs["L0_ffn_up_weight"]).T)
    DT_T = np.ascontiguousarray(np.asarray(inputs["L0_ffn_down_weight"]).T)

    def bT(b):
        return np.ascontiguousarray(b.reshape(NCH, 128).T)

    in_maps = []
    for c in range(NCORES):
        a0, b0 = span_starts(c)
        toks = np.r_[a0:a0 + SPAN, b0:b0 + SPAN]
        xTc = np.ascontiguousarray(x[toks, :].T)
        in_maps.append({
            "xT": xTc, "Wq": Wq, "Wk": Wk, "Wv": Wv,
            "bqT": bT(bq), "bkT": bT(bk), "bvT": bT(bv),
            "UT": UT_T, "DT": DT_T,
        })
    return in_maps


def kernel(**inputs):
    raise NotImplementedError("under construction: L2/L3 pending")



SUBW = [32, 64, 64, 128, 128, 256, 256, 512] + [512] * 8
WBLK = [2, 3, 4, 5, 7, 10, 14, 19, 27, 38, 54, 64, 64, 64, 64, 64]
KWIN = 68
HGROUPS = [(0, 1, 2), (3, 4, 5), (6, 7, 8), (9, 10, 11), (12, 13, 14), (15,)]
KBATCH = 8


def attention_sites(span):
    groups = []
    for gh in HGROUPS:
        persite = []
        kbs_needed = set()
        for h in gh:
            W = SUBW[h]
            nsub = 512 // W
            w = WBLK[h]
            for s in range(nsub):
                q0 = W * s
                jlast = (W * (s + 1) - 1) // 128
                kb_hi = 64 + jlast
                kb_lo = kb_hi - (w - 1)
                kb_lo = max(kb_lo, 0)
                for kb in range(kb_lo, kb_hi + 1):
                    kbs_needed.add(kb)
                persite.append((h, s, q0, W, kb_lo, kb_hi))
        kbs_sorted = sorted(kbs_needed)
        batches = []
        for b0 in range(0, len(kbs_sorted), KBATCH):
            kb_list = kbs_sorted[b0:b0 + KBATCH]
            site_list = []
            for (h, s, q0, W, kb_lo, kb_hi) in persite:
                for kb in kb_list:
                    if kb_lo <= kb <= kb_hi:
                        r = q0 - 128 * (kb - 64)
                        jq_lo = q0 // 128
                        jq_hi = (q0 + W - 1) // 128
                        overlaps = (64 + jq_lo) <= kb <= (64 + jq_hi)
                        mask_key = (W, r) if overlaps else None
                        site_list.append(dict(h=h, sub=s, q0=q0, W=W, kb=kb,
                                              first=(kb == kb_lo), last=(kb == kb_hi),
                                              mask_key=mask_key))
            batches.append((kb_list, site_list))
        groups.append((gh, batches))
    return groups


def all_mask_keys():
    keys = []
    for span in range(1):
        for gh, batches in attention_sites(0):
            for kb_list, sites in batches:
                for st in sites:
                    if st["mask_key"] and st["mask_key"] not in keys:
                        keys.append(st["mask_key"])
    return sorted(keys)


def build_masks():
    keys = all_mask_keys()
    offs = {}
    cols = []
    pos = 0
    for (W, r) in keys:
        kk = np.arange(128)[:, None]
        qq = np.arange(W)[None, :]
        m = (kk <= qq + r).astype(BF)
        offs[(W, r)] = (pos, W)
        cols.append(m)
        pos += W
    return np.concatenate(cols, axis=1), offs


def site_count():
    n = 0
    for gh, batches in attention_sites(0):
        for kb_list, sites in batches:
            n += len(sites)
    return n


def build_bias_table(core):
    nsites = site_count()
    tab = np.zeros((128, 2 * nsites), np.float32)
    for sp in range(2):
        a0, b0 = span_starts(core)
        q_glob0 = a0 if sp == 0 else b0
        blk0 = q_glob0 // 128
        idx = 0
        for gh, batches in attention_sites(sp):
            for kb_list, sites in batches:
                for st in sites:
                    h, kb, W, q0 = st["h"], st["kb"], st["W"], st["q0"]
                    gkb = blk0 + (kb - 64)
                    col = sp * nsites + idx
                    if gkb < 0:
                        tab[:, col] = -1e30
                    else:
                        slope = SLOPES[h]
                        kglob = 128 * gkb + np.arange(128)
                        qg0 = q_glob0 + q0
                        c = slope * (W / 2.0)
                        v = slope * (kglob - qg0) - c
                        v[kglob > qg0 + W - 1] = -1e30
                        tab[:, col] = v
                    idx += 1
    return tab



class Bank:
    def __init__(self, ap):
        self.ap = ap
        self.w = None
        self.r = None


def wms(e, ms):
    if ms is not None:
        e.wait_ge(ms[0].h, ms[1])


_M, _OFFS = None, None


def masks_and_offs():
    global _M, _OFFS
    if _M is None:
        _M, _OFFS = build_masks()
    return _M, _OFFS


def emit_ln2(ctx, sp, x_ready_ms):
    (p, banks, sems, X, XN, AR, WS, ONESC, ONESR, EPS1, STATR, ABrow, ABC, TMP,
     XBF2, SQ2, UPS, DSOS, UTSd, DTSd, mm_done, ws_last) = ctx['core']
    s_v, s_a, s_ts, ch_v, ch_a = sems['v'], sems['a'], sems['ts'], sems['chv'], sems['cha']
    s_wl = [sems['wl0'], sems['wl1']]
    s_wls = [sems['wls0'], sems['wls1']]
    t0 = sp * SPAN
    b_mu, b_sq, b_a, b_b = banks[4], banks[5], banks[6], banks[7]
    c0 = ch_v.n
    mu_rd, sq_rd = b_mu.r, b_sq.r
    tchk0 = s_ts.n

    def cast_chunks(e, t0=t0, x_ready_ms=x_ready_ms, c0=c0, tchk0=tchk0):
        wms(e, x_ready_ms)
        for i in range(NCH):
            if i >= 2:
                e.wait_ge(s_ts.h, tchk0 + i - 1)
            e.tensor_copy(XBF2[i % 2][:], X[:, i * TPC + t0:i * TPC + t0 + SPAN]
                          ).then_inc(ch_v.h, 1)
            e.wait_ge(ch_v.h, c0 + 2 * i + 1)
            e.tensor_tensor(SQ2[i % 2][:], XBF2[i % 2][:], XBF2[i % 2][:],
                            AluOpType.mult).then_inc(ch_v.h, 1)
    p.add("vector", cast_chunks)
    ch_v.bump(2 * NCH)

    def stats_mm(e, c0=c0, b_mu_ap=b_mu.ap, b_sq_ap=b_sq.ap, rd=(mu_rd, sq_rd)):
        wms(e, rd[0])
        wms(e, rd[1])
        for i in range(NCH):
            e.wait_ge(ch_v.h, c0 + 2 * i + 2)
            e.matmul(b_mu_ap[0:1, :], ONESC[:], XBF2[i % 2][:],
                     start=(i == 0), stop=(i == NCH - 1))
            inst = e.matmul(b_sq_ap[0:1, :], ONESC[:], SQ2[i % 2][:],
                            start=(i == 0), stop=(i == NCH - 1))
            inst.then_inc(s_ts.h, 1)
    p.add("tensor", stats_mm)
    stats_done = s_ts.bump(NCH)
    b_mu.w = b_sq.w = (s_ts, stats_done)
    c1, c2, c3 = ch_v.bump(1), ch_v.bump(1), ch_v.bump(1)

    def stats_fin(e, stats_done=stats_done, b_mu_ap=b_mu.ap, b_sq_ap=b_sq.ap,
                  c1=c1, c2=c2):
        e.wait_ge(s_ts.h, stats_done)
        e.tensor_scalar(STATR[0:1, 0:SPAN], b_mu_ap[0:1, :], 1.0 / D, None, AluOpType.mult)
        e.tensor_scalar(STATR[0:1, SPAN:2 * SPAN], b_sq_ap[0:1, :], 1.0 / D, None,
                        AluOpType.mult).then_inc(ch_v.h, 1)
        e.wait_ge(ch_v.h, c1)
        e.tensor_tensor(ABrow[0:1, SPAN:2 * SPAN], STATR[0:1, 0:SPAN], STATR[0:1, 0:SPAN],
                        AluOpType.mult).then_inc(ch_v.h, 1)
        e.wait_ge(ch_v.h, c2)
        e.tensor_tensor(ABrow[0:1, SPAN:2 * SPAN], STATR[0:1, SPAN:2 * SPAN],
                        ABrow[0:1, SPAN:2 * SPAN], AluOpType.subtract).then_inc(ch_v.h, 1)
    p.add("vector", stats_fin)
    var_done = (ch_v, ch_v.n)
    b_mu.r = b_sq.r = var_done
    ca = ch_a.bump(1)

    def rstd_f(e, var_done=var_done, ca=ca):
        wms(e, var_done)
        e.activation(ABrow[0:1, 0:SPAN], ABrow[0:1, SPAN:2 * SPAN], AF.Ln,
                     bias=EPS1[0:1, 0:1], scale=1.0).then_inc(ch_a.h, 1)
        e.wait_ge(ch_a.h, ca)
        e.activation(ABrow[0:1, 0:SPAN], ABrow[0:1, 0:SPAN], AF.Exp,
                     scale=-0.5).then_inc(s_a.h, 1)
    p.add("scalar", rstd_f)
    rstd_done = s_a.bump(1)
    c4 = ch_v.bump(1)

    def brow_f(e, rstd_done=rstd_done, c4=c4):
        e.wait_ge(s_a.h, rstd_done)
        e.tensor_tensor(ABrow[0:1, SPAN:2 * SPAN], STATR[0:1, 0:SPAN], ABrow[0:1, 0:SPAN],
                        AluOpType.mult).then_inc(ch_v.h, 1)
        e.wait_ge(ch_v.h, c4)
        e.tensor_scalar(ABrow[0:1, SPAN:2 * SPAN], ABrow[0:1, SPAN:2 * SPAN], -1.0, None,
                        AluOpType.mult).then_inc(s_v.h, 1)
    p.add("vector", brow_f)
    brow_done = s_v.bump(1)

    def bcast_mm(e, brow_done=brow_done, b_a_ap=b_a.ap, b_b_ap=b_b.ap,
                 rd=(b_a.r, b_b.r)):
        e.wait_ge(s_v.h, brow_done)
        wms(e, rd[0])
        wms(e, rd[1])
        e.matmul(b_a_ap[:], ONESR[:], ABrow[0:1, 0:SPAN], start=True, stop=True)
        e.matmul(b_b_ap[:], ONESR[:], ABrow[0:1, SPAN:2 * SPAN], start=True,
                 stop=True).then_inc(s_ts.h, 1)
    p.add("tensor", bcast_mm)
    bcast_done = s_ts.bump(1)
    b_a.w = b_b.w = (s_ts, bcast_done)
    c5 = ch_v.bump(1)

    def evac_bcast(e, bcast_done=bcast_done, b_a_ap=b_a.ap, b_b_ap=b_b.ap, c5=c5):
        e.wait_ge(s_ts.h, bcast_done)
        e.tensor_copy(ABC[:, 0:SPAN], b_a_ap[:]).then_inc(ch_v.h, 1)
        e.wait_ge(ch_v.h, c5)
        e.tensor_copy(ABC[:, SPAN:2 * SPAN], b_b_ap[:]).then_inc(s_v.h, 1)
    p.add("vector", evac_bcast)
    abc_done = s_v.bump(1)
    b_a.r = b_b.r = (s_v, abc_done)
    ap_c0 = ch_v.n

    def apply_f(e, abc_done=abc_done, t0=t0, ap_c0=ap_c0):
        e.wait_ge(s_v.h, abc_done)
        for i in range(NCH):
            if i:
                e.wait_ge(ch_v.h, ap_c0 + 2 * i)
            e.tensor_tensor(TMP[:, 0:SPAN], X[:, i * TPC + t0:i * TPC + t0 + SPAN],
                            ABC[:, 0:SPAN], AluOpType.mult).then_inc(ch_v.h, 1)
            e.wait_ge(ch_v.h, ap_c0 + 2 * i + 1)
            e.tensor_tensor(XN[:, i * SPAN:(i + 1) * SPAN], TMP[:, 0:SPAN],
                            ABC[:, SPAN:2 * SPAN], AluOpType.add).then_inc(ch_v.h, 1)
    p.add("vector", apply_f)
    ch_v.bump(2 * NCH)
    return (ch_v, ch_v.n)



def emit_ffn_span(ctx, sp, xn_ms, name):
    (p, banks, sems, X, XN, AR, WS, ONESC, ONESR, EPS1, STATR, ABrow, ABC, TMP,
     XBF2, SQ2, UPS, DSOS, UTSd, DTSd, mm_done, ws_last) = ctx['core']
    s_v, s_a, s_ts, ch_v, ch_a = sems['v'], sems['a'], sems['ts'], sems['chv'], sems['cha']
    s_wl = [sems['wl0'], sems['wl1']]
    s_wls = [sems['wls0'], sems['wls1']]
    for hs in range(32):
        buf = hs % 2
        guard = ws_last[buf]

        def wload(e, hs=hs, buf=buf, guard=guard):
            wms(e, guard)
            src = UTSd.rearrange("(c p) hh -> p c hh", c=NCH)[:, :, hs * 256:(hs + 1) * 256]
            e.dma_start(out=WS[buf][:].rearrange("p (c hh) -> p c hh", c=NCH),
                        in_=src).then_inc(s_wls[buf].h, 16)
        p.add("sync", wload)
        set_loaded = s_wls[buf].bump(16)

        for hb2 in range(2):
            hb = hs * 2 + hb2
            bank = banks[hb % 2]
            my_mm = s_ts.n + 1

            def mms(e, buf=buf, hb2=hb2, hb=hb, bank_ap=bank.ap, set_loaded=set_loaded,
                    xn_ms=xn_ms, bank_r=bank.r):
                e.wait_ge(s_wls[buf].h, set_loaded)
                if hb == 0:
                    wms(e, xn_ms)
                wms(e, bank_r)
                for i in range(NCH):
                    inst = e.matmul(bank_ap[:],
                                    WS[buf][:, i * 256 + hb2 * 128:i * 256 + (hb2 + 1) * 128],
                                    XN[:, i * SPAN:(i + 1) * SPAN],
                                    start=(i == 0), stop=(i == NCH - 1))
                inst.then_inc(s_ts.h, 1)
            p.add("tensor", mms)
            s_ts.bump(1)
            bank.w = (s_ts, my_mm)
            mm_done[(name + "up", hs)] = my_mm
            ws_last[buf] = (s_ts, my_mm)

            def evac_up(e, bank_ap=bank.ap, hb=hb, my_mm=my_mm):
                e.wait_ge(s_ts.h, my_mm)
                e.activation(AR[:, hb * SPAN:(hb + 1) * SPAN], bank_ap[:], AF.Relu,
                             scale=UPS[:, hb:hb + 1]).then_inc(s_a.h, 1)
            p.add("scalar", evac_up)
            bank.r = (s_a, s_a.bump(1))
    up_all = (s_a, s_a.n)

    last = None
    for ddb in range(NCH):
        my_mm = s_ts.n + 2
        for half in range(2):
            buf = (ddb * 2 + half) % 2
            guard = ws_last[buf]

            def wload(e, ddb=ddb, half=half, buf=buf, guard=guard):
                wms(e, guard)
                src = DTSd.rearrange("(c p) dd -> p c dd", c=64)[
                    :, half * 32:(half + 1) * 32, ddb * 128:(ddb + 1) * 128]
                e.dma_start(out=WS[buf][:].rearrange("p (c dd) -> p c dd", c=32),
                            in_=src).then_inc(s_wls[buf].h, 16)
            p.add("sync", wload)
            set_loaded = s_wls[buf].bump(16)
            bank = banks[2 + ddb % 2]

            def mms(e, buf=buf, half=half, bank_ap=bank.ap, set_loaded=set_loaded,
                    up_all=up_all, ddb=ddb, bank_r=bank.r):
                e.wait_ge(s_wls[buf].h, set_loaded)
                if ddb == 0 and half == 0:
                    wms(e, up_all)
                if half == 0:
                    wms(e, bank_r)
                for i in range(32):
                    hc = half * 32 + i
                    inst = e.matmul(bank_ap[:],
                                    WS[buf][:, i * 128:(i + 1) * 128],
                                    AR[:, hc * SPAN:(hc + 1) * SPAN],
                                    start=(hc == 0), stop=(hc == 63))
                inst.then_inc(s_ts.h, 1)
            p.add("tensor", mms)
            s_ts.bump(1)
            mm_done[(name + "dn", ddb * 2 + half)] = s_ts.n
            ws_last[buf] = (s_ts, s_ts.n)
        bank = banks[2 + ddb % 2]
        bank.w = (s_ts, my_mm)

        def evac_dn(e, bank_ap=bank.ap, ddb=ddb, sp=sp, my_mm=my_mm):
            e.wait_ge(s_ts.h, my_mm)
            e.scalar_tensor_tensor(
                X[:, ddb * TPC + sp * SPAN:ddb * TPC + (sp + 1) * SPAN],
                bank_ap[:], DSOS[:, ddb:ddb + 1],
                X[:, ddb * TPC + sp * SPAN:ddb * TPC + (sp + 1) * SPAN],
                AluOpType.mult, AluOpType.add).then_inc(s_v.h, 1)
        p.add("vector", evac_dn)
        last = (s_v, s_v.bump(1))
        bank.r = last
    return last



def emit_proj_to_x(ctx, src_off, Wd, bias_ap, ready_ms, name, residual=True):
    (p, banks, sems, X, XN, AR, WS, ONESC, ONESR, EPS1, STATR, ABrow, ABC, TMP,
     XBF2, SQ2, UPS, DSOS, UTSd, DTSd, mm_done, ws_last) = ctx['core']
    s_v, s_a, s_ts, ch_v, ch_a = sems['v'], sems['a'], sems['ts'], sems['chv'], sems['cha']
    s_wl = [sems['wl0'], sems['wl1']]
    s_wls = [sems['wls0'], sems['wls1']]
    last = None
    for ec in range(NCH):
        buf = ec % 2
        guard = ws_last[buf]

        def wload(e, Wd=Wd, ec=ec, buf=buf, guard=guard):
            wms(e, guard)
            src = Wd.rearrange("(c p) ee -> p c ee", c=NCH)[:, :, ec * 128:(ec + 1) * 128]
            e.dma_start(out=WS[buf][:, 0:NCH * 128].rearrange("p (c ee) -> p c ee", c=NCH),
                        in_=src).then_inc(s_wl[buf].h, 16)
        p.add("gpsimd", wload)
        set_loaded = s_wl[buf].bump(16)

        for tsp in range(2):
            j = ec * 2 + tsp
            bank = banks[j % 2]
            my_mm = s_ts.n + 1

            def mms(e, buf=buf, tsp=tsp, bank_ap=bank.ap, src_off=src_off,
                    set_loaded=set_loaded, j=j, ready_ms=ready_ms, bank_r=bank.r):
                e.wait_ge(s_wl[buf].h, set_loaded)
                if j == 0:
                    wms(e, ready_ms)
                wms(e, bank_r)
                for i in range(NCH):
                    inst = e.matmul(bank_ap[:],
                                    WS[buf][:, i * 128:(i + 1) * 128],
                                    AR[:, src_off + i * TPC + tsp * SPAN:
                                       src_off + i * TPC + (tsp + 1) * SPAN],
                                    start=(i == 0), stop=(i == NCH - 1))
                inst.then_inc(s_ts.h, 1)
            p.add("tensor", mms)
            s_ts.bump(1)
            bank.w = (s_ts, my_mm)
            mm_done[(name, ec)] = my_mm
            ws_last[buf] = (s_ts, my_mm)

            def evac(e, bank_ap=bank.ap, ec=ec, tsp=tsp, my_mm=my_mm, bias_ap=bias_ap):
                e.wait_ge(s_ts.h, my_mm)
                e.scalar_tensor_tensor(
                    X[:, ec * TPC + tsp * SPAN:ec * TPC + (tsp + 1) * SPAN],
                    bank_ap[:], bias_ap[:, ec:ec + 1],
                    X[:, ec * TPC + tsp * SPAN:ec * TPC + (tsp + 1) * SPAN],
                    AluOpType.add, AluOpType.add).then_inc(s_v.h, 1)
            p.add("vector", evac)
            last = (s_v, s_v.bump(1))
            bank.r = last
    return last



def build_l2(upto="full"):
    MASKARR, MASK_OFFS = masks_and_offs()
    MASK_COLS = MASKARR.shape[1]
    nsites = site_count()

    nc = bass.Bass("TRN2", target_bir_lowering=False)
    dram = lambda n, shape, dt, kind="ExternalInput": nc.dram_tensor(n, shape, dt, kind=kind).ap()
    XTd = dram("XT", [D, TPC], F32)
    QTd = dram("QT", [D, TPC], BF16)
    KWd = [dram("KWA", [D, KWIN * 128], BF16), dram("KWB", [D, KWIN * 128], BF16)]
    VWd = [dram("VWA", [KWIN * 128, D], BF16), dram("VWB", [KWIN * 128, D], BF16)]
    BTABd = dram("BTAB", [128, 2 * nsites], F32)
    MASKSd = dram("MASKS", [128, MASK_COLS], BF16)
    Wod = dram("Wo", [D, D], F32)
    boTd = dram("boT", [128, NCH], F32)
    UTSd = dram("UTS", [D, DH], BF16)
    DTSd = dram("DTS", [DH, D], BF16)
    upsTd = dram("upsT", [128, DH // 128], F32)
    dsTd = dram("dsT", [128, NCH], F32)
    oscd = dram("osc", [128, 1], F32)
    bWTd = [dram(n, [D, D], F32) for n in ("bWkT", "bWvT", "bWrT")]
    tdTd = dram("tdT", [128, NH], F32)
    tfTd = dram("tfT", [128, NH], F32)
    find_d = dram("FIRSTIND", [128, 1], F32)

    X2Td = dram("X2T", [D, TPC], F32, "ExternalOutput")
    NUMLd = dram("NUML", [D, TPC], F32, "ExternalOutput")
    DENLd = dram("DENL", [D, TPC], F32, "ExternalOutput")
    SIGd = dram("SIG", [D, TPC], F32, "ExternalOutput")
    SUMSd = dram("SUMS", [128, NCH * 4], F32, "ExternalOutput")

    es = ExitStack()
    sb = lambda name, shape, dt: es.enter_context(nc.sbuf_tensor(name, shape, dt))

    X = sb("X", [128, NCH * TPC], F32)
    XN = sb("XN", [128, NCH * SPAN], BF16)
    AR = sb("AR", [128, 2 * NCH * TPC], BF16)
    WS = [sb(f"WS{i}", [128, 4096], BF16) for i in range(2)]
    BIA2 = sb("BIA2", [128, NCH], F32)
    ONESC = sb("ONESC", [128, 1], BF16)
    ONESR = sb("ONESR", [1, 128], F32)
    EPS1 = sb("EPS1", [1, 1], F32)
    STATR = sb("STATR", [1, 2 * SPAN], F32)
    ABrow = sb("ABrow", [1, 2 * SPAN], F32)
    ABC = sb("ABC", [128, 2 * SPAN], F32)
    TMP = sb("TMP", [128, SPAN], F32)
    XBF2 = [sb(f"XBF2{i}", [128, SPAN], BF16) for i in range(2)]
    SQ2 = [sb(f"SQ2{i}", [128, SPAN], BF16) for i in range(2)]
    UPS = sb("UPS", [128, DH // 128], F32)
    DST = sb("DST", [128, NCH], F32)
    OSC = sb("OSC", [128, 1], F32)
    DSOS = sb("DSOS", [128, NCH], F32)
    TD = sb("TD", [128, NH], F32)
    TF = sb("TF", [128, NH], F32)
    IND = sb("IND", [128, 1], F32)
    E1 = sb("E1", [128, NH], F32)
    DEC = sb("DEC", [128, NH], F32)
    FEFF = sb("FEFF", [128, NH], F32)
    SUMS = sb("SUMSs", [128, NCH * 4], F32)

    es_attn = ExitStack()
    sba = lambda name, shape, dt: es_attn.enter_context(nc.sbuf_tensor(name, shape, dt))
    KB = sba("KB", [128, 3 * KBATCH * 128], BF16)
    VB = sba("VB", [128, 3 * KBATCH * 128], BF16)
    PT = [sba(f"PT{i}", [128, SPAN], BF16) for i in range(2)]
    BTAB = sba("BTABs", [128, nsites], F32)
    MASKS = sba("MASKSs", [128, MASK_COLS], BF16)
    RL = sba("RL", [1, SPAN], F32)
    RLB = sba("RLB", [128, SPAN], F32)

    PS = [es.enter_context(nc.psum_tensor(f"PS{i}", [128, SPAN], F32)) for i in range(8)]
    banks = [Bank(t) for t in PS]

    sems = {k: Sem(es.enter_context(nc.semaphore("s_" + k)))
            for k in ("ld", "ld2", "v", "a", "ts", "tp", "g", "kv", "wl0", "wl1",
                      "wls0", "wls1", "st0", "st1", "stx", "chv", "cha",
                      "w30", "w31", "w32", "w33", "w34", "w35")}
    s_ld, s_v, s_a, s_g = sems["ld"], sems["v"], sems["a"], sems["g"]
    s_ld2 = sems["ld2"]
    s_ts, s_tp, s_kv = sems["ts"], sems["tp"], sems["kv"]
    ch_v, ch_a = sems["chv"], sems["cha"]
    s_wl = [sems["wl0"], sems["wl1"]]
    s_wls = [sems["wls0"], sems["wls1"]]
    s_st = [sems["st0"], sems["st1"]]
    s_stx = sems["stx"]
    s_w3 = [sems["w3" + str(i)] for i in range(6)]

    p = Plan()
    QT_OFF = 0
    OT_OFF = NCH * TPC

    def loads(e):
        e.dma_start(out=X[:].rearrange("p (c t) -> p c t", c=NCH),
                    in_=XTd.rearrange("(c p) t -> p c t", c=NCH)).then_inc(s_ld.h, 16)
        e.dma_start(out=AR[:, QT_OFF:QT_OFF + NCH * TPC].rearrange("p (c t) -> p c t", c=NCH),
                    in_=QTd.rearrange("(c p) t -> p c t", c=NCH)).then_inc(s_ld.h, 16)
        e.dma_start(out=BTAB[:], in_=BTABd[:, 0:nsites]).then_inc(s_ld.h, 16)
        e.dma_start(out=MASKS[:], in_=MASKSd).then_inc(s_ld.h, 16)
        e.dma_start(out=BIA2[:], in_=boTd).then_inc(s_ld.h, 16)
        e.dma_start(out=UPS[:], in_=upsTd).then_inc(s_ld.h, 16)
        e.dma_start(out=DST[:], in_=dsTd).then_inc(s_ld.h, 16)
        e.dma_start(out=OSC[:], in_=oscd).then_inc(s_ld.h, 16)
        e.dma_start(out=TD[:], in_=tdTd).then_inc(s_ld.h, 16)
        e.dma_start(out=TF[:], in_=tfTd).then_inc(s_ld.h, 16)
        e.dma_start(out=IND[:], in_=find_d).then_inc(s_ld.h, 16)
    p.add("sync", loads)
    loads_done = s_ld.bump(16 * 11)

    def consts(e):
        e.memset(ONESC[:], 1.0)
        e.memset(ONESR[:], 1.0)
        e.memset(EPS1[:], LN_EPS).then_inc(s_g.h, 1)
    p.add("gpsimd", consts)
    consts_done = s_g.bump()

    for eng in ("tensor", "scalar", "vector"):
        def gate(e, _ld=loads_done, _g=consts_done):
            e.wait_ge(s_ld.h, _ld)
            e.wait_ge(s_g.h, _g)
        p.add(eng, gate)

    site_idx = 0
    btab_reload_ms = None
    for sp in range(2):
        if sp == 1:
            exp_a_done = s_a.n

            def btab_reload(e, exp_a_done=exp_a_done):
                e.wait_ge(s_a.h, exp_a_done)
                e.dma_start(out=BTAB[:], in_=BTABd[:, nsites:2 * nsites]).then_inc(s_ld2.h, 16)
            p.add("sync", btab_reload)
            btab_reload_ms = (s_ld2, s_ld2.bump(16))
        for gi, (gh, batches) in enumerate(attention_sites(sp)):
            oh_banks = {h: banks[j] for j, h in enumerate(gh)}
            l_banks = {h: banks[3 + j] for j, h in enumerate(gh)}
            sping = [banks[6], banks[7]]
            first_pv = {h: True for h in gh}
            first_l = {h: True for h in gh}
            for (kb_list, sites) in batches:
                kb_lo = kb_list[0]
                nkb = kb_list[-1] - kb_lo + 1
                prev_tp = s_tp.n

                def kvload(e, sp=sp, gh=gh, kb_lo=kb_lo, nkb=nkb, prev_tp=prev_tp):
                    if prev_tp:
                        e.wait_ge(s_tp.h, prev_tp)
                    h0, nh = gh[0], len(gh)
                    src_k = KWd[sp][128 * h0:128 * (h0 + nh), 128 * kb_lo:128 * (kb_lo + nkb)]
                    e.dma_start(out=KB[:, 0:nh * nkb * 128].rearrange("p (h k) -> p h k", h=nh),
                                in_=src_k.rearrange("(h p) k -> p h k", h=nh)).then_inc(s_kv.h, 16)
                    src_v = VWd[sp][128 * kb_lo:128 * (kb_lo + nkb), 128 * h0:128 * (h0 + nh)]
                    e.dma_start(out=VB[:, 0:nkb * nh * 128].rearrange("p (k hd) -> p k hd", k=nkb),
                                in_=src_v.rearrange("(k p) hd -> p k hd", k=nkb)).then_inc(s_kv.h, 16)
                p.add("sync", kvload)
                kv_done = s_kv.bump(32)

                for st in sites:
                    h, q0, W, kb = st["h"], st["q0"], st["W"], st["kb"]
                    hi = gh.index(h)
                    kbi = kb - kb_lo
                    pp = site_idx % 2
                    ob, lb = oh_banks[h], l_banks[h]
                    sb_ = sping[pp]
                    col = site_idx % nsites
                    my_score = s_ts.n + 1
                    my_exp = s_a.n + 1
                    my_end = s_tp.n + 1
                    is_first_pv = first_pv[h]; first_pv[h] = False
                    is_first_l = first_l[h]; first_l[h] = False
                    mask_key = st["mask_key"]

                    def score(e, hi=hi, kbi=kbi, q0=q0, W=W, sp=sp, h=h, nkb=nkb,
                              kv_done=kv_done, sb_ap=sb_.ap, bank_r=sb_.r):
                        e.wait_ge(s_kv.h, kv_done)
                        wms(e, bank_r)
                        e.matmul(sb_ap[:, 0:W],
                                 KB[:, (hi * nkb + kbi) * 128:(hi * nkb + kbi + 1) * 128],
                                 AR[:, QT_OFF + h * TPC + sp * SPAN + q0:
                                    QT_OFF + h * TPC + sp * SPAN + q0 + W],
                                 start=True, stop=True).then_inc(s_ts.h, 1)
                    p.add("tensor", score)
                    s_ts.bump(1)
                    sb_.w = (s_ts, my_score)

                    pt_guard = (s_tp.n - 1) if site_idx >= 2 else 0

                    def expf(e, pp=pp, W=W, col=col, my_score=my_score, sb_ap=sb_.ap,
                             pt_guard=pt_guard, btab_ms=btab_reload_ms if sp == 1 else None,
                             first_of_span=(site_idx % nsites == 0)):
                        if btab_ms is not None and first_of_span:
                            wms(e, btab_ms)
                        e.wait_ge(s_ts.h, my_score)
                        if pt_guard > 0:
                            e.wait_ge(s_tp.h, pt_guard)
                        e.activation(PT[pp][:, 0:W], sb_ap[:, 0:W], AF.Exp,
                                     bias=BTAB[:, col:col + 1], scale=1.0).then_inc(s_a.h, 1)
                    p.add("scalar", expf)
                    s_a.bump(1)
                    sb_.r = (s_a, my_exp)

                    mask_ms = None
                    if mask_key is not None:
                        moff, mw = MASK_OFFS[mask_key]

                        def maskf(e, pp=pp, W=W, moff=moff, my_exp=my_exp):
                            e.wait_ge(s_a.h, my_exp)
                            e.tensor_tensor(PT[pp][:, 0:W], PT[pp][:, 0:W],
                                            MASKS[:, moff:moff + W],
                                            AluOpType.mult).then_inc(s_v.h, 1)
                        p.add("vector", maskf)
                        mask_ms = (s_v, s_v.bump(1))

                    def pvl(e, pp=pp, hi=hi, kbi=kbi, q0=q0, W=W, nh=len(gh),
                            ob_ap=ob.ap, lb_ap=lb.ap, my_exp=my_exp, mask_ms=mask_ms,
                            is_first_pv=is_first_pv, is_first_l=is_first_l,
                            ob_r=ob.r, lb_r=lb.r):
                        if mask_ms is not None:
                            wms(e, mask_ms)
                        else:
                            e.wait_ge(s_a.h, my_exp)
                        if is_first_pv:
                            wms(e, ob_r)
                        if is_first_l:
                            wms(e, lb_r)
                        e.matmul(ob_ap[:, q0:q0 + W],
                                 VB[:, (kbi * nh + hi) * 128:(kbi * nh + hi + 1) * 128],
                                 PT[pp][:, 0:W], start=is_first_pv, stop=False,
                                 skip_group_check=True)
                        e.matmul(lb_ap[0:1, q0:q0 + W], ONESC[:], PT[pp][:, 0:W],
                                 start=is_first_l, stop=False,
                                 skip_group_check=True).then_inc(s_tp.h, 1)
                    p.add("tensor", pvl)
                    s_tp.bump(1)
                    ob.w = (s_tp, my_end)
                    lb.w = (s_tp, my_end)
                    site_idx += 1

            prev_bc = None
            for j, h in enumerate(gh):
                ob, lb = oh_banks[h], l_banks[h]
                bb = sping[0]
                c1 = ch_a.bump(1)

                def fin_rl(e, lb_ap=lb.ap, lb_w=lb.w, ob_w=ob.w, c1=c1, prev_bc=prev_bc):
                    wms(e, lb_w)
                    wms(e, ob_w)
                    wms(e, prev_bc)
                    e.activation(RL[0:1, :], lb_ap[0:1, :], AF.Ln).then_inc(ch_a.h, 1)
                    e.wait_ge(ch_a.h, c1)
                    e.activation(RL[0:1, :], RL[0:1, :], AF.Exp, scale=-1.0).then_inc(s_a.h, 1)
                p.add("scalar", fin_rl)
                rl_done = s_a.bump(1)
                lb.r = (s_a, rl_done)

                def fin_bc(e, bb_ap=bb.ap, rl_done=rl_done, bank_r=bb.r):
                    e.wait_ge(s_a.h, rl_done)
                    wms(e, bank_r)
                    e.matmul(bb_ap[:], ONESR[:], RL[0:1, :], start=True,
                             stop=True).then_inc(s_ts.h, 1)
                p.add("tensor", fin_bc)
                bc_done = s_ts.bump(1)
                bb.w = (s_ts, bc_done)
                prev_bc = (s_ts, bc_done)
                c2 = ch_v.bump(1)
                prev_ot = (s_v, s_v.n)

                def fin_ev(e, bb_ap=bb.ap, ob_ap=ob.ap, h=h, sp=sp, bc_done=bc_done,
                           c2=c2, prev_ot=prev_ot):
                    e.wait_ge(s_ts.h, bc_done)
                    wms(e, prev_ot)
                    e.tensor_copy(RLB[:], bb_ap[:]).then_inc(ch_v.h, 1)
                    e.wait_ge(ch_v.h, c2)
                    e.tensor_tensor(AR[:, OT_OFF + h * TPC + sp * SPAN:
                                       OT_OFF + h * TPC + (sp + 1) * SPAN],
                                    ob_ap[:], RLB[:], AluOpType.mult).then_inc(s_v.h, 1)
                p.add("vector", fin_ev)
                ot_done = s_v.bump(1)
                ob.r = (s_v, ot_done)
                bb.r = (s_v, ot_done)
    attn_done_ms = (s_v, s_v.n)
    es_attn.close()

    mm_done = {}
    ws_last = {0: None, 1: None}
    ctx = {'core': (p, banks, sems, X, XN, AR, WS, ONESC, ONESR, EPS1, STATR, ABrow, ABC, TMP,
                    XBF2, SQ2, UPS, DSOS, UTSd, DTSd, mm_done, ws_last)}

    x1_ms = emit_proj_to_x(ctx, OT_OFF, Wod, BIA2, attn_done_ms, "woproj")

    cdsos = ch_v.bump(1)

    def dsos_prep(e, cdsos=cdsos):
        e.tensor_scalar(DSOS[:], DST[:], OSC[:, 0:1], None, AluOpType.mult).then_inc(ch_v.h, 1)
    p.add("vector", dsos_prep)
    dsos_ms = (ch_v, cdsos)

    if upto in ("ffn", "full"):
        xn_a = emit_ln2(ctx, 0, x1_ms)
        ffa = emit_ffn_span(ctx, 0, xn_a, "f0")
        xn_b = emit_ln2(ctx, 1, x1_ms)
        ffb = emit_ffn_span(ctx, 1, xn_b, "f1")
        x2_ms = ffb
    else:
        x2_ms = x1_ms

    def store_x(e, x2_ms=x2_ms):
        wms(e, x2_ms)
        e.dma_start(out=X2Td.rearrange("(c p) t -> p c t", c=NCH),
                    in_=X[:].rearrange("p (c t) -> p c t", c=NCH)).then_inc(s_st[0].h, 16)
    p.add("sync", store_x)
    s_st[0].bump(16)

    if upto == "full":
        ca1, ca2 = ch_a.bump(1), ch_a.bump(1)

        def decay_prep(e, ca1=ca1, ca2=ca2):
            e.activation(E1[:], TD[:], AF.Exp).then_inc(ch_a.h, 1)
            e.wait_ge(ch_a.h, ca1)
            e.activation(DEC[:], E1[:], AF.Exp, scale=-1.0).then_inc(ch_a.h, 1)
            e.wait_ge(ch_a.h, ca2)
            e.activation(FEFF[:], TF[:], AF.Exp, scale=IND[:, 0:1]).then_inc(s_a.h, 1)
        p.add("scalar", decay_prep)
        decay_ms = (s_a, s_a.bump(1))

        es_b = ExitStack()
        sbb = lambda name, shape, dt: es_b.enter_context(nc.sbuf_tensor(name, shape, dt))
        WT = sbb("WT", [128, SPAN], F32)
        AA = sbb("AA", [128, SPAN], F32)
        NUMT = sbb("NUMT", [128, SPAN], F32)
        DENT = sbb("DENT", [128, SPAN], F32)
        SIGT = sbb("SIGT", [128, SPAN], F32)
        DECB = sbb("DECB", [128, SPAN], F32)
        ONES32 = sbb("ONES32", [128, SPAN], F32)
        WB3 = [sbb(f"WB3{i}", [128, 8 * 128], BF16) for i in range(6)]

        def consts2(e):
            e.memset(ONES32[:], 1.0).then_inc(s_g.h, 1)
        p.add("gpsimd", consts2)
        consts2_done = s_g.bump()

        for sp in range(2):
            xnb_ms = emit_ln2(ctx, sp, x2_ms)
            for cc in range(NCH):
                kbank = banks[0 + 3 * (cc % 2)]
                vbank = banks[1 + 3 * (cc % 2)]
                rbank = banks[2 + 3 * (cc % 2)]
                step = sp * NCH + cc
                my_mm = s_ts.n + 6
                set_ms = []
                for wj in range(3):
                    for half in range(2):
                        bslot = wj * 2 + (cc % 2)
                        guard = mm_done.get(("b3", step - 2, wj))
                        prev_half = (s_ts, s_ts.n + wj * 2 + half) if half else None

                        def wload(e, wj=wj, cc=cc, half=half, bslot=bslot, guard=guard,
                                  hguard=mm_done.get(("b3h", step, wj)) if half else None):
                            if guard:
                                e.wait_ge(s_ts.h, guard)
                            if hguard:
                                e.wait_ge(s_ts.h, hguard)
                            src = bWTd[wj].rearrange("(c p) cd -> p c cd", c=NCH)[
                                :, half * 8:(half + 1) * 8, cc * 128:(cc + 1) * 128]
                            e.dma_start(out=WB3[bslot][:].rearrange("p (c cd) -> p c cd", c=8),
                                        in_=src).then_inc(s_w3[bslot].h, 16)
                        p.add("gpsimd", wload)
                        set_ms.append((s_w3[bslot], s_w3[bslot].bump(16), bslot, wj, half))
                        hm = s_ts.n + 1

                        def mmh(e, wj=wj, half=half, cc=cc, bslot=bslot,
                                sl=set_ms[-1], xnb_ms=xnb_ms,
                                ap_=(kbank.ap, vbank.ap, rbank.ap)[wj],
                                rd=(kbank.r, vbank.r, rbank.r)[wj]):
                            e.wait_ge(sl[0].h, sl[1])
                            if wj == 0 and half == 0 and cc == 0:
                                wms(e, xnb_ms)
                            if half == 0:
                                wms(e, rd)
                            for i in range(8):
                                dchunk = half * 8 + i
                                inst = e.matmul(ap_[:],
                                                WB3[bslot][:, i * 128:(i + 1) * 128],
                                                XN[:, dchunk * SPAN:(dchunk + 1) * SPAN],
                                                start=(dchunk == 0), stop=(dchunk == NCH - 1))
                            inst.then_inc(s_ts.h, 1)
                        p.add("tensor", mmh)
                        s_ts.bump(1)
                        mm_done[("b3h", step, wj)] = s_ts.n
                    mm_done[("b3", step, wj)] = s_ts.n
                kbank.w = vbank.w = rbank.w = (s_ts, my_mm)

                def sig_f(e, rap=rbank.ap, my_mm=my_mm, step=step):
                    e.wait_ge(s_ts.h, my_mm)
                    if step:
                        e.wait_ge(s_stx.h, 48 * step)
                    e.activation(SIGT[:], rap[:], AF.Sigmoid).then_inc(s_a.h, 1)
                p.add("scalar", sig_f)
                sig_ms = (s_a, s_a.bump(1))
                rbank.r = sig_ms

                clip_v = s_v.bump(1)
                cbase = ch_v.n
                exp_done_val = s_a.n + 1

                def kvchain(e, kap=kbank.ap, vap=vbank.ap, my_mm=my_mm, cc=cc, sp=sp,
                            cbase=cbase, decay_ms=decay_ms, step=step,
                            exp_done=exp_done_val, c2_done=consts2_done):
                    e.wait_ge(s_ts.h, my_mm)
                    wms(e, decay_ms)
                    e.wait_ge(s_g.h, c2_done)
                    if step:
                        e.wait_ge(s_stx.h, 48 * step)
                    e.tensor_scalar(WT[:], kap[:], 10.0, -10.0, AluOpType.min,
                                    AluOpType.max).then_inc(s_v.h, 1)
                    e.wait_ge(s_a.h, exp_done)
                    e.tensor_tensor(AA[:], WT[:], vap[:], AluOpType.mult).then_inc(ch_v.h, 1)
                    e.tensor_scalar(DECB[:], ONES32[:], DEC[:, cc:cc + 1], None,
                                    AluOpType.mult).then_inc(ch_v.h, 1)
                    e.wait_ge(ch_v.h, cbase + 2)
                    e.tensor_tensor_scan(NUMT[:], DECB[:], AA[:], 0.0, AluOpType.mult,
                                         AluOpType.add).then_inc(ch_v.h, 1)
                    e.tensor_tensor_scan(DENT[:], DECB[:], WT[:], 0.0, AluOpType.mult,
                                         AluOpType.add).then_inc(ch_v.h, 1)
                    e.wait_ge(ch_v.h, cbase + 4)
                    e.tensor_copy(SUMS[:, cc * 4 + 2 * sp:cc * 4 + 2 * sp + 1],
                                  NUMT[:, SPAN - 1:SPAN]).then_inc(ch_v.h, 1)
                    e.tensor_copy(SUMS[:, cc * 4 + 2 * sp + 1:cc * 4 + 2 * sp + 2],
                                  DENT[:, SPAN - 1:SPAN]).then_inc(s_v.h, 1)
                p.add("vector", kvchain)
                ch_v.bump(5)
                scan_ms = (s_v, s_v.bump(1))
                kbank.r = scan_ms
                vbank.r = scan_ms

                ca = ch_a.bump(1)

                def exp_w(e, clip_v=clip_v, cc=cc, sp=sp, ca=ca):
                    e.wait_ge(s_v.h, clip_v)
                    e.activation(WT[:], WT[:], AF.Exp).then_inc(ch_a.h, 1)
                    e.wait_ge(ch_a.h, ca)
                    e.activation(WT[:, 0:1], WT[:, 0:1], AF.Copy,
                                 scale=FEFF[:, cc:cc + 1] if sp == 0 else 1.0).then_inc(s_a.h, 1)
                p.add("scalar", exp_w)
                s_a.bump(1)

                def stores(e, cc=cc, sp=sp, scan_ms=scan_ms, sig_ms=sig_ms):
                    wms(e, scan_ms)
                    wms(e, sig_ms)
                    e.dma_start(out=NUMLd[cc * 128:(cc + 1) * 128, sp * SPAN:(sp + 1) * SPAN],
                                in_=NUMT[:]).then_inc(s_stx.h, 16)
                    e.dma_start(out=DENLd[cc * 128:(cc + 1) * 128, sp * SPAN:(sp + 1) * SPAN],
                                in_=DENT[:]).then_inc(s_stx.h, 16)
                    e.dma_start(out=SIGd[cc * 128:(cc + 1) * 128, sp * SPAN:(sp + 1) * SPAN],
                                in_=SIGT[:]).then_inc(s_stx.h, 16)
                p.add("sync", stores)
                s_stx.bump(48)

        def store_sums(e, n=s_v.n):
            e.wait_ge(s_v.h, n)
            e.dma_start(out=SUMSd, in_=SUMS[:]).then_inc(s_st[1].h, 16)
        p.add("sync", store_sums)
        s_st[1].bump(16)
        es_b.close()

    def fin(e, a=s_st[0].n, b=s_st[1].n, c=s_stx.n):
        e.wait_ge(s_st[0].h, a)
        if b:
            e.wait_ge(s_st[1].h, b)
        if c:
            e.wait_ge(s_stx.h, c)
    p.add("sync", fin)

    p.emit(es.enter_context(nc.Block()))
    es.close()
    return nc


def stage_l2(inputs, l1_outs, core):
    a0, b0 = span_starts(core)
    toks = np.r_[a0:a0 + SPAN, b0:b0 + SPAN]
    x = np.asarray(inputs["L0_attn_a_x"])[0]
    xTc = np.ascontiguousarray(x[toks, :].T)

    kT_full = np.zeros((D, T), BF)
    vT_full = np.zeros((D, T), BF)
    for c in range(NCORES):
        ca, cb = span_starts(c)
        kT_full[:, ca:ca + SPAN] = l1_outs[c]["kT"][:, 0:SPAN]
        kT_full[:, cb:cb + SPAN] = l1_outs[c]["kT"][:, SPAN:2 * SPAN]
        vT_full[:, ca:ca + SPAN] = l1_outs[c]["vT"][:, 0:SPAN]
        vT_full[:, cb:cb + SPAN] = l1_outs[c]["vT"][:, SPAN:2 * SPAN]
    v_full = np.ascontiguousarray(vT_full.T)

    def kwin(sp):
        q0 = a0 if sp == 0 else b0
        blk0 = q0 // 128
        KW = np.zeros((D, KWIN * 128), BF)
        VWm = np.zeros((KWIN * 128, D), BF)
        for i in range(KWIN):
            g = blk0 + (i - 64)
            if 0 <= g < T // 128:
                KW[:, 128 * i:128 * (i + 1)] = kT_full[:, 128 * g:128 * (g + 1)]
                VWm[128 * i:128 * (i + 1), :] = v_full[128 * g:128 * (g + 1), :]
        return KW, VWm

    KWA, VWA = kwin(0)
    KWB, VWB = kwin(1)
    maskarr, _ = masks_and_offs()

    def bT(b):
        return np.ascontiguousarray(np.asarray(b).reshape(NCH, 128).T.astype(np.float32))

    U = np.asarray(inputs["L0_ffn_up_weight"])
    Dw = np.asarray(inputs["L0_ffn_down_weight"])
    return {
        "XT": xTc,
        "QT": l1_outs[core]["qT"],
        "KWA": KWA, "KWB": KWB, "VWA": VWA, "VWB": VWB,
        "BTAB": build_bias_table(core),
        "MASKS": maskarr,
        "Wo": np.asarray(inputs["L0_attn_a_Wo"]),
        "boT": bT(inputs["L0_attn_a_bo"]),
        "UTS": l1_outs[core]["UTS"],
        "DTS": l1_outs[core]["DTS"],
        "upsT": np.ascontiguousarray(np.asarray(inputs["L0_ffn_up_scale"])
                                     .reshape(DH // 128, 128).T.astype(np.float32)),
        "dsT": bT(inputs["L0_ffn_down_scale"]),
        "osc": np.broadcast_to(np.asarray(inputs["L0_ffn_output_scale"]).reshape(1, 1),
                               (128, 1)).copy(),
        "bWkT": np.ascontiguousarray(np.asarray(inputs["L0_attn_b_Wk"]).T),
        "bWvT": np.ascontiguousarray(np.asarray(inputs["L0_attn_b_Wv"]).T),
        "bWrT": np.ascontiguousarray(np.asarray(inputs["L0_attn_b_Wr"]).T),
        "tdT": np.ascontiguousarray(np.asarray(inputs["L0_attn_b_time_decay"]).T.astype(np.float32)),
        "tfT": np.ascontiguousarray(np.asarray(inputs["L0_attn_b_time_first"]).T.astype(np.float32)),
        "FIRSTIND": np.full((128, 1), 1.0 if core == 0 else 0.0, np.float32),
    }



def build_l3():
    nc = bass.Bass("TRN2", target_bir_lowering=False)
    dram = lambda n, shape, dt, kind="ExternalInput": nc.dram_tensor(n, shape, dt, kind=kind).ap()
    X2Td = dram("X2T", [D, TPC], F32)
    NUMLd = dram("NUML", [D, TPC], F32)
    DENLd = dram("DENL", [D, TPC], F32)
    SIGd = dram("SIG", [D, TPC], F32)
    SUMNd = dram("SUMN", [128, NCH * 16], F32)
    SUMDd = dram("SUMD", [128, NCH * 16], F32)
    SELAd = dram("SELA", [128, 16], F32)
    SELBd = dram("SELB", [128, 16], F32)
    bWoTd = dram("bWoT", [D, D], F32)
    UTSd = dram("UTS", [D, DH], BF16)
    DTSd = dram("DTS", [DH, D], BF16)
    upsTd = dram("upsT", [128, DH // 128], F32)
    dsTd = dram("dsT", [128, NCH], F32)
    oscd = dram("osc", [128, 1], F32)
    tdTd = dram("tdT", [128, NH], F32)
    XOUTd = dram("XOUT", [D, TPC], F32, "ExternalOutput")

    es = ExitStack()
    sb = lambda name, shape, dt: es.enter_context(nc.sbuf_tensor(name, shape, dt))

    X = sb("X", [128, NCH * TPC], F32)
    XN = sb("XN", [128, NCH * SPAN], BF16)
    AR = sb("AR", [128, 2 * NCH * TPC], BF16)
    WS = [sb(f"WS{i}", [128, 4096], BF16) for i in range(2)]
    ONESC = sb("ONESC", [128, 1], BF16)
    ONESR = sb("ONESR", [1, 128], F32)
    EPS1 = sb("EPS1", [1, 1], F32)
    EPS6 = sb("EPS6", [128, 1], F32)
    STATR = sb("STATR", [1, 2 * SPAN], F32)
    ABrow = sb("ABrow", [1, 2 * SPAN], F32)
    ABC = sb("ABC", [128, 2 * SPAN], F32)
    TMP = sb("TMP", [128, SPAN], F32)
    XBF2 = [sb(f"XBF2{i}", [128, SPAN], BF16) for i in range(2)]
    SQ2 = [sb(f"SQ2{i}", [128, SPAN], BF16) for i in range(2)]
    UPS = sb("UPS", [128, DH // 128], F32)
    DST = sb("DST", [128, NCH], F32)
    OSC = sb("OSC", [128, 1], F32)
    DSOS = sb("DSOS", [128, NCH], F32)
    ZB = sb("ZB", [128, NCH], F32)
    SUMN = sb("SUMNs", [128, NCH * 16], F32)
    SUMD = sb("SUMDs", [128, NCH * 16], F32)
    SELA = sb("SELAs", [128, 16], F32)
    SELB = sb("SELBs", [128, 16], F32)
    CARN = sb("CARN", [128, NCH * 16], F32)
    CARD = sb("CARD", [128, NCH * 16], F32)
    CAN = [sb(f"CAN{i}", [128, NCH], F32) for i in range(2)]
    CAD = [sb(f"CAD{i}", [128, NCH], F32) for i in range(2)]
    SELT = sb("SELT", [128, 16], F32)
    TD = sb("TD", [128, NH], F32)
    E1 = sb("E1", [128, NH], F32)
    NE1 = sb("NE1", [128, NH], F32)
    D512 = sb("D512", [128, NH], F32)
    IOTA1 = sb("IOTA1", [128, SPAN], F32)
    POW = sb("POW", [128, SPAN], F32)
    TMPP = sb("TMPP", [128, SPAN], F32)
    NUMT = sb("NUMT", [128, SPAN], F32)
    DENT = sb("DENT", [128, SPAN], F32)
    SGT = sb("SGT", [128, SPAN], F32)
    RECIP = sb("RECIP", [128, SPAN], F32)
    WKVF = sb("WKVF", [128, SPAN], F32)

    PS = [es.enter_context(nc.psum_tensor(f"PS{i}", [128, SPAN], F32)) for i in range(8)]
    banks = [Bank(t) for t in PS]

    sems = {k: Sem(es.enter_context(nc.semaphore("s_" + k)))
            for k in ("ld", "ldx", "v", "a", "ts", "g", "wl0", "wl1", "wls0", "wls1",
                      "st0", "chv", "cha")}
    s_ld, s_v, s_a, s_g = sems["ld"], sems["v"], sems["a"], sems["g"]
    s_ldx, s_ts = sems["ldx"], sems["ts"]
    ch_v, ch_a = sems["chv"], sems["cha"]
    s_st = sems["st0"]

    p = Plan()
    mm_done = {}
    ws_last = {0: None, 1: None}
    ctx = {'core': (p, banks, sems, X, XN, AR, WS, ONESC, ONESR, EPS1, STATR, ABrow, ABC, TMP,
                    XBF2, SQ2, UPS, DSOS, UTSd, DTSd, mm_done, ws_last)}

    def loads(e):
        e.dma_start(out=X[:].rearrange("p (c t) -> p c t", c=NCH),
                    in_=X2Td.rearrange("(c p) t -> p c t", c=NCH)).then_inc(s_ld.h, 16)
        e.dma_start(out=SUMN[:], in_=SUMNd).then_inc(s_ld.h, 16)
        e.dma_start(out=SUMD[:], in_=SUMDd).then_inc(s_ld.h, 16)
        e.dma_start(out=SELA[:], in_=SELAd).then_inc(s_ld.h, 16)
        e.dma_start(out=SELB[:], in_=SELBd).then_inc(s_ld.h, 16)
        e.dma_start(out=UPS[:], in_=upsTd).then_inc(s_ld.h, 16)
        e.dma_start(out=DST[:], in_=dsTd).then_inc(s_ld.h, 16)
        e.dma_start(out=OSC[:], in_=oscd).then_inc(s_ld.h, 16)
        e.dma_start(out=TD[:], in_=tdTd).then_inc(s_ld.h, 16)
    p.add("sync", loads)
    loads_done = s_ld.bump(16 * 9)

    def consts(e):
        e.memset(ONESC[:], 1.0)
        e.memset(ONESR[:], 1.0)
        e.memset(EPS1[:], LN_EPS)
        e.memset(EPS6[:], 1e-6)
        e.memset(ZB[:], 0.0)
        e.memset(CARN[:], 0.0)
        e.memset(CARD[:], 0.0)
        e.iota(IOTA1[:], [[1, SPAN]], base=1, channel_multiplier=0,
               allow_small_or_imprecise_dtypes=True).then_inc(s_g.h, 1)
    p.add("gpsimd", consts)
    consts_done = s_g.bump()

    for eng in ("tensor", "scalar", "vector"):
        def gate(e, _ld=loads_done, _g=consts_done):
            e.wait_ge(s_ld.h, _ld)
            e.wait_ge(s_g.h, _g)
        p.add(eng, gate)

    ca1 = ch_a.bump(1)

    def decay_f(e, ca1=ca1):
        e.activation(E1[:], TD[:], AF.Exp).then_inc(ch_a.h, 1)
        e.wait_ge(ch_a.h, ca1)
        e.activation(D512[:], E1[:], AF.Exp, scale=-512.0).then_inc(s_a.h, 1)
    p.add("scalar", decay_f)
    e1_done = s_a.bump(1)
    cv1 = ch_v.bump(1)

    def ne1_f(e, e1_done=e1_done, cv1=cv1):
        e.wait_ge(s_a.h, e1_done)
        e.tensor_scalar(NE1[:], E1[:], -1.0, None, AluOpType.mult).then_inc(ch_v.h, 1)
    p.add("vector", ne1_f)
    ne1_done = (ch_v, cv1)

    carry_c0 = ch_v.n

    def carry_f(e, e1_done=e1_done, c0=carry_c0):
        e.wait_ge(s_a.h, e1_done)
        step = 0
        for s_ in range(1, 16):
            if s_ > 1:
                e.wait_ge(ch_v.h, c0 + step)
            for cc in range(NCH):
                i0 = cc * 16 + s_
                e.scalar_tensor_tensor(CARN[:, i0:i0 + 1], CARN[:, i0 - 1:i0], D512[:, cc:cc + 1],
                                       SUMN[:, i0 - 1:i0], AluOpType.mult, AluOpType.add)
                inst = e.scalar_tensor_tensor(CARD[:, i0:i0 + 1], CARD[:, i0 - 1:i0], D512[:, cc:cc + 1],
                                              SUMD[:, i0 - 1:i0], AluOpType.mult, AluOpType.add)
            step += 1
            inst.then_inc(ch_v.h, 1)
        e.wait_ge(ch_v.h, c0 + 15)
        k = c0 + 15
        for spi, SEL in ((0, SELA), (1, SELB)):
            for car, dst in ((CARN, CAN[spi]), (CARD, CAD[spi])):
                for cc in range(NCH):
                    e.wait_ge(ch_v.h, k)
                    e.tensor_tensor(SELT[:], car[:, cc * 16:(cc + 1) * 16], SEL[:],
                                    AluOpType.mult).then_inc(ch_v.h, 1)
                    k += 1
                    e.wait_ge(ch_v.h, k)
                    e.reduce_sum(dst[:, cc:cc + 1], SELT[:], axis=mybir.AxisListType.X
                                 ).then_inc(ch_v.h, 1)
                    k += 1
        e.nop().then_inc(s_v.h, 1)
    p.add("vector", carry_f)
    ch_v.bump(15 + 4 * NCH * 2)
    carry_done = s_v.bump(1)

    cdsos = ch_v.bump(1)

    def dsos_prep(e, cdsos=cdsos):
        e.tensor_scalar(DSOS[:], DST[:], OSC[:, 0:1], None, AluOpType.mult).then_inc(ch_v.h, 1)
    p.add("vector", dsos_prep)

    rw_ms = None
    prev_pow_done = 0
    prev_corr1 = None
    for cc in range(NCH):
        def pow_ts(e, cc=cc, ne1_done=ne1_done, prev_pow_done=prev_pow_done):
            wms(e, ne1_done)
            if prev_pow_done:
                e.wait_ge(s_a.h, prev_pow_done)
            e.tensor_scalar(TMPP[:], IOTA1[:], NE1[:, cc:cc + 1], None,
                            AluOpType.mult).then_inc(s_v.h, 1)
        p.add("vector", pow_ts)
        tspw = s_v.bump(1)

        def exp_pow(e, tspw=tspw, prev_corr1=prev_corr1):
            e.wait_ge(s_v.h, tspw)
            wms(e, prev_corr1)
            e.activation(POW[:], TMPP[:], AF.Exp).then_inc(s_a.h, 1)
        p.add("scalar", exp_pow)
        pow_done = s_a.bump(1)
        prev_pow_done = pow_done

        for sp in range(2):
            ld0 = s_ldx.n
            prev_use = rw_ms

            def tload(e, cc=cc, sp=sp, prev_use=prev_use):
                wms(e, prev_use)
                e.dma_start(out=NUMT[:], in_=NUMLd[cc * 128:(cc + 1) * 128,
                                               sp * SPAN:(sp + 1) * SPAN]).then_inc(s_ldx.h, 16)
                e.dma_start(out=DENT[:], in_=DENLd[cc * 128:(cc + 1) * 128,
                                               sp * SPAN:(sp + 1) * SPAN]).then_inc(s_ldx.h, 16)
                e.dma_start(out=SGT[:], in_=SIGd[cc * 128:(cc + 1) * 128,
                                              sp * SPAN:(sp + 1) * SPAN]).then_inc(s_ldx.h, 16)
            p.add("sync", tload)
            tl_done = s_ldx.bump(48)
            cfix = ch_v.bump(1)

            def correct(e, cc=cc, sp=sp, tl_done=tl_done, pow_done=pow_done,
                        carry_done=carry_done, cfix=cfix):
                e.wait_ge(s_ldx.h, tl_done)
                e.wait_ge(s_a.h, pow_done)
                e.wait_ge(s_v.h, carry_done)
                e.scalar_tensor_tensor(NUMT[:], POW[:], CAN[sp][:, cc:cc + 1], NUMT[:],
                                       AluOpType.mult, AluOpType.add).then_inc(ch_v.h, 1)
                e.scalar_tensor_tensor(DENT[:], POW[:], CAD[sp][:, cc:cc + 1], DENT[:],
                                       AluOpType.mult, AluOpType.add).then_inc(s_v.h, 1)
            p.add("vector", correct)
            corr_done = s_v.bump(1)
            ca2 = ch_a.bump(1)

            def recip_f(e, corr_done=corr_done, ca2=ca2):
                e.wait_ge(s_v.h, corr_done)
                e.activation(RECIP[:], DENT[:], AF.Ln, bias=EPS6[:, 0:1]).then_inc(ch_a.h, 1)
                e.wait_ge(ch_a.h, ca2)
                e.activation(RECIP[:], RECIP[:], AF.Exp, scale=-1.0).then_inc(s_a.h, 1)
            p.add("scalar", recip_f)
            recip_done = s_a.bump(1)
            cwv = ch_v.bump(1)

            def rwkv_f(e, cc=cc, sp=sp, recip_done=recip_done, cwv=cwv):
                e.wait_ge(s_a.h, recip_done)
                e.tensor_tensor(WKVF[:], NUMT[:], RECIP[:], AluOpType.mult).then_inc(ch_v.h, 1)
                e.wait_ge(ch_v.h, cwv)
                e.tensor_tensor(AR[:, cc * TPC + sp * SPAN:cc * TPC + (sp + 1) * SPAN],
                                WKVF[:], SGT[:], AluOpType.mult).then_inc(s_v.h, 1)
            p.add("vector", rwkv_f)
            rw_ms = (s_v, s_v.bump(1))
            if sp == 1:
                prev_corr1 = (s_v, corr_done)
    rw_all = rw_ms

    x3_ms = emit_proj_to_x(ctx, 0, bWoTd, ZB, rw_all, "bwo")
    xn_a = emit_ln2(ctx, 0, x3_ms)
    ffa = emit_ffn_span(ctx, 0, xn_a, "g0")
    xn_b = emit_ln2(ctx, 1, x3_ms)
    ffb = emit_ffn_span(ctx, 1, xn_b, "g1")

    def store_x(e, ffb=ffb):
        wms(e, ffb)
        e.dma_start(out=XOUTd.rearrange("(c p) t -> p c t", c=NCH),
                    in_=X[:].rearrange("p (c t) -> p c t", c=NCH)).then_inc(s_st.h, 16)
    p.add("sync", store_x)
    s_st.bump(16)

    def fin(e, a=s_st.n):
        e.wait_ge(s_st.h, a)
    p.add("sync", fin)

    p.emit(es.enter_context(nc.Block()))
    es.close()
    return nc


def stage_l3(inputs, l1_outs, l2_outs, core):
    a0, b0 = span_starts(core)
    SUMN = np.zeros((128, NCH * 16), np.float32)
    SUMD = np.zeros((128, NCH * 16), np.float32)
    for s_ in range(16):
        if s_ < 8:
            oc, osp = s_, 0
        else:
            oc, osp = 15 - s_, 1
        sm = l2_outs[oc]["SUMS"]
        for cc in range(NCH):
            SUMN[:, cc * 16 + s_] = sm[:, cc * 4 + 2 * osp]
            SUMD[:, cc * 16 + s_] = sm[:, cc * 4 + 2 * osp + 1]
    SELA = np.zeros((128, 16), np.float32)
    SELB = np.zeros((128, 16), np.float32)
    SELA[:, core] = 1.0
    SELB[:, 15 - core] = 1.0

    def bT(b):
        return np.ascontiguousarray(np.asarray(b).reshape(NCH, 128).T.astype(np.float32))

    return {
        "X2T": l2_outs[core]["X2T"],
        "NUML": l2_outs[core]["NUML"],
        "DENL": l2_outs[core]["DENL"],
        "SIG": l2_outs[core]["SIG"],
        "SUMN": SUMN, "SUMD": SUMD, "SELA": SELA, "SELB": SELB,
        "bWoT": np.ascontiguousarray(np.asarray(inputs["L0_attn_b_Wo"]).T),
        "UTS": l1_outs[core]["UTS"],
        "DTS": l1_outs[core]["DTS"],
        "upsT": np.ascontiguousarray(np.asarray(inputs["L0_ffn_up_scale"])
                                     .reshape(DH // 128, 128).T.astype(np.float32)),
        "dsT": bT(inputs["L0_ffn_down_scale"]),
        "osc": np.broadcast_to(np.asarray(inputs["L0_ffn_output_scale"]).reshape(1, 1),
                               (128, 1)).copy(),
        "tdT": np.ascontiguousarray(np.asarray(inputs["L0_attn_b_time_decay"]).T.astype(np.float32)),
    }


_NC_CACHE = {}


def kernel(**inputs):
    if "l1" not in _NC_CACHE:
        _NC_CACHE["l1"] = build_l1()
        _NC_CACHE["l2"] = build_l2("full")
        _NC_CACHE["l3"] = build_l3()
    cores = list(range(NCORES))
    r1 = run_bass_kernel_spmd(_NC_CACHE["l1"], stage_l1(inputs), core_ids=cores)
    l1_outs = r1.results
    r2 = run_bass_kernel_spmd(_NC_CACHE["l2"],
                              [stage_l2(inputs, l1_outs, c) for c in cores], core_ids=cores)
    l2_outs = r2.results
    r3 = run_bass_kernel_spmd(_NC_CACHE["l3"],
                              [stage_l3(inputs, l1_outs, l2_outs, c) for c in cores],
                              core_ids=cores)
    out = np.zeros((T, D), np.float32)
    for c in cores:
        a0, b0 = span_starts(c)
        xo = r3.results[c]["XOUT"]
        out[a0:a0 + SPAN] = xo[:, 0:SPAN].T
        out[b0:b0 + SPAN] = xo[:, SPAN:2 * SPAN].T
    return out[None, :, :]


# version 05
# speedup vs baseline: 1.1358x; 1.1358x over previous
import math
import numpy as np
import ml_dtypes
from contextlib import ExitStack

import concourse.bass as bass
import concourse.mybir as mybir
from concourse.alu_op_type import AluOpType
from concourse.bass_utils import run_bass_kernel_spmd

F32 = mybir.dt.float32
BF16 = mybir.dt.bfloat16
AF = mybir.ActivationFunctionType
BF = ml_dtypes.bfloat16

NCORES = 8
T = 8192
D = 2048
NH = 16
HDIM = 128
DH = 8192
NCH = 16
TPC = 1024
SPAN = 512
QSCALE = 1.0 / math.sqrt(HDIM)
LN_EPS = 1e-5

SLOPES = [2.0 ** (-0.5 * (i + 1)) for i in range(NH)]


def span_starts(c):
    return 512 * c, 7680 - 512 * c


class Sem:
    def __init__(self, handle):
        self.h = handle
        self.n = 0

    def bump(self, k=1):
        self.n += k
        return self.n


class Plan:
    ENGS = ("tensor", "vector", "scalar", "gpsimd", "sync")

    def __init__(self):
        self.ops = {e: [] for e in self.ENGS}

    def add(self, eng, fn):
        self.ops[eng].append(fn)

    def emit(self, block):
        plan = self

        @block.sync
        def _(e):
            for fn in plan.ops["sync"]:
                fn(e)

        @block.tensor
        def _(e):
            for fn in plan.ops["tensor"]:
                fn(e)

        @block.vector
        def _(e):
            for fn in plan.ops["vector"]:
                fn(e)

        @block.scalar
        def _(e):
            for fn in plan.ops["scalar"]:
                fn(e)

        @block.gpsimd
        def _(e):
            for fn in plan.ops["gpsimd"]:
                fn(e)


def chunked_colmajor(ap, nch, per):
    return ap.rearrange("(c p) t -> p (c t)", c=nch)



def emit_layernorm(p, sems, bufs, nch, t_stride, t0, span, src_X, dst_XN, dst_dtype_note=""):
    s_v, s_a, s_t, s_g, s_gx = sems["v"], sems["a"], sems["t"], sems["g"], sems["gx"]
    XBF2, SQ2 = bufs["XBF2"], bufs["SQ2"]
    ONESC, ONESR, EPS1 = bufs["ONESC"], bufs["ONESR"], bufs["EPS1"]
    STATR, AB, ABC, TMP = bufs["STATR"], bufs["AB"], bufs["ABC"], bufs["TMP"]
    PS_MU, PS_SQ, PS_AB = bufs["PS_MU"], bufs["PS_SQ"], bufs["PS_AB"]
    consts_ready = bufs["consts_ready"]
    x_ready = bufs["x_ready"]
    ch_v = sems["chv"]

    c0 = ch_v.n
    tchk0 = s_t.n

    def cast_chunks(e, c0=c0, tchk0=tchk0):
        for i in range(nch):
            if i >= 2:
                e.wait_ge(s_t.h, tchk0 + i - 1)
            e.tensor_copy(XBF2[i % 2][:], src_X[:, i * t_stride + t0:i * t_stride + t0 + span]
                          ).then_inc(ch_v.h, 1)
            e.wait_ge(ch_v.h, c0 + 2 * i + 1)
            e.tensor_tensor(SQ2[i % 2][:], XBF2[i % 2][:], XBF2[i % 2][:],
                            AluOpType.mult).then_inc(ch_v.h, 1)
    p.add("vector", cast_chunks)
    ch_v.bump(2 * nch)

    def stats_mm(e, c0=c0):
        e.wait_ge(s_g.h, consts_ready)
        for i in range(nch):
            e.wait_ge(ch_v.h, c0 + 2 * i + 2)
            e.matmul(PS_MU[0:1, :], ONESC[:], XBF2[i % 2][:],
                     start=(i == 0), stop=(i == nch - 1))
            inst = e.matmul(PS_SQ[0:1, :], ONESC[:], SQ2[i % 2][:],
                            start=(i == 0), stop=(i == nch - 1))
            inst.then_inc(s_t.h, 1)
    p.add("tensor", stats_mm)
    stats_done = s_t.bump(nch)
    bufs["stats_done_out"] = stats_done

    def stats_fin(e):
        e.wait_ge(s_t.h, stats_done)
        e.tensor_scalar(STATR[0:1, 0:span], PS_MU[0:1, :], 1.0 / D, None, AluOpType.mult)
        e.tensor_scalar(STATR[0:1, span:2 * span], PS_SQ[0:1, :], 1.0 / D, None,
                        AluOpType.mult).then_inc(ch_v.h, 1)
        e.wait_ge(ch_v.h, stats_fin.c1)
        e.tensor_tensor(AB[0:1, span:2 * span], STATR[0:1, 0:span], STATR[0:1, 0:span],
                        AluOpType.mult).then_inc(ch_v.h, 1)
        e.wait_ge(ch_v.h, stats_fin.c2)
        e.tensor_tensor(AB[0:1, span:2 * span], STATR[0:1, span:2 * span], AB[0:1, span:2 * span],
                        AluOpType.subtract).then_inc(s_v.h, 1)
    stats_fin.c1 = ch_v.bump()
    stats_fin.c2 = ch_v.bump()
    p.add("vector", stats_fin)
    var_done = s_v.bump()

    ch_a = sems["cha"]

    def rstd_f(e):
        e.wait_ge(s_v.h, var_done)
        e.activation(AB[0:1, 0:span], AB[0:1, span:2 * span], AF.Ln, bias=EPS1[0:1, 0:1],
                     scale=1.0).then_inc(ch_a.h, 1)
        e.wait_ge(ch_a.h, rstd_f.c1)
        e.activation(AB[0:1, 0:span], AB[0:1, 0:span], AF.Exp, scale=-0.5).then_inc(s_a.h, 1)
    rstd_f.c1 = ch_a.bump()
    p.add("scalar", rstd_f)
    rstd_done = s_a.bump()

    def brow_f(e):
        e.wait_ge(s_a.h, rstd_done)
        e.tensor_tensor(AB[0:1, span:2 * span], STATR[0:1, 0:span], AB[0:1, 0:span],
                        AluOpType.mult).then_inc(ch_v.h, 1)
        e.wait_ge(ch_v.h, brow_f.c1)
        e.tensor_scalar(AB[0:1, span:2 * span], AB[0:1, span:2 * span], -1.0, None,
                        AluOpType.mult).then_inc(s_v.h, 1)
    brow_f.c1 = ch_v.bump()
    p.add("vector", brow_f)
    brow_done = s_v.bump()

    def bcast_mm(e):
        e.wait_ge(s_v.h, brow_done)
        e.matmul(PS_AB[:, 0:span], ONESR[:], AB[0:1, 0:span], start=True, stop=True)
        e.matmul(PS_AB[:, span:2 * span], ONESR[:], AB[0:1, span:2 * span],
                 start=True, stop=True).then_inc(s_t.h, 1)
    p.add("tensor", bcast_mm)
    bcast_done = s_t.bump()

    def evac_bcast(e):
        e.wait_ge(s_t.h, bcast_done)
        e.tensor_copy(ABC[:, 0:span], PS_AB[:, 0:span])
        e.tensor_copy(ABC[:, span:2 * span], PS_AB[:, span:2 * span]).then_inc(s_v.h, 1)
    p.add("vector", evac_bcast)
    abc_done = s_v.bump()

    def apply_f(e):
        e.wait_ge(s_v.h, abc_done)
        for i in range(nch):
            if i:
                e.wait_ge(ch_v.h, apply_f.c0 + 2 * i)
            e.tensor_tensor(TMP[:, 0:span], src_X[:, i * t_stride + t0:i * t_stride + t0 + span],
                            ABC[:, 0:span], AluOpType.mult).then_inc(ch_v.h, 1)
            e.wait_ge(ch_v.h, apply_f.c0 + 2 * i + 1)
            e.tensor_tensor(dst_XN[:, i * t_stride + t0:i * t_stride + t0 + span], TMP[:, 0:span],
                            ABC[:, span:2 * span], AluOpType.add).then_inc(ch_v.h, 1)
    apply_f.c0 = ch_v.n
    done = ch_v.bump(2 * nch)
    p.add("vector", apply_f)
    return done



def build_l1():
    nc = bass.Bass("TRN2", target_bir_lowering=False)
    xT = nc.dram_tensor("xT", [D, TPC], F32, kind="ExternalInput").ap()
    Ws = {n: nc.dram_tensor(n, [D, D], F32, kind="ExternalInput").ap() for n in ("Wq", "Wk", "Wv")}
    bTs = {n: nc.dram_tensor(n, [128, NCH], F32, kind="ExternalInput").ap()
           for n in ("bqT", "bkT", "bvT")}
    OUTs = {n: nc.dram_tensor(n, [D, TPC], BF16, kind="ExternalOutput").ap()
            for n in ("qT", "kT", "vT")}
    UTd = nc.dram_tensor("UT", [D, DH], F32, kind="ExternalInput").ap()
    DTd = nc.dram_tensor("DT", [DH, D], F32, kind="ExternalInput").ap()
    UTSd = nc.dram_tensor("UTS", [D, DH], BF16, kind="ExternalOutput").ap()
    DTSd = nc.dram_tensor("DTS", [DH, D], BF16, kind="ExternalOutput").ap()

    es = ExitStack()
    sb = lambda name, shape, dt: es.enter_context(nc.sbuf_tensor(name, shape, dt))
    ps_ = lambda name, shape, dt=F32: es.enter_context(nc.psum_tensor(name, shape, dt))

    X = sb("X", [128, NCH * TPC], F32)
    XN = sb("XN", [128, NCH * TPC], BF16)
    XBF2 = [sb(f"XBF2{i}", [128, SPAN], BF16) for i in range(2)]
    SQ2 = [sb(f"SQ2{i}", [128, SPAN], BF16) for i in range(2)]
    ONESC = sb("ONESC", [128, 1], BF16)
    EPS1 = sb("EPS1", [1, 1], F32)
    ONESR = sb("ONESR", [1, 128], F32)
    STATR = sb("STATR", [1, 2 * SPAN], F32)
    AB = sb("AB", [1, 2 * SPAN], F32)
    ABC = sb("ABC", [128, 2 * SPAN], F32)
    TMP = sb("TMP", [128, SPAN], F32)
    WBF = [sb(f"WBF{i}", [128, NCH * 512], BF16) for i in range(2)]
    BIA = sb("BIA", [128, 3 * NCH], F32)
    OUTB = [sb(f"OUTB{i}", [128, SPAN], BF16) for i in range(2)]
    SGB = [sb(f"SGB{i}", [128, 4096], BF16) for i in range(2)]
    SGF = [sb(f"SGF{i}", [128, 4096], F32) for i in range(2)]

    PS_MU = ps_("PS_MU", [1, SPAN])
    PS_SQ = ps_("PS_SQ", [1, SPAN])
    PS_AB = ps_("PS_AB", [128, 2 * SPAN])
    PS_O = [ps_(f"PS_O{i}", [128, SPAN]) for i in range(2)]

    sems = {k: Sem(es.enter_context(nc.semaphore("s_" + k)))
            for k in ("ld", "v", "a", "t", "g", "gx", "st0", "st1", "wl0", "wl1", "ev",
                      "chv", "cha", "sg0", "sg1", "so0", "so1", "sa")}
    s_ld, s_v, s_a, s_t, s_g = sems["ld"], sems["v"], sems["a"], sems["t"], sems["g"]
    s_st = [sems["st0"], sems["st1"]]
    s_wl = [sems["wl0"], sems["wl1"]]
    s_ev, s_gx = sems["ev"], sems["gx"]
    s_sg = [sems["sg0"], sems["sg1"]]
    s_so = [sems["so0"], sems["so1"]]
    s_sa = sems["sa"]

    p = Plan()

    def loads(e):
        e.dma_start(out=X[:].rearrange("p (c t) -> p c t", c=NCH),
                    in_=xT.rearrange("(c p) t -> p c t", c=NCH)).then_inc(s_ld.h, 16)
        for j, n in enumerate(("bqT", "bkT", "bvT")):
            e.dma_start(out=BIA[:, j * NCH:(j + 1) * NCH], in_=bTs[n]).then_inc(s_ld.h, 16)
    p.add("sync", loads)
    s_ld.bump(16 * 4)
    x_loaded = 16
    bias_loaded = s_ld.n

    def consts(e):
        e.memset(ONESC[:], 1.0)
        e.memset(ONESR[:], 1.0)
        e.memset(EPS1[:], LN_EPS).then_inc(s_g.h, 1)
    p.add("gpsimd", consts)
    consts_ready = s_g.bump()

    def bias_prep(e):
        e.wait_ge(s_ld.h, bias_loaded)
        e.tensor_scalar(BIA[:, 0:NCH], BIA[:, 0:NCH], QSCALE, None, AluOpType.mult).then_inc(s_v.h, 1)
    p.add("vector", bias_prep)
    bias_ready = s_v.bump()

    ln_done = []
    for sp in range(2):
        t0 = sp * SPAN
        bufs = dict(XBF2=XBF2, SQ2=SQ2, ONESC=ONESC, ONESR=ONESR, STATR=STATR, AB=AB, ABC=ABC,
                    TMP=TMP, PS_MU=PS_MU, PS_SQ=PS_SQ, PS_AB=PS_AB, EPS1=EPS1,
                    consts_ready=consts_ready, x_ready=0)

        def wait_x(e):
            e.wait_ge(s_ld.h, s_ld.n)
            e.nop().then_inc(s_v.h, 1)
        p.add("vector", wait_x)
        s_v.bump()

        done = emit_layernorm(p, {"v": s_v, "a": s_a, "t": s_t, "g": s_g, "gx": s_gx,
                                  "chv": sems["chv"], "cha": sems["cha"]}, bufs,
                              NCH, TPC, t0, SPAN, X, XN)
        ln_done.append(done)

    evac_cnt = 0
    mm_groups = 0
    wl_sets = 0
    projs = [("Wq", "qT", 0, QSCALE), ("Wk", "kT", 1, 1.0), ("Wv", "vT", 2, 1.0)]
    set_mm_done = {}
    for oi, (wn, on, bidx, scal) in enumerate(projs):
        W, OUT = Ws[wn], OUTs[on]
        for espan in range(4):
            step = oi * 4 + espan
            buf = step % 2
            guard = set_mm_done.get(step - 2, 0)

            def wload(e, W=W, espan=espan, buf=buf, guard=guard):
                if guard:
                    e.wait_ge(s_t.h, guard)
                src = W.rearrange("(c p) e -> p c e", c=NCH)[:, :, espan * 512:(espan + 1) * 512]
                e.dma_start(out=WBF[buf][:].rearrange("p (c e) -> p c e", c=NCH),
                            in_=src).then_inc(s_wl[buf].h, 16)
            p.add("gpsimd", wload)
            set_loaded = s_wl[buf].bump(16)

            chv_h = sems["chv"].h

            def mms(e, buf=buf, set_loaded=set_loaded, step=step, ln_all=tuple(ln_done),
                    chv_h=chv_h):
                e.wait_ge(s_wl[buf].h, set_loaded)
                if step == 0:
                    e.wait_ge(chv_h, max(ln_all))
                for eb in range(4):
                    for tsp in range(2):
                        j = (step * 8 + eb * 2 + tsp)
                        bank = j % 2
                        if j >= 2:
                            e.wait_ge(s_ev.h, j - 1)
                        for i in range(NCH):
                            inst = e.matmul(PS_O[bank][:],
                                            WBF[buf][:, i * 512 + eb * 128:i * 512 + (eb + 1) * 128],
                                            XN[:, i * TPC + tsp * SPAN:i * TPC + (tsp + 1) * SPAN],
                                            start=(i == 0), stop=(i == NCH - 1))
                        inst.then_inc(s_t.h, 1)
            p.add("tensor", mms)
            first_tile = s_t.n + 1
            set_mm_done[step] = s_t.bump(8)

            def evacs(e, bidx=bidx, scal=scal, first_tile=first_tile, step=step):
                for eb in range(4):
                    for tsp in range(2):
                        j = step * 8 + eb * 2 + tsp
                        bank = j % 2
                        e.wait_ge(s_t.h, first_tile + eb * 2 + tsp)
                        if j >= 2:
                            e.wait_ge(s_st[j % 2].h, 16 * (j // 2))
                        col = bidx * NCH + (step % 4) * 4 + eb
                        e.tensor_scalar(OUTB[bank][:], PS_O[bank][:], scal,
                                        BIA[:, col:col + 1], AluOpType.mult, AluOpType.add
                                        ).then_inc(s_ev.h, 1)
            p.add("vector", evacs)
            first_ev = s_ev.n + 1
            s_ev.bump(8)

            def stores(e, OUT=OUT, espan=espan, first_ev=first_ev, step=step):
                for eb in range(4):
                    for tsp in range(2):
                        j = step * 8 + eb * 2 + tsp
                        bank = j % 2
                        e.wait_ge(s_ev.h, first_ev + eb * 2 + tsp)
                        r0 = espan * 512 + eb * 128
                        e.dma_start(out=OUT[r0:r0 + 128, tsp * SPAN:(tsp + 1) * SPAN],
                                    in_=OUTB[bank][:]).then_inc(s_st[bank].h, 16)
            p.add("sync", stores)
            s_st[0].bump(16 * 4)
            s_st[1].bump(16 * 4)

    sign_units = []
    for rc in range(16):
        for half in range(2):
            sign_units.append((UTd[rc * 128:(rc + 1) * 128, half * 4096:(half + 1) * 4096],
                               UTSd[rc * 128:(rc + 1) * 128, half * 4096:(half + 1) * 4096]))
    for u in range(32):
        sign_units.append((DTd[u * 256:(u + 1) * 256, :].rearrange("(c p) dd -> p c dd", c=2),
                           DTSd[u * 256:(u + 1) * 256, :].rearrange("(c p) dd -> p c dd", c=2)))

    for ui, (sap, dap) in enumerate(sign_units):
        par = ui % 2
        qname = "sync" if (ui // 2) % 2 == 0 else "scalar"

        def sg_load(e, sap=sap, par=par, ui=ui):
            if ui >= 2:
                e.wait_ge(s_sa.h, ui - 1)
            dst = SGF[par][:]
            if len(sap.shape) == 3:
                dst = dst.rearrange("p (c dd) -> p c dd", c=2)
            e.dma_start(out=dst, in_=sap).then_inc(s_sg[par].h, 16)
        p.add(qname, sg_load)
        s_sg[par].bump(16)
        ld_val = s_sg[par].n

        def sg_sign(e, par=par, ld_val=ld_val, ui=ui):
            e.wait_ge(s_sg[par].h, ld_val)
            if ui >= 2:
                e.wait_ge(s_so[par].h, 16 * (ui // 2))
            e.activation(SGB[par][:], SGF[par][:], AF.Sign).then_inc(s_sa.h, 1)
        p.add("scalar", sg_sign)
        sa_val = s_sa.bump(1)

        def sg_store(e, dap=dap, par=par, sa_val=sa_val):
            e.wait_ge(s_sa.h, sa_val)
            srcb = SGB[par][:]
            if len(dap.shape) == 3:
                srcb = srcb.rearrange("p (c dd) -> p c dd", c=2)
            e.dma_start(out=dap, in_=srcb).then_inc(s_so[par].h, 16)
        p.add("sync" if par == 0 else "scalar", sg_store)
        s_so[par].bump(16)

    def fin(e):
        e.wait_ge(s_st[0].h, s_st[0].n)
        e.wait_ge(s_st[1].h, s_st[1].n)
        e.wait_ge(s_so[0].h, s_so[0].n)
        e.wait_ge(s_so[1].h, s_so[1].n)
    p.add("sync", fin)

    p.emit(es.enter_context(nc.Block()))
    es.close()
    return nc



def stage_l1(inputs):
    x = np.asarray(inputs["L0_attn_a_x"])[0]
    Wq = np.asarray(inputs["L0_attn_a_Wq"])
    Wk = np.asarray(inputs["L0_attn_a_Wk"])
    Wv = np.asarray(inputs["L0_attn_a_Wv"])
    bq = np.asarray(inputs["L0_attn_a_bq"])
    bk = np.asarray(inputs["L0_attn_a_bk"])
    bv = np.asarray(inputs["L0_attn_a_bv"])
    UT_T = np.ascontiguousarray(np.asarray(inputs["L0_ffn_up_weight"]).T)
    DT_T = np.ascontiguousarray(np.asarray(inputs["L0_ffn_down_weight"]).T)

    def bT(b):
        return np.ascontiguousarray(b.reshape(NCH, 128).T)

    in_maps = []
    for c in range(NCORES):
        a0, b0 = span_starts(c)
        toks = np.r_[a0:a0 + SPAN, b0:b0 + SPAN]
        xTc = np.ascontiguousarray(x[toks, :].T)
        in_maps.append({
            "xT": xTc, "Wq": Wq, "Wk": Wk, "Wv": Wv,
            "bqT": bT(bq), "bkT": bT(bk), "bvT": bT(bv),
            "UT": UT_T, "DT": DT_T,
        })
    return in_maps


def kernel(**inputs):
    raise NotImplementedError("under construction: L2/L3 pending")



SUBW = [32, 64, 64, 128, 128, 256, 256, 512] + [512] * 8
WBLK = [2, 3, 4, 5, 7, 10, 14, 19, 27, 38, 54, 64, 64, 64, 64, 64]
KWIN = 68
HGROUPS = [(0, 1, 2), (3, 4, 5), (6, 7, 8), (9, 10, 11), (12, 13, 14), (15,)]
KBATCH = 4


def attention_sites(span):
    groups = []
    for gh in HGROUPS:
        persite = []
        kbs_needed = set()
        for h in gh:
            W = SUBW[h]
            nsub = 512 // W
            w = WBLK[h]
            for s in range(nsub):
                q0 = W * s
                jlast = (W * (s + 1) - 1) // 128
                kb_hi = 64 + jlast
                kb_lo = kb_hi - (w - 1)
                kb_lo = max(kb_lo, 0)
                for kb in range(kb_lo, kb_hi + 1):
                    kbs_needed.add(kb)
                persite.append((h, s, q0, W, kb_lo, kb_hi))
        kbs_sorted = sorted(kbs_needed)
        batches = []
        for b0 in range(0, len(kbs_sorted), KBATCH):
            kb_list = kbs_sorted[b0:b0 + KBATCH]
            site_list = []
            for (h, s, q0, W, kb_lo, kb_hi) in persite:
                for kb in kb_list:
                    if kb_lo <= kb <= kb_hi:
                        r = q0 - 128 * (kb - 64)
                        jq_lo = q0 // 128
                        jq_hi = (q0 + W - 1) // 128
                        overlaps = (64 + jq_lo) <= kb <= (64 + jq_hi)
                        mask_key = (W, r) if overlaps else None
                        site_list.append(dict(h=h, sub=s, q0=q0, W=W, kb=kb,
                                              first=(kb == kb_lo), last=(kb == kb_hi),
                                              mask_key=mask_key))
            batches.append((kb_list, site_list))
        groups.append((gh, batches))
    return groups


def all_mask_keys():
    keys = []
    for span in range(1):
        for gh, batches in attention_sites(0):
            for kb_list, sites in batches:
                for st in sites:
                    if st["mask_key"] and st["mask_key"] not in keys:
                        keys.append(st["mask_key"])
    return sorted(keys)


def build_masks():
    keys = all_mask_keys()
    offs = {}
    cols = []
    pos = 0
    for (W, r) in keys:
        kk = np.arange(128)[:, None]
        qq = np.arange(W)[None, :]
        m = (kk <= qq + r).astype(BF)
        offs[(W, r)] = (pos, W)
        cols.append(m)
        pos += W
    return np.concatenate(cols, axis=1), offs


def site_count():
    n = 0
    for gh, batches in attention_sites(0):
        for kb_list, sites in batches:
            n += len(sites)
    return n


def build_bias_table(core):
    nsites = site_count()
    tab = np.zeros((128, 2 * nsites), np.float32)
    for sp in range(2):
        a0, b0 = span_starts(core)
        q_glob0 = a0 if sp == 0 else b0
        blk0 = q_glob0 // 128
        idx = 0
        for gh, batches in attention_sites(sp):
            for kb_list, sites in batches:
                for st in sites:
                    h, kb, W, q0 = st["h"], st["kb"], st["W"], st["q0"]
                    gkb = blk0 + (kb - 64)
                    col = sp * nsites + idx
                    if gkb < 0:
                        tab[:, col] = -1e30
                    else:
                        slope = SLOPES[h]
                        kglob = 128 * gkb + np.arange(128)
                        qg0 = q_glob0 + q0
                        c = slope * (W / 2.0)
                        v = slope * (kglob - qg0) - c
                        v[kglob > qg0 + W - 1] = -1e30
                        tab[:, col] = v
                    idx += 1
    return tab



class Bank:
    def __init__(self, ap):
        self.ap = ap
        self.w = None
        self.r = None


def wms(e, ms):
    if ms is not None:
        e.wait_ge(ms[0].h, ms[1])


_M, _OFFS = None, None


def masks_and_offs():
    global _M, _OFFS
    if _M is None:
        _M, _OFFS = build_masks()
    return _M, _OFFS


def emit_ln2(ctx, sp, x_ready_ms):
    (p, banks, sems, X, XN, AR, WS, ONESC, ONESR, EPS1, STATR, ABrow, ABC, TMP,
     XBF2, SQ2, UPS, DSOS, UTSd, DTSd, mm_done, ws_last) = ctx['core']
    s_v, s_a, s_ts, ch_v, ch_a = sems['v'], sems['a'], sems['ts'], sems['chv'], sems['cha']
    s_wl = [sems['wl0'], sems['wl1']]
    s_wls = [sems['wls0'], sems['wls1']]
    t0 = sp * SPAN
    b_mu, b_sq, b_a, b_b = banks[4], banks[5], banks[6], banks[7]
    c0 = ch_v.n
    mu_rd, sq_rd = b_mu.r, b_sq.r
    tchk0 = s_ts.n

    def cast_chunks(e, t0=t0, x_ready_ms=x_ready_ms, c0=c0, tchk0=tchk0):
        wms(e, x_ready_ms)
        for i in range(NCH):
            if i >= 2:
                e.wait_ge(s_ts.h, tchk0 + i - 1)
            e.tensor_copy(XBF2[i % 2][:], X[:, i * TPC + t0:i * TPC + t0 + SPAN]
                          ).then_inc(ch_v.h, 1)
            e.wait_ge(ch_v.h, c0 + 2 * i + 1)
            e.tensor_tensor(SQ2[i % 2][:], XBF2[i % 2][:], XBF2[i % 2][:],
                            AluOpType.mult).then_inc(ch_v.h, 1)
    p.add("vector", cast_chunks)
    ch_v.bump(2 * NCH)

    def stats_mm(e, c0=c0, b_mu_ap=b_mu.ap, b_sq_ap=b_sq.ap, rd=(mu_rd, sq_rd)):
        wms(e, rd[0])
        wms(e, rd[1])
        for i in range(NCH):
            e.wait_ge(ch_v.h, c0 + 2 * i + 2)
            e.matmul(b_mu_ap[0:1, :], ONESC[:], XBF2[i % 2][:],
                     start=(i == 0), stop=(i == NCH - 1))
            inst = e.matmul(b_sq_ap[0:1, :], ONESC[:], SQ2[i % 2][:],
                            start=(i == 0), stop=(i == NCH - 1))
            inst.then_inc(s_ts.h, 1)
    p.add("tensor", stats_mm)
    stats_done = s_ts.bump(NCH)
    b_mu.w = b_sq.w = (s_ts, stats_done)
    c1, c2, c3 = ch_v.bump(1), ch_v.bump(1), ch_v.bump(1)

    def stats_fin(e, stats_done=stats_done, b_mu_ap=b_mu.ap, b_sq_ap=b_sq.ap,
                  c1=c1, c2=c2):
        e.wait_ge(s_ts.h, stats_done)
        e.tensor_scalar(STATR[0:1, 0:SPAN], b_mu_ap[0:1, :], 1.0 / D, None, AluOpType.mult)
        e.tensor_scalar(STATR[0:1, SPAN:2 * SPAN], b_sq_ap[0:1, :], 1.0 / D, None,
                        AluOpType.mult).then_inc(ch_v.h, 1)
        e.wait_ge(ch_v.h, c1)
        e.tensor_tensor(ABrow[0:1, SPAN:2 * SPAN], STATR[0:1, 0:SPAN], STATR[0:1, 0:SPAN],
                        AluOpType.mult).then_inc(ch_v.h, 1)
        e.wait_ge(ch_v.h, c2)
        e.tensor_tensor(ABrow[0:1, SPAN:2 * SPAN], STATR[0:1, SPAN:2 * SPAN],
                        ABrow[0:1, SPAN:2 * SPAN], AluOpType.subtract).then_inc(ch_v.h, 1)
    p.add("vector", stats_fin)
    var_done = (ch_v, ch_v.n)
    b_mu.r = b_sq.r = var_done
    ca = ch_a.bump(1)

    def rstd_f(e, var_done=var_done, ca=ca):
        wms(e, var_done)
        e.activation(ABrow[0:1, 0:SPAN], ABrow[0:1, SPAN:2 * SPAN], AF.Ln,
                     bias=EPS1[0:1, 0:1], scale=1.0).then_inc(ch_a.h, 1)
        e.wait_ge(ch_a.h, ca)
        e.activation(ABrow[0:1, 0:SPAN], ABrow[0:1, 0:SPAN], AF.Exp,
                     scale=-0.5).then_inc(s_a.h, 1)
    p.add("scalar", rstd_f)
    rstd_done = s_a.bump(1)
    c4 = ch_v.bump(1)

    def brow_f(e, rstd_done=rstd_done, c4=c4):
        e.wait_ge(s_a.h, rstd_done)
        e.tensor_tensor(ABrow[0:1, SPAN:2 * SPAN], STATR[0:1, 0:SPAN], ABrow[0:1, 0:SPAN],
                        AluOpType.mult).then_inc(ch_v.h, 1)
        e.wait_ge(ch_v.h, c4)
        e.tensor_scalar(ABrow[0:1, SPAN:2 * SPAN], ABrow[0:1, SPAN:2 * SPAN], -1.0, None,
                        AluOpType.mult).then_inc(s_v.h, 1)
    p.add("vector", brow_f)
    brow_done = s_v.bump(1)

    def bcast_mm(e, brow_done=brow_done, b_a_ap=b_a.ap, b_b_ap=b_b.ap,
                 rd=(b_a.r, b_b.r)):
        e.wait_ge(s_v.h, brow_done)
        wms(e, rd[0])
        wms(e, rd[1])
        e.matmul(b_a_ap[:], ONESR[:], ABrow[0:1, 0:SPAN], start=True, stop=True)
        e.matmul(b_b_ap[:], ONESR[:], ABrow[0:1, SPAN:2 * SPAN], start=True,
                 stop=True).then_inc(s_ts.h, 1)
    p.add("tensor", bcast_mm)
    bcast_done = s_ts.bump(1)
    b_a.w = b_b.w = (s_ts, bcast_done)
    c5 = ch_v.bump(1)

    def evac_bcast(e, bcast_done=bcast_done, b_a_ap=b_a.ap, b_b_ap=b_b.ap, c5=c5):
        e.wait_ge(s_ts.h, bcast_done)
        e.tensor_copy(ABC[:, 0:SPAN], b_a_ap[:]).then_inc(ch_v.h, 1)
        e.wait_ge(ch_v.h, c5)
        e.tensor_copy(ABC[:, SPAN:2 * SPAN], b_b_ap[:]).then_inc(s_v.h, 1)
    p.add("vector", evac_bcast)
    abc_done = s_v.bump(1)
    b_a.r = b_b.r = (s_v, abc_done)
    ap_c0 = ch_v.n

    def apply_f(e, abc_done=abc_done, t0=t0, ap_c0=ap_c0):
        e.wait_ge(s_v.h, abc_done)
        for i in range(NCH):
            if i:
                e.wait_ge(ch_v.h, ap_c0 + 2 * i)
            e.tensor_tensor(TMP[:, 0:SPAN], X[:, i * TPC + t0:i * TPC + t0 + SPAN],
                            ABC[:, 0:SPAN], AluOpType.mult).then_inc(ch_v.h, 1)
            e.wait_ge(ch_v.h, ap_c0 + 2 * i + 1)
            e.tensor_tensor(XN[:, i * SPAN:(i + 1) * SPAN], TMP[:, 0:SPAN],
                            ABC[:, SPAN:2 * SPAN], AluOpType.add).then_inc(ch_v.h, 1)
    p.add("vector", apply_f)
    ch_v.bump(2 * NCH)
    return (ch_v, ch_v.n)



def emit_ffn_span(ctx, sp, xn_ms, name):
    (p, banks, sems, X, XN, AR, WS, ONESC, ONESR, EPS1, STATR, ABrow, ABC, TMP,
     XBF2, SQ2, UPS, DSOS, UTSd, DTSd, mm_done, ws_last) = ctx['core']
    s_v, s_a, s_ts, ch_v, ch_a = sems['v'], sems['a'], sems['ts'], sems['chv'], sems['cha']
    s_wls = [sems['wls0'], sems['wls1']]
    WSF = ctx['WSF']
    wsf_last = ctx.setdefault('wsf_last', {0: None, 1: None})
    for hs in range(16):
        buf = hs % 2
        guard = wsf_last[buf]

        def wload(e, hs=hs, buf=buf, guard=guard):
            wms(e, guard)
            src = UTSd.rearrange("(c p) hh -> p c hh", c=NCH)[:, :, hs * 512:(hs + 1) * 512]
            e.dma_start(out=WSF[buf][:].rearrange("p (c hh) -> p c hh", c=NCH),
                        in_=src).then_inc(s_wls[buf].h, 16)
        p.add("sync", wload)
        set_loaded = s_wls[buf].bump(16)

        for hb4 in range(4):
            hb = hs * 4 + hb4
            bank = banks[hb % 2]
            my_mm = s_ts.n + 1

            def mms(e, buf=buf, hb4=hb4, hb=hb, bank_ap=bank.ap, set_loaded=set_loaded,
                    xn_ms=xn_ms, bank_r=bank.r):
                e.wait_ge(s_wls[buf].h, set_loaded)
                if hb == 0:
                    wms(e, xn_ms)
                wms(e, bank_r)
                for i in range(NCH):
                    inst = e.matmul(bank_ap[:],
                                    WSF[buf][:, i * 512 + hb4 * 128:i * 512 + (hb4 + 1) * 128],
                                    XN[:, i * SPAN:(i + 1) * SPAN],
                                    start=(i == 0), stop=(i == NCH - 1))
                inst.then_inc(s_ts.h, 1)
            p.add("tensor", mms)
            s_ts.bump(1)
            bank.w = (s_ts, my_mm)
            wsf_last[buf] = (s_ts, my_mm)

            def evac_up(e, bank_ap=bank.ap, hb=hb, my_mm=my_mm):
                e.wait_ge(s_ts.h, my_mm)
                e.activation(AR[:, hb * SPAN:(hb + 1) * SPAN], bank_ap[:], AF.Relu,
                             scale=UPS[:, hb:hb + 1]).then_inc(s_a.h, 1)
            p.add("scalar", evac_up)
            bank.r = (s_a, s_a.bump(1))
    up_all = (s_a, s_a.n)

    last = None
    for ddb in range(NCH):
        buf = ddb % 2
        guard = wsf_last[buf]

        def wload(e, ddb=ddb, buf=buf, guard=guard):
            wms(e, guard)
            src = DTSd.rearrange("(c p) dd -> p c dd", c=64)[:, :, ddb * 128:(ddb + 1) * 128]
            e.dma_start(out=WSF[buf][:].rearrange("p (c dd) -> p c dd", c=64),
                        in_=src).then_inc(s_wls[buf].h, 16)
        p.add("sync", wload)
        set_loaded = s_wls[buf].bump(16)
        bank = banks[2 + ddb % 2]
        my_mm = s_ts.n + 1

        def mms(e, buf=buf, bank_ap=bank.ap, set_loaded=set_loaded,
                up_all=up_all, ddb=ddb, bank_r=bank.r):
            e.wait_ge(s_wls[buf].h, set_loaded)
            if ddb == 0:
                wms(e, up_all)
            wms(e, bank_r)
            for i in range(64):
                inst = e.matmul(bank_ap[:],
                                WSF[buf][:, i * 128:(i + 1) * 128],
                                AR[:, i * SPAN:(i + 1) * SPAN],
                                start=(i == 0), stop=(i == 63))
            inst.then_inc(s_ts.h, 1)
        p.add("tensor", mms)
        s_ts.bump(1)
        bank.w = (s_ts, my_mm)
        wsf_last[buf] = (s_ts, my_mm)

        def evac_dn(e, bank_ap=bank.ap, ddb=ddb, sp=sp, my_mm=my_mm):
            e.wait_ge(s_ts.h, my_mm)
            e.scalar_tensor_tensor(
                X[:, ddb * TPC + sp * SPAN:ddb * TPC + (sp + 1) * SPAN],
                bank_ap[:], DSOS[:, ddb:ddb + 1],
                X[:, ddb * TPC + sp * SPAN:ddb * TPC + (sp + 1) * SPAN],
                AluOpType.mult, AluOpType.add).then_inc(s_v.h, 1)
        p.add("vector", evac_dn)
        last = (s_v, s_v.bump(1))
        bank.r = last
    return last


def emit_proj_to_x(ctx, src_off, Wd, bias_ap, ready_ms, name, residual=True):
    (p, banks, sems, X, XN, AR, WS, ONESC, ONESR, EPS1, STATR, ABrow, ABC, TMP,
     XBF2, SQ2, UPS, DSOS, UTSd, DTSd, mm_done, ws_last) = ctx['core']
    s_v, s_a, s_ts, ch_v, ch_a = sems['v'], sems['a'], sems['ts'], sems['chv'], sems['cha']
    s_wl = [sems['wl0'], sems['wl1']]
    s_wls = [sems['wls0'], sems['wls1']]
    WSF = ctx['WSF']
    ws_last = ctx.setdefault('wsf_last', {0: None, 1: None})
    last = None
    for ec in range(NCH):
        buf = ec % 2
        guard = ws_last[buf]

        def wload(e, Wd=Wd, ec=ec, buf=buf, guard=guard):
            wms(e, guard)
            src = Wd.rearrange("(c p) ee -> p c ee", c=NCH)[:, :, ec * 128:(ec + 1) * 128]
            e.dma_start(out=WSF[buf][:, 0:NCH * 128].rearrange("p (c ee) -> p c ee", c=NCH),
                        in_=src).then_inc(s_wl[buf].h, 16)
        p.add("gpsimd", wload)
        set_loaded = s_wl[buf].bump(16)

        for tsp in range(2):
            j = ec * 2 + tsp
            bank = banks[j % 2]
            my_mm = s_ts.n + 1

            def mms(e, buf=buf, tsp=tsp, bank_ap=bank.ap, src_off=src_off,
                    set_loaded=set_loaded, j=j, ready_ms=ready_ms, bank_r=bank.r):
                e.wait_ge(s_wl[buf].h, set_loaded)
                if j == 0:
                    wms(e, ready_ms)
                wms(e, bank_r)
                for i in range(NCH):
                    inst = e.matmul(bank_ap[:],
                                    WSF[buf][:, i * 128:(i + 1) * 128],
                                    AR[:, src_off + i * TPC + tsp * SPAN:
                                       src_off + i * TPC + (tsp + 1) * SPAN],
                                    start=(i == 0), stop=(i == NCH - 1))
                inst.then_inc(s_ts.h, 1)
            p.add("tensor", mms)
            s_ts.bump(1)
            bank.w = (s_ts, my_mm)
            mm_done[(name, ec)] = my_mm
            ws_last[buf] = (s_ts, my_mm)

            def evac(e, bank_ap=bank.ap, ec=ec, tsp=tsp, my_mm=my_mm, bias_ap=bias_ap):
                e.wait_ge(s_ts.h, my_mm)
                e.scalar_tensor_tensor(
                    X[:, ec * TPC + tsp * SPAN:ec * TPC + (tsp + 1) * SPAN],
                    bank_ap[:], bias_ap[:, ec:ec + 1],
                    X[:, ec * TPC + tsp * SPAN:ec * TPC + (tsp + 1) * SPAN],
                    AluOpType.add, AluOpType.add).then_inc(s_v.h, 1)
            p.add("vector", evac)
            last = (s_v, s_v.bump(1))
            bank.r = last
    return last



def build_l2(upto="full"):
    MASKARR, MASK_OFFS = masks_and_offs()
    MASK_COLS = MASKARR.shape[1]
    nsites = site_count()

    nc = bass.Bass("TRN2", target_bir_lowering=False)
    dram = lambda n, shape, dt, kind="ExternalInput": nc.dram_tensor(n, shape, dt, kind=kind).ap()
    XTd = dram("XT", [D, TPC], F32)
    QTd = dram("QT", [D, TPC], BF16)
    KWd = [dram("KWA", [D, KWIN * 128], BF16), dram("KWB", [D, KWIN * 128], BF16)]
    VWd = [dram("VWA", [KWIN * 128, D], BF16), dram("VWB", [KWIN * 128, D], BF16)]
    BTABd = dram("BTAB", [128, 2 * nsites], F32)
    MASKSd = dram("MASKS", [128, MASK_COLS], BF16)
    Wod = dram("Wo", [D, D], F32)
    boTd = dram("boT", [128, NCH], F32)
    UTSd = dram("UTS", [D, DH], BF16)
    DTSd = dram("DTS", [DH, D], BF16)
    upsTd = dram("upsT", [128, DH // 128], F32)
    dsTd = dram("dsT", [128, NCH], F32)
    oscd = dram("osc", [128, 1], F32)
    bWTd = [dram(n, [D, D], F32) for n in ("bWkT", "bWvT", "bWrT")]
    tdTd = dram("tdT", [128, NH], F32)
    tfTd = dram("tfT", [128, NH], F32)
    find_d = dram("FIRSTIND", [128, 1], F32)

    X2Td = dram("X2T", [D, TPC], F32, "ExternalOutput")
    NUMLd = dram("NUML", [D, TPC], F32, "ExternalOutput")
    DENLd = dram("DENL", [D, TPC], F32, "ExternalOutput")
    SIGd = dram("SIG", [D, TPC], F32, "ExternalOutput")
    SUMSd = dram("SUMS", [128, NCH * 4], F32, "ExternalOutput")

    es = ExitStack()
    sb = lambda name, shape, dt: es.enter_context(nc.sbuf_tensor(name, shape, dt))

    X = sb("X", [128, NCH * TPC], F32)
    XN = sb("XN", [128, NCH * SPAN], BF16)
    AR = sb("AR", [128, 2 * NCH * TPC], BF16)
    WS = None
    BIA2 = sb("BIA2", [128, NCH], F32)
    ONESC = sb("ONESC", [128, 1], BF16)
    ONESR = sb("ONESR", [1, 128], F32)
    EPS1 = sb("EPS1", [1, 1], F32)
    STATR = sb("STATR", [1, 2 * SPAN], F32)
    ABrow = sb("ABrow", [1, 2 * SPAN], F32)
    ABC = sb("ABC", [128, 2 * SPAN], F32)
    TMP = sb("TMP", [128, SPAN], F32)
    XBF2 = [sb(f"XBF2{i}", [128, SPAN], BF16) for i in range(2)]
    SQ2 = [sb(f"SQ2{i}", [128, SPAN], BF16) for i in range(2)]
    UPS = sb("UPS", [128, DH // 128], F32)
    DST = sb("DST", [128, NCH], F32)
    OSC = sb("OSC", [128, 1], F32)
    DSOS = sb("DSOS", [128, NCH], F32)
    TD = sb("TD", [128, NH], F32)
    TF = sb("TF", [128, NH], F32)
    IND = sb("IND", [128, 1], F32)
    E1 = sb("E1", [128, NH], F32)
    DEC = sb("DEC", [128, NH], F32)
    FEFF = sb("FEFF", [128, NH], F32)
    SUMS = sb("SUMSs", [128, NCH * 4], F32)

    es_attn = ExitStack()
    sba = lambda name, shape, dt: es_attn.enter_context(nc.sbuf_tensor(name, shape, dt))
    KB = [sba(f"KB{i}", [128, 3 * KBATCH * 128], BF16) for i in range(2)]
    VB = [sba(f"VB{i}", [128, 3 * KBATCH * 128], BF16) for i in range(2)]
    PT = [sba(f"PT{i}", [128, SPAN], BF16) for i in range(2)]
    BTAB = sba("BTABs", [128, nsites], F32)
    MASKS = sba("MASKSs", [128, MASK_COLS], BF16)
    RL = sba("RL", [1, SPAN], F32)
    RLB = sba("RLB", [128, SPAN], F32)

    PS = [es.enter_context(nc.psum_tensor(f"PS{i}", [128, SPAN], F32)) for i in range(8)]
    banks = [Bank(t) for t in PS]

    sems = {k: Sem(es.enter_context(nc.semaphore("s_" + k)))
            for k in ("ld", "ld2", "v", "a", "ts", "tp", "g", "kv0", "kv1", "wl0", "wl1",
                      "wls0", "wls1", "st0", "st1", "stx", "chv", "cha",
                      "w30", "w31", "w32", "w33", "w34", "w35")}
    s_ld, s_v, s_a, s_g = sems["ld"], sems["v"], sems["a"], sems["g"]
    s_ld2 = sems["ld2"]
    s_ts, s_tp = sems["ts"], sems["tp"]
    s_kv = [sems["kv0"], sems["kv1"]]
    ch_v, ch_a = sems["chv"], sems["cha"]
    s_wl = [sems["wl0"], sems["wl1"]]
    s_wls = [sems["wls0"], sems["wls1"]]
    s_st = [sems["st0"], sems["st1"]]
    s_stx = sems["stx"]
    s_w3 = [sems["w3" + str(i)] for i in range(6)]

    p = Plan()
    QT_OFF = 0
    OT_OFF = NCH * TPC

    def loads(e):
        e.dma_start(out=X[:].rearrange("p (c t) -> p c t", c=NCH),
                    in_=XTd.rearrange("(c p) t -> p c t", c=NCH)).then_inc(s_ld.h, 16)
        e.dma_start(out=AR[:, QT_OFF:QT_OFF + NCH * TPC].rearrange("p (c t) -> p c t", c=NCH),
                    in_=QTd.rearrange("(c p) t -> p c t", c=NCH)).then_inc(s_ld.h, 16)
        e.dma_start(out=BTAB[:], in_=BTABd[:, 0:nsites]).then_inc(s_ld.h, 16)
        e.dma_start(out=MASKS[:], in_=MASKSd).then_inc(s_ld.h, 16)
        e.dma_start(out=BIA2[:], in_=boTd).then_inc(s_ld.h, 16)
        e.dma_start(out=UPS[:], in_=upsTd).then_inc(s_ld.h, 16)
        e.dma_start(out=DST[:], in_=dsTd).then_inc(s_ld.h, 16)
        e.dma_start(out=OSC[:], in_=oscd).then_inc(s_ld.h, 16)
        e.dma_start(out=TD[:], in_=tdTd).then_inc(s_ld.h, 16)
        e.dma_start(out=TF[:], in_=tfTd).then_inc(s_ld.h, 16)
        e.dma_start(out=IND[:], in_=find_d).then_inc(s_ld.h, 16)
    p.add("sync", loads)
    loads_done = s_ld.bump(16 * 11)

    def consts(e):
        e.memset(ONESC[:], 1.0)
        e.memset(ONESR[:], 1.0)
        e.memset(EPS1[:], LN_EPS).then_inc(s_g.h, 1)
    p.add("gpsimd", consts)
    consts_done = s_g.bump()

    for eng in ("tensor", "scalar", "vector"):
        def gate(e, _ld=loads_done, _g=consts_done):
            e.wait_ge(s_ld.h, _ld)
            e.wait_ge(s_g.h, _g)
        p.add(eng, gate)

    site_idx = 0
    btab_reload_ms = None
    kv_parity = 0
    kv_guard = {0: 0, 1: 0}
    for sp in range(2):
        if sp == 1:
            exp_a_done = s_a.n

            def btab_reload(e, exp_a_done=exp_a_done):
                e.wait_ge(s_a.h, exp_a_done)
                e.dma_start(out=BTAB[:], in_=BTABd[:, nsites:2 * nsites]).then_inc(s_ld2.h, 16)
            p.add("sync", btab_reload)
            btab_reload_ms = (s_ld2, s_ld2.bump(16))
        for gi, (gh, batches) in enumerate(attention_sites(sp)):
            oh_banks = {h: banks[j] for j, h in enumerate(gh)}
            l_banks = {h: banks[3 + j] for j, h in enumerate(gh)}
            sping = [banks[6], banks[7]]
            first_pv = {h: True for h in gh}
            first_l = {h: True for h in gh}
            for bi, (kb_list, sites) in enumerate(batches):
                kb_lo = kb_list[0]
                nkb = kb_list[-1] - kb_lo + 1
                kbuf = kv_parity % 2
                kv_parity += 1
                guard_tp = kv_guard[kbuf]

                def kvload(e, sp=sp, gh=gh, kb_lo=kb_lo, nkb=nkb, guard_tp=guard_tp, kbuf=kbuf):
                    if guard_tp:
                        e.wait_ge(s_tp.h, guard_tp)
                    h0, nh = gh[0], len(gh)
                    src_k = KWd[sp][128 * h0:128 * (h0 + nh), 128 * kb_lo:128 * (kb_lo + nkb)]
                    e.dma_start(out=KB[kbuf][:, 0:nh * nkb * 128].rearrange("p (h k) -> p h k", h=nh),
                                in_=src_k.rearrange("(h p) k -> p h k", h=nh)).then_inc(s_kv[kbuf].h, 16)
                    src_v = VWd[sp][128 * kb_lo:128 * (kb_lo + nkb), 128 * h0:128 * (h0 + nh)]
                    e.dma_start(out=VB[kbuf][:, 0:nkb * nh * 128].rearrange("p (k hd) -> p k hd", k=nkb),
                                in_=src_v.rearrange("(k p) hd -> p k hd", k=nkb)).then_inc(s_kv[kbuf].h, 16)
                p.add("sync", kvload)
                kv_done = s_kv[kbuf].bump(32)

                for st in sites:
                    h, q0, W, kb = st["h"], st["q0"], st["W"], st["kb"]
                    hi = gh.index(h)
                    kbi = kb - kb_lo
                    pp = site_idx % 2
                    ob, lb = oh_banks[h], l_banks[h]
                    sb_ = sping[pp]
                    col = site_idx % nsites
                    my_score = s_ts.n + 1
                    my_exp = s_a.n + 1
                    my_end = s_tp.n + 1
                    is_first_pv = first_pv[h]; first_pv[h] = False
                    is_first_l = first_l[h]; first_l[h] = False
                    mask_key = st["mask_key"]

                    def score(e, hi=hi, kbi=kbi, q0=q0, W=W, sp=sp, h=h, nkb=nkb,
                              kv_done=kv_done, sb_ap=sb_.ap, bank_r=sb_.r, kbuf=kbuf):
                        e.wait_ge(s_kv[kbuf].h, kv_done)
                        wms(e, bank_r)
                        e.matmul(sb_ap[:, 0:W],
                                 KB[kbuf][:, (hi * nkb + kbi) * 128:(hi * nkb + kbi + 1) * 128],
                                 AR[:, QT_OFF + h * TPC + sp * SPAN + q0:
                                    QT_OFF + h * TPC + sp * SPAN + q0 + W],
                                 start=True, stop=True).then_inc(s_ts.h, 1)
                    p.add("tensor", score)
                    s_ts.bump(1)
                    sb_.w = (s_ts, my_score)

                    pt_guard = (s_tp.n - 1) if site_idx >= 2 else 0

                    def expf(e, pp=pp, W=W, col=col, my_score=my_score, sb_ap=sb_.ap,
                             pt_guard=pt_guard, btab_ms=btab_reload_ms if sp == 1 else None,
                             first_of_span=(site_idx % nsites == 0)):
                        if btab_ms is not None and first_of_span:
                            wms(e, btab_ms)
                        e.wait_ge(s_ts.h, my_score)
                        if pt_guard > 0:
                            e.wait_ge(s_tp.h, pt_guard)
                        e.activation(PT[pp][:, 0:W], sb_ap[:, 0:W], AF.Exp,
                                     bias=BTAB[:, col:col + 1], scale=1.0).then_inc(s_a.h, 1)
                    p.add("scalar", expf)
                    s_a.bump(1)
                    sb_.r = (s_a, my_exp)

                    mask_ms = None
                    if mask_key is not None:
                        moff, mw = MASK_OFFS[mask_key]

                        def maskf(e, pp=pp, W=W, moff=moff, my_exp=my_exp):
                            e.wait_ge(s_a.h, my_exp)
                            e.tensor_tensor(PT[pp][:, 0:W], PT[pp][:, 0:W],
                                            MASKS[:, moff:moff + W],
                                            AluOpType.mult).then_inc(s_v.h, 1)
                        p.add("vector", maskf)
                        mask_ms = (s_v, s_v.bump(1))

                    def pvl(e, pp=pp, hi=hi, kbi=kbi, q0=q0, W=W, nh=len(gh),
                            ob_ap=ob.ap, lb_ap=lb.ap, my_exp=my_exp, mask_ms=mask_ms,
                            is_first_pv=is_first_pv, is_first_l=is_first_l,
                            ob_r=ob.r, lb_r=lb.r, kbuf=kbuf):
                        if mask_ms is not None:
                            wms(e, mask_ms)
                        else:
                            e.wait_ge(s_a.h, my_exp)
                        if is_first_pv:
                            wms(e, ob_r)
                        if is_first_l:
                            wms(e, lb_r)
                        e.matmul(ob_ap[:, q0:q0 + W],
                                 VB[kbuf][:, (kbi * nh + hi) * 128:(kbi * nh + hi + 1) * 128],
                                 PT[pp][:, 0:W], start=is_first_pv, stop=False,
                                 skip_group_check=True)
                        e.matmul(lb_ap[0:1, q0:q0 + W], ONESC[:], PT[pp][:, 0:W],
                                 start=is_first_l, stop=False,
                                 skip_group_check=True).then_inc(s_tp.h, 1)
                    p.add("tensor", pvl)
                    s_tp.bump(1)
                    ob.w = (s_tp, my_end)
                    lb.w = (s_tp, my_end)
                    site_idx += 1
                kv_guard[kbuf] = s_tp.n

            prev_bc = None
            for j, h in enumerate(gh):
                ob, lb = oh_banks[h], l_banks[h]
                bb = sping[0]
                c1 = ch_a.bump(1)

                def fin_rl(e, lb_ap=lb.ap, lb_w=lb.w, ob_w=ob.w, c1=c1, prev_bc=prev_bc):
                    wms(e, lb_w)
                    wms(e, ob_w)
                    wms(e, prev_bc)
                    e.activation(RL[0:1, :], lb_ap[0:1, :], AF.Ln).then_inc(ch_a.h, 1)
                    e.wait_ge(ch_a.h, c1)
                    e.activation(RL[0:1, :], RL[0:1, :], AF.Exp, scale=-1.0).then_inc(s_a.h, 1)
                p.add("scalar", fin_rl)
                rl_done = s_a.bump(1)
                lb.r = (s_a, rl_done)

                def fin_bc(e, bb_ap=bb.ap, rl_done=rl_done, bank_r=bb.r):
                    e.wait_ge(s_a.h, rl_done)
                    wms(e, bank_r)
                    e.matmul(bb_ap[:], ONESR[:], RL[0:1, :], start=True,
                             stop=True).then_inc(s_ts.h, 1)
                p.add("tensor", fin_bc)
                bc_done = s_ts.bump(1)
                bb.w = (s_ts, bc_done)
                prev_bc = (s_ts, bc_done)
                c2 = ch_v.bump(1)
                prev_ot = (s_v, s_v.n)

                def fin_ev(e, bb_ap=bb.ap, ob_ap=ob.ap, h=h, sp=sp, bc_done=bc_done,
                           c2=c2, prev_ot=prev_ot):
                    e.wait_ge(s_ts.h, bc_done)
                    wms(e, prev_ot)
                    e.tensor_copy(RLB[:], bb_ap[:]).then_inc(ch_v.h, 1)
                    e.wait_ge(ch_v.h, c2)
                    e.tensor_tensor(AR[:, OT_OFF + h * TPC + sp * SPAN:
                                       OT_OFF + h * TPC + (sp + 1) * SPAN],
                                    ob_ap[:], RLB[:], AluOpType.mult).then_inc(s_v.h, 1)
                p.add("vector", fin_ev)
                ot_done = s_v.bump(1)
                ob.r = (s_v, ot_done)
                bb.r = (s_v, ot_done)
    attn_done_ms = (s_v, s_v.n)
    es_attn.close()
    es_ffn = ExitStack()
    WSF = [es_ffn.enter_context(nc.sbuf_tensor(f"WSF{i}", [128, 8192], BF16)) for i in range(2)]

    mm_done = {}
    ws_last = {0: None, 1: None}
    ctx = {'core': (p, banks, sems, X, XN, AR, WS, ONESC, ONESR, EPS1, STATR, ABrow, ABC, TMP,
                    XBF2, SQ2, UPS, DSOS, UTSd, DTSd, mm_done, ws_last),
           'WSF': WSF,
           'wsf_last': {0: attn_done_ms, 1: attn_done_ms}}

    x1_ms = emit_proj_to_x(ctx, OT_OFF, Wod, BIA2, attn_done_ms, "woproj")

    cdsos = ch_v.bump(1)

    def dsos_prep(e, cdsos=cdsos):
        e.tensor_scalar(DSOS[:], DST[:], OSC[:, 0:1], None, AluOpType.mult).then_inc(ch_v.h, 1)
    p.add("vector", dsos_prep)
    dsos_ms = (ch_v, cdsos)

    if upto in ("ffn", "full"):
        xn_a = emit_ln2(ctx, 0, x1_ms)
        ffa = emit_ffn_span(ctx, 0, xn_a, "f0")
        xn_b = emit_ln2(ctx, 1, x1_ms)
        ffb = emit_ffn_span(ctx, 1, xn_b, "f1")
        x2_ms = ffb
    else:
        x2_ms = x1_ms

    def store_x(e, x2_ms=x2_ms):
        wms(e, x2_ms)
        e.dma_start(out=X2Td.rearrange("(c p) t -> p c t", c=NCH),
                    in_=X[:].rearrange("p (c t) -> p c t", c=NCH)).then_inc(s_st[0].h, 16)
    p.add("sync", store_x)
    s_st[0].bump(16)

    es_ffn.close()
    if upto == "full":
        ca1, ca2 = ch_a.bump(1), ch_a.bump(1)

        def decay_prep(e, ca1=ca1, ca2=ca2):
            e.activation(E1[:], TD[:], AF.Exp).then_inc(ch_a.h, 1)
            e.wait_ge(ch_a.h, ca1)
            e.activation(DEC[:], E1[:], AF.Exp, scale=-1.0).then_inc(ch_a.h, 1)
            e.wait_ge(ch_a.h, ca2)
            e.activation(FEFF[:], TF[:], AF.Exp, scale=IND[:, 0:1]).then_inc(s_a.h, 1)
        p.add("scalar", decay_prep)
        decay_ms = (s_a, s_a.bump(1))

        es_b = ExitStack()
        sbb = lambda name, shape, dt: es_b.enter_context(nc.sbuf_tensor(name, shape, dt))
        WT = sbb("WT", [128, SPAN], F32)
        AA = sbb("AA", [128, SPAN], F32)
        NUMT = sbb("NUMT", [128, SPAN], F32)
        DENT = sbb("DENT", [128, SPAN], F32)
        SIGT = sbb("SIGT", [128, SPAN], F32)
        DECB = sbb("DECB", [128, SPAN], F32)
        ONES32 = sbb("ONES32", [128, SPAN], F32)
        WB3 = [sbb(f"WB3{i}", [128, NCH * 128], BF16) for i in range(6)]

        def consts2(e, wl0=s_wl[0].n, wl1=s_wl[1].n, wls0=s_wls[0].n, wls1=s_wls[1].n,
                    ffn_mm=s_ts.n):
            e.wait_ge(s_wl[0].h, wl0)
            e.wait_ge(s_wl[1].h, wl1)
            e.wait_ge(s_wls[0].h, wls0)
            e.wait_ge(s_wls[1].h, wls1)
            e.wait_ge(s_ts.h, ffn_mm)
            e.memset(ONES32[:], 1.0).then_inc(s_g.h, 1)
        p.add("gpsimd", consts2)
        consts2_done = s_g.bump()

        for sp in range(2):
            xnb_ms = emit_ln2(ctx, sp, x2_ms)
            for cc in range(NCH):
                kbank = banks[0 + 3 * (cc % 2)]
                vbank = banks[1 + 3 * (cc % 2)]
                rbank = banks[2 + 3 * (cc % 2)]
                step = sp * NCH + cc
                my_mm = s_ts.n + 3
                for wj in range(3):
                    bslot = wj * 2 + (cc % 2)
                    guard = mm_done.get(("b3", step - 2, wj))

                    def wload(e, wj=wj, cc=cc, bslot=bslot, guard=guard, step=step,
                              c2=consts2_done):
                        if step < 2:
                            e.wait_ge(s_g.h, c2)
                        if guard:
                            e.wait_ge(s_ts.h, guard)
                        src = bWTd[wj].rearrange("(c p) cd -> p c cd", c=NCH)[
                            :, :, cc * 128:(cc + 1) * 128]
                        e.dma_start(out=WB3[bslot][:].rearrange("p (c cd) -> p c cd", c=NCH),
                                    in_=src).then_inc(s_w3[bslot].h, 16)
                    p.add("gpsimd", wload)
                    sl = (s_w3[bslot], s_w3[bslot].bump(16))

                    def mmh(e, wj=wj, cc=cc, bslot=bslot, sl=sl, xnb_ms=xnb_ms,
                            ap_=(kbank.ap, vbank.ap, rbank.ap)[wj],
                            rd=(kbank.r, vbank.r, rbank.r)[wj]):
                        e.wait_ge(sl[0].h, sl[1])
                        if wj == 0 and cc == 0:
                            wms(e, xnb_ms)
                        wms(e, rd)
                        for i in range(NCH):
                            inst = e.matmul(ap_[:],
                                            WB3[bslot][:, i * 128:(i + 1) * 128],
                                            XN[:, i * SPAN:(i + 1) * SPAN],
                                            start=(i == 0), stop=(i == NCH - 1))
                        inst.then_inc(s_ts.h, 1)
                    p.add("tensor", mmh)
                    s_ts.bump(1)
                    mm_done[("b3", step, wj)] = s_ts.n
                kbank.w = vbank.w = rbank.w = (s_ts, my_mm)

                def sig_f(e, rap=rbank.ap, my_mm=my_mm, step=step):
                    e.wait_ge(s_ts.h, my_mm)
                    if step:
                        e.wait_ge(s_stx.h, 48 * step)
                    e.activation(SIGT[:], rap[:], AF.Sigmoid).then_inc(s_a.h, 1)
                p.add("scalar", sig_f)
                sig_ms = (s_a, s_a.bump(1))
                rbank.r = sig_ms

                clip_v = s_v.bump(1)
                cbase = ch_v.n
                exp_done_val = s_a.n + 1

                def kvchain(e, kap=kbank.ap, vap=vbank.ap, my_mm=my_mm, cc=cc, sp=sp,
                            cbase=cbase, decay_ms=decay_ms, step=step,
                            exp_done=exp_done_val, c2_done=consts2_done):
                    e.wait_ge(s_ts.h, my_mm)
                    wms(e, decay_ms)
                    e.wait_ge(s_g.h, c2_done)
                    if step:
                        e.wait_ge(s_stx.h, 48 * step)
                    e.tensor_scalar(WT[:], kap[:], 10.0, -10.0, AluOpType.min,
                                    AluOpType.max).then_inc(s_v.h, 1)
                    e.wait_ge(s_a.h, exp_done)
                    e.tensor_tensor(AA[:], WT[:], vap[:], AluOpType.mult).then_inc(ch_v.h, 1)
                    e.tensor_scalar(DECB[:], ONES32[:], DEC[:, cc:cc + 1], None,
                                    AluOpType.mult).then_inc(ch_v.h, 1)
                    e.wait_ge(ch_v.h, cbase + 2)
                    e.tensor_tensor_scan(NUMT[:], DECB[:], AA[:], 0.0, AluOpType.mult,
                                         AluOpType.add).then_inc(ch_v.h, 1)
                    e.tensor_tensor_scan(DENT[:], DECB[:], WT[:], 0.0, AluOpType.mult,
                                         AluOpType.add).then_inc(ch_v.h, 1)
                    e.wait_ge(ch_v.h, cbase + 4)
                    e.tensor_copy(SUMS[:, cc * 4 + 2 * sp:cc * 4 + 2 * sp + 1],
                                  NUMT[:, SPAN - 1:SPAN]).then_inc(ch_v.h, 1)
                    e.tensor_copy(SUMS[:, cc * 4 + 2 * sp + 1:cc * 4 + 2 * sp + 2],
                                  DENT[:, SPAN - 1:SPAN]).then_inc(s_v.h, 1)
                p.add("vector", kvchain)
                ch_v.bump(5)
                scan_ms = (s_v, s_v.bump(1))
                kbank.r = scan_ms
                vbank.r = scan_ms

                ca = ch_a.bump(1)

                def exp_w(e, clip_v=clip_v, cc=cc, sp=sp, ca=ca):
                    e.wait_ge(s_v.h, clip_v)
                    e.activation(WT[:], WT[:], AF.Exp).then_inc(ch_a.h, 1)
                    e.wait_ge(ch_a.h, ca)
                    e.activation(WT[:, 0:1], WT[:, 0:1], AF.Copy,
                                 scale=FEFF[:, cc:cc + 1] if sp == 0 else 1.0).then_inc(s_a.h, 1)
                p.add("scalar", exp_w)
                s_a.bump(1)

                def stores(e, cc=cc, sp=sp, scan_ms=scan_ms, sig_ms=sig_ms):
                    wms(e, scan_ms)
                    wms(e, sig_ms)
                    e.dma_start(out=NUMLd[cc * 128:(cc + 1) * 128, sp * SPAN:(sp + 1) * SPAN],
                                in_=NUMT[:]).then_inc(s_stx.h, 16)
                    e.dma_start(out=DENLd[cc * 128:(cc + 1) * 128, sp * SPAN:(sp + 1) * SPAN],
                                in_=DENT[:]).then_inc(s_stx.h, 16)
                    e.dma_start(out=SIGd[cc * 128:(cc + 1) * 128, sp * SPAN:(sp + 1) * SPAN],
                                in_=SIGT[:]).then_inc(s_stx.h, 16)
                p.add("sync", stores)
                s_stx.bump(48)

        def store_sums(e, n=s_v.n):
            e.wait_ge(s_v.h, n)
            e.dma_start(out=SUMSd, in_=SUMS[:]).then_inc(s_st[1].h, 16)
        p.add("sync", store_sums)
        s_st[1].bump(16)
        es_b.close()

    def fin(e, a=s_st[0].n, b=s_st[1].n, c=s_stx.n):
        e.wait_ge(s_st[0].h, a)
        if b:
            e.wait_ge(s_st[1].h, b)
        if c:
            e.wait_ge(s_stx.h, c)
    p.add("sync", fin)

    p.emit(es.enter_context(nc.Block()))
    es.close()
    return nc


def stage_l2(inputs, l1_outs, core):
    a0, b0 = span_starts(core)
    toks = np.r_[a0:a0 + SPAN, b0:b0 + SPAN]
    x = np.asarray(inputs["L0_attn_a_x"])[0]
    xTc = np.ascontiguousarray(x[toks, :].T)

    kT_full = np.zeros((D, T), BF)
    vT_full = np.zeros((D, T), BF)
    for c in range(NCORES):
        ca, cb = span_starts(c)
        kT_full[:, ca:ca + SPAN] = l1_outs[c]["kT"][:, 0:SPAN]
        kT_full[:, cb:cb + SPAN] = l1_outs[c]["kT"][:, SPAN:2 * SPAN]
        vT_full[:, ca:ca + SPAN] = l1_outs[c]["vT"][:, 0:SPAN]
        vT_full[:, cb:cb + SPAN] = l1_outs[c]["vT"][:, SPAN:2 * SPAN]
    v_full = np.ascontiguousarray(vT_full.T)

    def kwin(sp):
        q0 = a0 if sp == 0 else b0
        blk0 = q0 // 128
        KW = np.zeros((D, KWIN * 128), BF)
        VWm = np.zeros((KWIN * 128, D), BF)
        for i in range(KWIN):
            g = blk0 + (i - 64)
            if 0 <= g < T // 128:
                KW[:, 128 * i:128 * (i + 1)] = kT_full[:, 128 * g:128 * (g + 1)]
                VWm[128 * i:128 * (i + 1), :] = v_full[128 * g:128 * (g + 1), :]
        return KW, VWm

    KWA, VWA = kwin(0)
    KWB, VWB = kwin(1)
    maskarr, _ = masks_and_offs()

    def bT(b):
        return np.ascontiguousarray(np.asarray(b).reshape(NCH, 128).T.astype(np.float32))

    U = np.asarray(inputs["L0_ffn_up_weight"])
    Dw = np.asarray(inputs["L0_ffn_down_weight"])
    return {
        "XT": xTc,
        "QT": l1_outs[core]["qT"],
        "KWA": KWA, "KWB": KWB, "VWA": VWA, "VWB": VWB,
        "BTAB": build_bias_table(core),
        "MASKS": maskarr,
        "Wo": np.asarray(inputs["L0_attn_a_Wo"]),
        "boT": bT(inputs["L0_attn_a_bo"]),
        "UTS": l1_outs[core]["UTS"],
        "DTS": l1_outs[core]["DTS"],
        "upsT": np.ascontiguousarray(np.asarray(inputs["L0_ffn_up_scale"])
                                     .reshape(DH // 128, 128).T.astype(np.float32)),
        "dsT": bT(inputs["L0_ffn_down_scale"]),
        "osc": np.broadcast_to(np.asarray(inputs["L0_ffn_output_scale"]).reshape(1, 1),
                               (128, 1)).copy(),
        "bWkT": np.ascontiguousarray(np.asarray(inputs["L0_attn_b_Wk"]).T),
        "bWvT": np.ascontiguousarray(np.asarray(inputs["L0_attn_b_Wv"]).T),
        "bWrT": np.ascontiguousarray(np.asarray(inputs["L0_attn_b_Wr"]).T),
        "tdT": np.ascontiguousarray(np.asarray(inputs["L0_attn_b_time_decay"]).T.astype(np.float32)),
        "tfT": np.ascontiguousarray(np.asarray(inputs["L0_attn_b_time_first"]).T.astype(np.float32)),
        "FIRSTIND": np.full((128, 1), 1.0 if core == 0 else 0.0, np.float32),
    }



def build_l3():
    nc = bass.Bass("TRN2", target_bir_lowering=False)
    dram = lambda n, shape, dt, kind="ExternalInput": nc.dram_tensor(n, shape, dt, kind=kind).ap()
    X2Td = dram("X2T", [D, TPC], F32)
    NUMLd = dram("NUML", [D, TPC], F32)
    DENLd = dram("DENL", [D, TPC], F32)
    SIGd = dram("SIG", [D, TPC], F32)
    SUMNd = dram("SUMN", [128, NCH * 16], F32)
    SUMDd = dram("SUMD", [128, NCH * 16], F32)
    SELAd = dram("SELA", [128, 16], F32)
    SELBd = dram("SELB", [128, 16], F32)
    bWoTd = dram("bWoT", [D, D], F32)
    UTSd = dram("UTS", [D, DH], BF16)
    DTSd = dram("DTS", [DH, D], BF16)
    upsTd = dram("upsT", [128, DH // 128], F32)
    dsTd = dram("dsT", [128, NCH], F32)
    oscd = dram("osc", [128, 1], F32)
    tdTd = dram("tdT", [128, NH], F32)
    XOUTd = dram("XOUT", [D, TPC], F32, "ExternalOutput")

    es = ExitStack()
    sb = lambda name, shape, dt: es.enter_context(nc.sbuf_tensor(name, shape, dt))

    X = sb("X", [128, NCH * TPC], F32)
    XN = sb("XN", [128, NCH * SPAN], BF16)
    AR = sb("AR", [128, 2 * NCH * TPC], BF16)
    WS = None
    ONESC = sb("ONESC", [128, 1], BF16)
    ONESR = sb("ONESR", [1, 128], F32)
    EPS1 = sb("EPS1", [1, 1], F32)
    EPS6 = sb("EPS6", [128, 1], F32)
    STATR = sb("STATR", [1, 2 * SPAN], F32)
    ABrow = sb("ABrow", [1, 2 * SPAN], F32)
    ABC = sb("ABC", [128, 2 * SPAN], F32)
    TMP = sb("TMP", [128, SPAN], F32)
    XBF2 = [sb(f"XBF2{i}", [128, SPAN], BF16) for i in range(2)]
    SQ2 = [sb(f"SQ2{i}", [128, SPAN], BF16) for i in range(2)]
    UPS = sb("UPS", [128, DH // 128], F32)
    DST = sb("DST", [128, NCH], F32)
    OSC = sb("OSC", [128, 1], F32)
    DSOS = sb("DSOS", [128, NCH], F32)
    ZB = sb("ZB", [128, NCH], F32)
    TD = sb("TD", [128, NH], F32)
    E1 = sb("E1", [128, NH], F32)
    NE1 = sb("NE1", [128, NH], F32)
    D512 = sb("D512", [128, NH], F32)
    es_w = ExitStack()
    sbw = lambda name, shape, dt: es_w.enter_context(nc.sbuf_tensor(name, shape, dt))
    SUMN = sbw("SUMNs", [128, NCH * 16], F32)
    SUMD = sbw("SUMDs", [128, NCH * 16], F32)
    SELA = sbw("SELAs", [128, 16], F32)
    SELB = sbw("SELBs", [128, 16], F32)
    CARN = sbw("CARN", [128, NCH * 16], F32)
    CARD = sbw("CARD", [128, NCH * 16], F32)
    CAN = [sbw(f"CAN{i}", [128, NCH], F32) for i in range(2)]
    CAD = [sbw(f"CAD{i}", [128, NCH], F32) for i in range(2)]
    SELT = sbw("SELT", [128, 16], F32)
    IOTA1 = sbw("IOTA1", [128, SPAN], F32)
    POW = sbw("POW", [128, SPAN], F32)
    TMPP = sbw("TMPP", [128, SPAN], F32)
    NUMT = sbw("NUMT", [128, SPAN], F32)
    DENT = sbw("DENT", [128, SPAN], F32)
    SGT = sbw("SGT", [128, SPAN], F32)
    RECIP = sbw("RECIP", [128, SPAN], F32)
    WKVF = sbw("WKVF", [128, SPAN], F32)

    PS = [es.enter_context(nc.psum_tensor(f"PS{i}", [128, SPAN], F32)) for i in range(8)]
    banks = [Bank(t) for t in PS]

    sems = {k: Sem(es.enter_context(nc.semaphore("s_" + k)))
            for k in ("ld", "ldx", "v", "a", "ts", "g", "wl0", "wl1", "wls0", "wls1",
                      "st0", "chv", "cha")}
    s_ld, s_v, s_a, s_g = sems["ld"], sems["v"], sems["a"], sems["g"]
    s_ldx, s_ts = sems["ldx"], sems["ts"]
    ch_v, ch_a = sems["chv"], sems["cha"]
    s_st = sems["st0"]

    p = Plan()
    mm_done = {}
    ws_last = {0: None, 1: None}
    ctx = {'core': (p, banks, sems, X, XN, AR, WS, ONESC, ONESR, EPS1, STATR, ABrow, ABC, TMP,
                    XBF2, SQ2, UPS, DSOS, UTSd, DTSd, mm_done, ws_last)}

    def loads(e):
        e.dma_start(out=X[:].rearrange("p (c t) -> p c t", c=NCH),
                    in_=X2Td.rearrange("(c p) t -> p c t", c=NCH)).then_inc(s_ld.h, 16)
        e.dma_start(out=SUMN[:], in_=SUMNd).then_inc(s_ld.h, 16)
        e.dma_start(out=SUMD[:], in_=SUMDd).then_inc(s_ld.h, 16)
        e.dma_start(out=SELA[:], in_=SELAd).then_inc(s_ld.h, 16)
        e.dma_start(out=SELB[:], in_=SELBd).then_inc(s_ld.h, 16)
        e.dma_start(out=UPS[:], in_=upsTd).then_inc(s_ld.h, 16)
        e.dma_start(out=DST[:], in_=dsTd).then_inc(s_ld.h, 16)
        e.dma_start(out=OSC[:], in_=oscd).then_inc(s_ld.h, 16)
        e.dma_start(out=TD[:], in_=tdTd).then_inc(s_ld.h, 16)
    p.add("sync", loads)
    loads_done = s_ld.bump(16 * 9)

    def consts(e):
        e.memset(ONESC[:], 1.0)
        e.memset(ONESR[:], 1.0)
        e.memset(EPS1[:], LN_EPS)
        e.memset(EPS6[:], 1e-6)
        e.memset(ZB[:], 0.0)
        e.memset(CARN[:], 0.0)
        e.memset(CARD[:], 0.0)
        e.iota(IOTA1[:], [[1, SPAN]], base=1, channel_multiplier=0,
               allow_small_or_imprecise_dtypes=True).then_inc(s_g.h, 1)
    p.add("gpsimd", consts)
    consts_done = s_g.bump()

    for eng in ("tensor", "scalar", "vector"):
        def gate(e, _ld=loads_done, _g=consts_done):
            e.wait_ge(s_ld.h, _ld)
            e.wait_ge(s_g.h, _g)
        p.add(eng, gate)

    ca1 = ch_a.bump(1)

    def decay_f(e, ca1=ca1):
        e.activation(E1[:], TD[:], AF.Exp).then_inc(ch_a.h, 1)
        e.wait_ge(ch_a.h, ca1)
        e.activation(D512[:], E1[:], AF.Exp, scale=-512.0).then_inc(s_a.h, 1)
    p.add("scalar", decay_f)
    e1_done = s_a.bump(1)
    cv1 = ch_v.bump(1)

    def ne1_f(e, e1_done=e1_done, cv1=cv1):
        e.wait_ge(s_a.h, e1_done)
        e.tensor_scalar(NE1[:], E1[:], -1.0, None, AluOpType.mult).then_inc(ch_v.h, 1)
    p.add("vector", ne1_f)
    ne1_done = (ch_v, cv1)

    carry_c0 = ch_v.n

    def carry_f(e, e1_done=e1_done, c0=carry_c0):
        e.wait_ge(s_a.h, e1_done)
        step = 0
        for s_ in range(1, 16):
            if s_ > 1:
                e.wait_ge(ch_v.h, c0 + step)
            for cc in range(NCH):
                i0 = cc * 16 + s_
                e.scalar_tensor_tensor(CARN[:, i0:i0 + 1], CARN[:, i0 - 1:i0], D512[:, cc:cc + 1],
                                       SUMN[:, i0 - 1:i0], AluOpType.mult, AluOpType.add)
                inst = e.scalar_tensor_tensor(CARD[:, i0:i0 + 1], CARD[:, i0 - 1:i0], D512[:, cc:cc + 1],
                                              SUMD[:, i0 - 1:i0], AluOpType.mult, AluOpType.add)
            step += 1
            inst.then_inc(ch_v.h, 1)
        e.wait_ge(ch_v.h, c0 + 15)
        k = c0 + 15
        for spi, SEL in ((0, SELA), (1, SELB)):
            for car, dst in ((CARN, CAN[spi]), (CARD, CAD[spi])):
                for cc in range(NCH):
                    e.wait_ge(ch_v.h, k)
                    e.tensor_tensor(SELT[:], car[:, cc * 16:(cc + 1) * 16], SEL[:],
                                    AluOpType.mult).then_inc(ch_v.h, 1)
                    k += 1
                    e.wait_ge(ch_v.h, k)
                    e.reduce_sum(dst[:, cc:cc + 1], SELT[:], axis=mybir.AxisListType.X
                                 ).then_inc(ch_v.h, 1)
                    k += 1
        e.nop().then_inc(s_v.h, 1)
    p.add("vector", carry_f)
    ch_v.bump(15 + 4 * NCH * 2)
    carry_done = s_v.bump(1)

    cdsos = ch_v.bump(1)

    def dsos_prep(e, cdsos=cdsos):
        e.tensor_scalar(DSOS[:], DST[:], OSC[:, 0:1], None, AluOpType.mult).then_inc(ch_v.h, 1)
    p.add("vector", dsos_prep)

    rw_ms = None
    prev_pow_done = 0
    prev_corr1 = None
    for cc in range(NCH):
        def pow_ts(e, cc=cc, ne1_done=ne1_done, prev_pow_done=prev_pow_done):
            wms(e, ne1_done)
            if prev_pow_done:
                e.wait_ge(s_a.h, prev_pow_done)
            e.tensor_scalar(TMPP[:], IOTA1[:], NE1[:, cc:cc + 1], None,
                            AluOpType.mult).then_inc(s_v.h, 1)
        p.add("vector", pow_ts)
        tspw = s_v.bump(1)

        def exp_pow(e, tspw=tspw, prev_corr1=prev_corr1):
            e.wait_ge(s_v.h, tspw)
            wms(e, prev_corr1)
            e.activation(POW[:], TMPP[:], AF.Exp).then_inc(s_a.h, 1)
        p.add("scalar", exp_pow)
        pow_done = s_a.bump(1)
        prev_pow_done = pow_done

        for sp in range(2):
            ld0 = s_ldx.n
            prev_use = rw_ms

            def tload(e, cc=cc, sp=sp, prev_use=prev_use):
                wms(e, prev_use)
                e.dma_start(out=NUMT[:], in_=NUMLd[cc * 128:(cc + 1) * 128,
                                               sp * SPAN:(sp + 1) * SPAN]).then_inc(s_ldx.h, 16)
                e.dma_start(out=DENT[:], in_=DENLd[cc * 128:(cc + 1) * 128,
                                               sp * SPAN:(sp + 1) * SPAN]).then_inc(s_ldx.h, 16)
                e.dma_start(out=SGT[:], in_=SIGd[cc * 128:(cc + 1) * 128,
                                              sp * SPAN:(sp + 1) * SPAN]).then_inc(s_ldx.h, 16)
            p.add("sync", tload)
            tl_done = s_ldx.bump(48)
            cfix = ch_v.bump(1)

            def correct(e, cc=cc, sp=sp, tl_done=tl_done, pow_done=pow_done,
                        carry_done=carry_done, cfix=cfix):
                e.wait_ge(s_ldx.h, tl_done)
                e.wait_ge(s_a.h, pow_done)
                e.wait_ge(s_v.h, carry_done)
                e.scalar_tensor_tensor(NUMT[:], POW[:], CAN[sp][:, cc:cc + 1], NUMT[:],
                                       AluOpType.mult, AluOpType.add).then_inc(ch_v.h, 1)
                e.scalar_tensor_tensor(DENT[:], POW[:], CAD[sp][:, cc:cc + 1], DENT[:],
                                       AluOpType.mult, AluOpType.add).then_inc(s_v.h, 1)
            p.add("vector", correct)
            corr_done = s_v.bump(1)
            ca2 = ch_a.bump(1)

            def recip_f(e, corr_done=corr_done, ca2=ca2):
                e.wait_ge(s_v.h, corr_done)
                e.activation(RECIP[:], DENT[:], AF.Ln, bias=EPS6[:, 0:1]).then_inc(ch_a.h, 1)
                e.wait_ge(ch_a.h, ca2)
                e.activation(RECIP[:], RECIP[:], AF.Exp, scale=-1.0).then_inc(s_a.h, 1)
            p.add("scalar", recip_f)
            recip_done = s_a.bump(1)
            cwv = ch_v.bump(1)

            def rwkv_f(e, cc=cc, sp=sp, recip_done=recip_done, cwv=cwv):
                e.wait_ge(s_a.h, recip_done)
                e.tensor_tensor(WKVF[:], NUMT[:], RECIP[:], AluOpType.mult).then_inc(ch_v.h, 1)
                e.wait_ge(ch_v.h, cwv)
                e.tensor_tensor(AR[:, cc * TPC + sp * SPAN:cc * TPC + (sp + 1) * SPAN],
                                WKVF[:], SGT[:], AluOpType.mult).then_inc(s_v.h, 1)
            p.add("vector", rwkv_f)
            rw_ms = (s_v, s_v.bump(1))
            if sp == 1:
                prev_corr1 = (s_v, corr_done)
    rw_all = rw_ms
    es_w.close()
    WSF = [es.enter_context(nc.sbuf_tensor(f"WSF{i}", [128, 8192], BF16)) for i in range(2)]
    ctx['WSF'] = WSF
    ctx['wsf_last'] = {0: rw_all, 1: rw_all}

    x3_ms = emit_proj_to_x(ctx, 0, bWoTd, ZB, rw_all, "bwo")
    xn_a = emit_ln2(ctx, 0, x3_ms)
    ffa = emit_ffn_span(ctx, 0, xn_a, "g0")
    xn_b = emit_ln2(ctx, 1, x3_ms)
    ffb = emit_ffn_span(ctx, 1, xn_b, "g1")

    def store_x(e, ffb=ffb):
        wms(e, ffb)
        e.dma_start(out=XOUTd.rearrange("(c p) t -> p c t", c=NCH),
                    in_=X[:].rearrange("p (c t) -> p c t", c=NCH)).then_inc(s_st.h, 16)
    p.add("sync", store_x)
    s_st.bump(16)

    def fin(e, a=s_st.n):
        e.wait_ge(s_st.h, a)
    p.add("sync", fin)

    p.emit(es.enter_context(nc.Block()))
    es.close()
    return nc


def stage_l3(inputs, l1_outs, l2_outs, core):
    a0, b0 = span_starts(core)
    SUMN = np.zeros((128, NCH * 16), np.float32)
    SUMD = np.zeros((128, NCH * 16), np.float32)
    for s_ in range(16):
        if s_ < 8:
            oc, osp = s_, 0
        else:
            oc, osp = 15 - s_, 1
        sm = l2_outs[oc]["SUMS"]
        for cc in range(NCH):
            SUMN[:, cc * 16 + s_] = sm[:, cc * 4 + 2 * osp]
            SUMD[:, cc * 16 + s_] = sm[:, cc * 4 + 2 * osp + 1]
    SELA = np.zeros((128, 16), np.float32)
    SELB = np.zeros((128, 16), np.float32)
    SELA[:, core] = 1.0
    SELB[:, 15 - core] = 1.0

    def bT(b):
        return np.ascontiguousarray(np.asarray(b).reshape(NCH, 128).T.astype(np.float32))

    return {
        "X2T": l2_outs[core]["X2T"],
        "NUML": l2_outs[core]["NUML"],
        "DENL": l2_outs[core]["DENL"],
        "SIG": l2_outs[core]["SIG"],
        "SUMN": SUMN, "SUMD": SUMD, "SELA": SELA, "SELB": SELB,
        "bWoT": np.ascontiguousarray(np.asarray(inputs["L0_attn_b_Wo"]).T),
        "UTS": l1_outs[core]["UTS"],
        "DTS": l1_outs[core]["DTS"],
        "upsT": np.ascontiguousarray(np.asarray(inputs["L0_ffn_up_scale"])
                                     .reshape(DH // 128, 128).T.astype(np.float32)),
        "dsT": bT(inputs["L0_ffn_down_scale"]),
        "osc": np.broadcast_to(np.asarray(inputs["L0_ffn_output_scale"]).reshape(1, 1),
                               (128, 1)).copy(),
        "tdT": np.ascontiguousarray(np.asarray(inputs["L0_attn_b_time_decay"]).T.astype(np.float32)),
    }


_NC_CACHE = {}


def kernel(**inputs):
    if "l1" not in _NC_CACHE:
        _NC_CACHE["l1"] = build_l1()
        _NC_CACHE["l2"] = build_l2("full")
        _NC_CACHE["l3"] = build_l3()
    cores = list(range(NCORES))
    r1 = run_bass_kernel_spmd(_NC_CACHE["l1"], stage_l1(inputs), core_ids=cores)
    l1_outs = r1.results
    r2 = run_bass_kernel_spmd(_NC_CACHE["l2"],
                              [stage_l2(inputs, l1_outs, c) for c in cores], core_ids=cores)
    l2_outs = r2.results
    r3 = run_bass_kernel_spmd(_NC_CACHE["l3"],
                              [stage_l3(inputs, l1_outs, l2_outs, c) for c in cores],
                              core_ids=cores)
    out = np.zeros((T, D), np.float32)
    for c in cores:
        a0, b0 = span_starts(c)
        xo = r3.results[c]["XOUT"]
        out[a0:a0 + SPAN] = xo[:, 0:SPAN].T
        out[b0:b0 + SPAN] = xo[:, SPAN:2 * SPAN].T
    return out[None, :, :]


# version 06
# speedup vs baseline: 1.5177x; 1.3362x over previous
import math
import numpy as np
import ml_dtypes
from contextlib import ExitStack

import concourse.bass as bass
import concourse.mybir as mybir
from concourse.alu_op_type import AluOpType
from concourse.bass_utils import run_bass_kernel_spmd

F32 = mybir.dt.float32
BF16 = mybir.dt.bfloat16
AF = mybir.ActivationFunctionType
BF = ml_dtypes.bfloat16

NCORES = 8
T = 8192
D = 2048
NH = 16
HDIM = 128
DH = 8192
NCH = 16
TPC = 1024
SPAN = 512
QSCALE = 1.0 / math.sqrt(HDIM)
LN_EPS = 1e-5

SLOPES = [2.0 ** (-0.5 * (i + 1)) for i in range(NH)]


def span_starts(c):
    return 512 * c, 7680 - 512 * c


class Sem:
    def __init__(self, handle):
        self.h = handle
        self.n = 0

    def bump(self, k=1):
        self.n += k
        return self.n


class Plan:
    ENGS = ("tensor", "vector", "scalar", "gpsimd", "sync")

    def __init__(self):
        self.ops = {e: [] for e in self.ENGS}

    def add(self, eng, fn):
        self.ops[eng].append(fn)

    def emit(self, block):
        plan = self

        @block.sync
        def _(e):
            for fn in plan.ops["sync"]:
                fn(e)

        @block.tensor
        def _(e):
            for fn in plan.ops["tensor"]:
                fn(e)

        @block.vector
        def _(e):
            for fn in plan.ops["vector"]:
                fn(e)

        @block.scalar
        def _(e):
            for fn in plan.ops["scalar"]:
                fn(e)

        @block.gpsimd
        def _(e):
            for fn in plan.ops["gpsimd"]:
                fn(e)


def chunked_colmajor(ap, nch, per):
    return ap.rearrange("(c p) t -> p (c t)", c=nch)



def emit_layernorm(p, sems, bufs, nch, t_stride, t0, span, src_X, dst_XN, dst_dtype_note=""):
    s_v, s_a, s_t, s_g, s_gx = sems["v"], sems["a"], sems["t"], sems["g"], sems["gx"]
    XBF2, SQ2 = bufs["XBF2"], bufs["SQ2"]
    ONESC, ONESR, EPS1 = bufs["ONESC"], bufs["ONESR"], bufs["EPS1"]
    STATR, AB, ABC, TMP = bufs["STATR"], bufs["AB"], bufs["ABC"], bufs["TMP"]
    PS_MU, PS_SQ, PS_AB = bufs["PS_MU"], bufs["PS_SQ"], bufs["PS_AB"]
    consts_ready = bufs["consts_ready"]
    x_ready = bufs["x_ready"]
    ch_v = sems["chv"]

    c0 = ch_v.n
    tchk0 = s_t.n

    def cast_chunks(e, c0=c0, tchk0=tchk0):
        for i in range(nch):
            if i >= 2:
                e.wait_ge(s_t.h, tchk0 + i - 1)
            e.tensor_copy(XBF2[i % 2][:], src_X[:, i * t_stride + t0:i * t_stride + t0 + span]
                          ).then_inc(ch_v.h, 1)
            e.wait_ge(ch_v.h, c0 + 2 * i + 1)
            e.tensor_tensor(SQ2[i % 2][:], XBF2[i % 2][:], XBF2[i % 2][:],
                            AluOpType.mult).then_inc(ch_v.h, 1)
    p.add("vector", cast_chunks)
    ch_v.bump(2 * nch)

    def stats_mm(e, c0=c0):
        e.wait_ge(s_g.h, consts_ready)
        for i in range(nch):
            e.wait_ge(ch_v.h, c0 + 2 * i + 2)
            e.matmul(PS_MU[0:1, :], ONESC[:], XBF2[i % 2][:],
                     start=(i == 0), stop=(i == nch - 1))
            inst = e.matmul(PS_SQ[0:1, :], ONESC[:], SQ2[i % 2][:],
                            start=(i == 0), stop=(i == nch - 1))
            inst.then_inc(s_t.h, 1)
    p.add("tensor", stats_mm)
    stats_done = s_t.bump(nch)
    bufs["stats_done_out"] = stats_done

    def stats_fin(e):
        e.wait_ge(s_t.h, stats_done)
        e.tensor_scalar(STATR[0:1, 0:span], PS_MU[0:1, :], 1.0 / D, None, AluOpType.mult)
        e.tensor_scalar(STATR[0:1, span:2 * span], PS_SQ[0:1, :], 1.0 / D, None,
                        AluOpType.mult).then_inc(ch_v.h, 1)
        e.wait_ge(ch_v.h, stats_fin.c1)
        e.tensor_tensor(AB[0:1, span:2 * span], STATR[0:1, 0:span], STATR[0:1, 0:span],
                        AluOpType.mult).then_inc(ch_v.h, 1)
        e.wait_ge(ch_v.h, stats_fin.c2)
        e.tensor_tensor(AB[0:1, span:2 * span], STATR[0:1, span:2 * span], AB[0:1, span:2 * span],
                        AluOpType.subtract).then_inc(s_v.h, 1)
    stats_fin.c1 = ch_v.bump()
    stats_fin.c2 = ch_v.bump()
    p.add("vector", stats_fin)
    var_done = s_v.bump()

    ch_a = sems["cha"]

    def rstd_f(e):
        e.wait_ge(s_v.h, var_done)
        e.activation(AB[0:1, 0:span], AB[0:1, span:2 * span], AF.Ln, bias=EPS1[0:1, 0:1],
                     scale=1.0).then_inc(ch_a.h, 1)
        e.wait_ge(ch_a.h, rstd_f.c1)
        e.activation(AB[0:1, 0:span], AB[0:1, 0:span], AF.Exp, scale=-0.5).then_inc(s_a.h, 1)
    rstd_f.c1 = ch_a.bump()
    p.add("scalar", rstd_f)
    rstd_done = s_a.bump()

    def brow_f(e):
        e.wait_ge(s_a.h, rstd_done)
        e.tensor_tensor(AB[0:1, span:2 * span], STATR[0:1, 0:span], AB[0:1, 0:span],
                        AluOpType.mult).then_inc(ch_v.h, 1)
        e.wait_ge(ch_v.h, brow_f.c1)
        e.tensor_scalar(AB[0:1, span:2 * span], AB[0:1, span:2 * span], -1.0, None,
                        AluOpType.mult).then_inc(s_v.h, 1)
    brow_f.c1 = ch_v.bump()
    p.add("vector", brow_f)
    brow_done = s_v.bump()

    def bcast_mm(e):
        e.wait_ge(s_v.h, brow_done)
        e.matmul(PS_AB[:, 0:span], ONESR[:], AB[0:1, 0:span], start=True, stop=True)
        e.matmul(PS_AB[:, span:2 * span], ONESR[:], AB[0:1, span:2 * span],
                 start=True, stop=True).then_inc(s_t.h, 1)
    p.add("tensor", bcast_mm)
    bcast_done = s_t.bump()

    def evac_bcast(e):
        e.wait_ge(s_t.h, bcast_done)
        e.tensor_copy(ABC[:, 0:span], PS_AB[:, 0:span])
        e.tensor_copy(ABC[:, span:2 * span], PS_AB[:, span:2 * span]).then_inc(s_v.h, 1)
    p.add("vector", evac_bcast)
    abc_done = s_v.bump()

    def apply_f(e):
        e.wait_ge(s_v.h, abc_done)
        for i in range(nch):
            if i:
                e.wait_ge(ch_v.h, apply_f.c0 + 2 * i)
            e.tensor_tensor(TMP[:, 0:span], src_X[:, i * t_stride + t0:i * t_stride + t0 + span],
                            ABC[:, 0:span], AluOpType.mult).then_inc(ch_v.h, 1)
            e.wait_ge(ch_v.h, apply_f.c0 + 2 * i + 1)
            e.tensor_tensor(dst_XN[:, i * t_stride + t0:i * t_stride + t0 + span], TMP[:, 0:span],
                            ABC[:, span:2 * span], AluOpType.add).then_inc(ch_v.h, 1)
    apply_f.c0 = ch_v.n
    done = ch_v.bump(2 * nch)
    p.add("vector", apply_f)
    return done



def build_l1():
    nc = bass.Bass("TRN2", target_bir_lowering=False)
    xT = nc.dram_tensor("xT", [D, TPC], F32, kind="ExternalInput").ap()
    Ws = {n: nc.dram_tensor(n, [D, D], F32, kind="ExternalInput").ap() for n in ("Wq", "Wk", "Wv")}
    bTs = {n: nc.dram_tensor(n, [128, NCH], F32, kind="ExternalInput").ap()
           for n in ("bqT", "bkT", "bvT")}
    OUTs = {n: nc.dram_tensor(n, [D, TPC], BF16, kind="ExternalOutput").ap()
            for n in ("qT", "kT", "vT")}
    UTd = nc.dram_tensor("UT", [D, DH], F32, kind="ExternalInput").ap()
    DTd = nc.dram_tensor("DT", [DH, D], F32, kind="ExternalInput").ap()
    UTSd = nc.dram_tensor("UTS", [D, DH], BF16, kind="ExternalOutput").ap()
    DTSd = nc.dram_tensor("DTS", [DH, D], BF16, kind="ExternalOutput").ap()

    es = ExitStack()
    sb = lambda name, shape, dt: es.enter_context(nc.sbuf_tensor(name, shape, dt))
    ps_ = lambda name, shape, dt=F32: es.enter_context(nc.psum_tensor(name, shape, dt))

    X = sb("X", [128, NCH * TPC], F32)
    XN = sb("XN", [128, NCH * TPC], BF16)
    XBF2 = [sb(f"XBF2{i}", [128, SPAN], BF16) for i in range(2)]
    SQ2 = [sb(f"SQ2{i}", [128, SPAN], BF16) for i in range(2)]
    ONESC = sb("ONESC", [128, 1], BF16)
    EPS1 = sb("EPS1", [1, 1], F32)
    ONESR = sb("ONESR", [1, 128], F32)
    STATR = sb("STATR", [1, 2 * SPAN], F32)
    AB = sb("AB", [1, 2 * SPAN], F32)
    ABC = sb("ABC", [128, 2 * SPAN], F32)
    TMP = sb("TMP", [128, SPAN], F32)
    WBF = [sb(f"WBF{i}", [128, NCH * 512], BF16) for i in range(2)]
    BIA = sb("BIA", [128, 3 * NCH], F32)
    OUTB = [sb(f"OUTB{i}", [128, SPAN], BF16) for i in range(2)]
    SGB = [sb(f"SGB{i}", [128, 4096], BF16) for i in range(2)]
    SGF = [sb(f"SGF{i}", [128, 4096], F32) for i in range(2)]

    PS_MU = ps_("PS_MU", [1, SPAN])
    PS_SQ = ps_("PS_SQ", [1, SPAN])
    PS_AB = ps_("PS_AB", [128, 2 * SPAN])
    PS_O = [ps_(f"PS_O{i}", [128, SPAN]) for i in range(2)]

    sems = {k: Sem(es.enter_context(nc.semaphore("s_" + k)))
            for k in ("ld", "v", "a", "t", "g", "gx", "st0", "st1", "wl0", "wl1", "ev",
                      "chv", "cha", "sg0", "sg1", "so0", "so1", "sa")}
    s_ld, s_v, s_a, s_t, s_g = sems["ld"], sems["v"], sems["a"], sems["t"], sems["g"]
    s_st = [sems["st0"], sems["st1"]]
    s_wl = [sems["wl0"], sems["wl1"]]
    s_ev, s_gx = sems["ev"], sems["gx"]
    s_sg = [sems["sg0"], sems["sg1"]]
    s_so = [sems["so0"], sems["so1"]]
    s_sa = sems["sa"]

    p = Plan()

    def loads(e):
        e.dma_start(out=X[:].rearrange("p (c t) -> p c t", c=NCH),
                    in_=xT.rearrange("(c p) t -> p c t", c=NCH)).then_inc(s_ld.h, 16)
        for j, n in enumerate(("bqT", "bkT", "bvT")):
            e.dma_start(out=BIA[:, j * NCH:(j + 1) * NCH], in_=bTs[n]).then_inc(s_ld.h, 16)
    p.add("sync", loads)
    s_ld.bump(16 * 4)
    x_loaded = 16
    bias_loaded = s_ld.n

    def consts(e):
        e.memset(ONESC[:], 1.0)
        e.memset(ONESR[:], 1.0)
        e.memset(EPS1[:], LN_EPS).then_inc(s_g.h, 1)
    p.add("gpsimd", consts)
    consts_ready = s_g.bump()

    def bias_prep(e):
        e.wait_ge(s_ld.h, bias_loaded)
        e.tensor_scalar(BIA[:, 0:NCH], BIA[:, 0:NCH], QSCALE, None, AluOpType.mult).then_inc(s_v.h, 1)
    p.add("vector", bias_prep)
    bias_ready = s_v.bump()

    ln_done = []
    for sp in range(2):
        t0 = sp * SPAN
        bufs = dict(XBF2=XBF2, SQ2=SQ2, ONESC=ONESC, ONESR=ONESR, STATR=STATR, AB=AB, ABC=ABC,
                    TMP=TMP, PS_MU=PS_MU, PS_SQ=PS_SQ, PS_AB=PS_AB, EPS1=EPS1,
                    consts_ready=consts_ready, x_ready=0)

        def wait_x(e):
            e.wait_ge(s_ld.h, s_ld.n)
            e.nop().then_inc(s_v.h, 1)
        p.add("vector", wait_x)
        s_v.bump()

        done = emit_layernorm(p, {"v": s_v, "a": s_a, "t": s_t, "g": s_g, "gx": s_gx,
                                  "chv": sems["chv"], "cha": sems["cha"]}, bufs,
                              NCH, TPC, t0, SPAN, X, XN)
        ln_done.append(done)

    evac_cnt = 0
    mm_groups = 0
    wl_sets = 0
    projs = [("Wq", "qT", 0, QSCALE), ("Wk", "kT", 1, 1.0), ("Wv", "vT", 2, 1.0)]
    set_mm_done = {}
    for oi, (wn, on, bidx, scal) in enumerate(projs):
        W, OUT = Ws[wn], OUTs[on]
        for espan in range(4):
            step = oi * 4 + espan
            buf = step % 2
            guard = set_mm_done.get(step - 2, 0)

            def wload(e, W=W, espan=espan, buf=buf, guard=guard):
                if guard:
                    e.wait_ge(s_t.h, guard)
                src = W.rearrange("(c p) e -> p c e", c=NCH)[:, :, espan * 512:(espan + 1) * 512]
                e.dma_start(out=WBF[buf][:].rearrange("p (c e) -> p c e", c=NCH),
                            in_=src).then_inc(s_wl[buf].h, 16)
            p.add("gpsimd", wload)
            set_loaded = s_wl[buf].bump(16)

            chv_h = sems["chv"].h

            def mms(e, buf=buf, set_loaded=set_loaded, step=step, ln_all=tuple(ln_done),
                    chv_h=chv_h):
                e.wait_ge(s_wl[buf].h, set_loaded)
                if step == 0:
                    e.wait_ge(chv_h, max(ln_all))
                for eb in range(4):
                    for tsp in range(2):
                        j = (step * 8 + eb * 2 + tsp)
                        bank = j % 2
                        if j >= 2:
                            e.wait_ge(s_ev.h, j - 1)
                        for i in range(NCH):
                            inst = e.matmul(PS_O[bank][:],
                                            WBF[buf][:, i * 512 + eb * 128:i * 512 + (eb + 1) * 128],
                                            XN[:, i * TPC + tsp * SPAN:i * TPC + (tsp + 1) * SPAN],
                                            start=(i == 0), stop=(i == NCH - 1))
                        inst.then_inc(s_t.h, 1)
            p.add("tensor", mms)
            first_tile = s_t.n + 1
            set_mm_done[step] = s_t.bump(8)

            def evacs(e, bidx=bidx, scal=scal, first_tile=first_tile, step=step):
                for eb in range(4):
                    for tsp in range(2):
                        j = step * 8 + eb * 2 + tsp
                        bank = j % 2
                        e.wait_ge(s_t.h, first_tile + eb * 2 + tsp)
                        if j >= 2:
                            e.wait_ge(s_st[j % 2].h, 16 * (j // 2))
                        col = bidx * NCH + (step % 4) * 4 + eb
                        e.tensor_scalar(OUTB[bank][:], PS_O[bank][:], scal,
                                        BIA[:, col:col + 1], AluOpType.mult, AluOpType.add
                                        ).then_inc(s_ev.h, 1)
            p.add("vector", evacs)
            first_ev = s_ev.n + 1
            s_ev.bump(8)

            def stores(e, OUT=OUT, espan=espan, first_ev=first_ev, step=step):
                for eb in range(4):
                    for tsp in range(2):
                        j = step * 8 + eb * 2 + tsp
                        bank = j % 2
                        e.wait_ge(s_ev.h, first_ev + eb * 2 + tsp)
                        r0 = espan * 512 + eb * 128
                        e.dma_start(out=OUT[r0:r0 + 128, tsp * SPAN:(tsp + 1) * SPAN],
                                    in_=OUTB[bank][:]).then_inc(s_st[bank].h, 16)
            p.add("sync", stores)
            s_st[0].bump(16 * 4)
            s_st[1].bump(16 * 4)

    sign_units = []
    for rc in range(16):
        for half in range(2):
            sign_units.append((UTd[rc * 128:(rc + 1) * 128, half * 4096:(half + 1) * 4096],
                               UTSd[rc * 128:(rc + 1) * 128, half * 4096:(half + 1) * 4096]))
    for u in range(32):
        sign_units.append((DTd[u * 256:(u + 1) * 256, :].rearrange("(c p) dd -> p c dd", c=2),
                           DTSd[u * 256:(u + 1) * 256, :].rearrange("(c p) dd -> p c dd", c=2)))

    def emit_sgload(ui):
        sap = sign_units[ui][0]
        par = ui % 2

        def sg_load(e, sap=sap, par=par, ui=ui, sa_guard=(ui - 1) if ui >= 2 else 0):
            if sa_guard:
                e.wait_ge(s_sa.h, sa_guard)
            dst = SGF[par][:]
            if len(sap.shape) == 3:
                dst = dst.rearrange("p (c dd) -> p c dd", c=2)
            e.dma_start(out=dst, in_=sap).then_inc(s_sg[par].h, 16)
        p.add("scalar", sg_load)
        s_sg[par].bump(16)
        return s_sg[par].n

    ldvals = {}
    ldvals[0] = emit_sgload(0)
    ldvals[1] = emit_sgload(1)
    for ui in range(len(sign_units)):
        par = ui % 2
        ld_val = ldvals[ui]

        def sg_sign(e, par=par, ld_val=ld_val, ui=ui):
            e.wait_ge(s_sg[par].h, ld_val)
            if ui >= 2:
                e.wait_ge(s_so[par].h, 16 * (ui // 2))
            e.activation(SGB[par][:], SGF[par][:], AF.Sign).then_inc(s_sa.h, 1)
        p.add("scalar", sg_sign)
        sa_val = s_sa.bump(1)
        if ui + 2 < len(sign_units):
            ldvals[ui + 2] = emit_sgload(ui + 2)

        def sg_store(e, dap=sign_units[ui][1], par=par, sa_val=sa_val):
            e.wait_ge(s_sa.h, sa_val)
            srcb = SGB[par][:]
            if len(dap.shape) == 3:
                srcb = srcb.rearrange("p (c dd) -> p c dd", c=2)
            e.dma_start(out=dap, in_=srcb).then_inc(s_so[par].h, 16)
        p.add("sync", sg_store)
        s_so[par].bump(16)

    def fin(e):
        e.wait_ge(s_st[0].h, s_st[0].n)
        e.wait_ge(s_st[1].h, s_st[1].n)
        e.wait_ge(s_so[0].h, s_so[0].n)
        e.wait_ge(s_so[1].h, s_so[1].n)
    p.add("sync", fin)

    p.emit(es.enter_context(nc.Block()))
    es.close()
    return nc



def stage_l1(inputs):
    x = np.asarray(inputs["L0_attn_a_x"])[0]
    Wq = np.asarray(inputs["L0_attn_a_Wq"])
    Wk = np.asarray(inputs["L0_attn_a_Wk"])
    Wv = np.asarray(inputs["L0_attn_a_Wv"])
    bq = np.asarray(inputs["L0_attn_a_bq"])
    bk = np.asarray(inputs["L0_attn_a_bk"])
    bv = np.asarray(inputs["L0_attn_a_bv"])
    UT_T = np.ascontiguousarray(np.asarray(inputs["L0_ffn_up_weight"]).T)
    DT_T = np.ascontiguousarray(np.asarray(inputs["L0_ffn_down_weight"]).T)

    def bT(b):
        return np.ascontiguousarray(b.reshape(NCH, 128).T)

    in_maps = []
    for c in range(NCORES):
        a0, b0 = span_starts(c)
        toks = np.r_[a0:a0 + SPAN, b0:b0 + SPAN]
        xTc = np.ascontiguousarray(x[toks, :].T)
        in_maps.append({
            "xT": xTc, "Wq": Wq, "Wk": Wk, "Wv": Wv,
            "bqT": bT(bq), "bkT": bT(bk), "bvT": bT(bv),
            "UT": UT_T, "DT": DT_T,
        })
    return in_maps


def kernel(**inputs):
    raise NotImplementedError("under construction: L2/L3 pending")



SUBW = [32, 64, 64, 128, 128, 256, 256, 512] + [512] * 8
WBLK = [2, 3, 4, 5, 7, 10, 14, 19, 27, 38, 54, 64, 64, 64, 64, 64]
KWIN = 68
HGROUPS = [(0, 1, 2), (3, 4, 5), (6, 7, 8), (9, 10, 11), (12, 13, 14), (15,)]
KBATCH = 4


def attention_sites(span):
    groups = []
    for gh in HGROUPS:
        persite = []
        kbs_needed = set()
        for h in gh:
            W = SUBW[h]
            nsub = 512 // W
            w = WBLK[h]
            for s in range(nsub):
                q0 = W * s
                jlast = (W * (s + 1) - 1) // 128
                kb_hi = 64 + jlast
                kb_lo = kb_hi - (w - 1)
                kb_lo = max(kb_lo, 0)
                for kb in range(kb_lo, kb_hi + 1):
                    kbs_needed.add(kb)
                persite.append((h, s, q0, W, kb_lo, kb_hi))
        kbs_sorted = sorted(kbs_needed)
        batches = []
        for b0 in range(0, len(kbs_sorted), KBATCH):
            kb_list = kbs_sorted[b0:b0 + KBATCH]
            site_list = []
            for (h, s, q0, W, kb_lo, kb_hi) in persite:
                for kb in kb_list:
                    if kb_lo <= kb <= kb_hi:
                        r = q0 - 128 * (kb - 64)
                        jq_lo = q0 // 128
                        jq_hi = (q0 + W - 1) // 128
                        overlaps = (64 + jq_lo) <= kb <= (64 + jq_hi)
                        mask_key = (W, r) if overlaps else None
                        site_list.append(dict(h=h, sub=s, q0=q0, W=W, kb=kb,
                                              first=(kb == kb_lo), last=(kb == kb_hi),
                                              mask_key=mask_key))
            batches.append((kb_list, site_list))
        groups.append((gh, batches))
    return groups


def all_mask_keys():
    keys = []
    for span in range(1):
        for gh, batches in attention_sites(0):
            for kb_list, sites in batches:
                for st in sites:
                    if st["mask_key"] and st["mask_key"] not in keys:
                        keys.append(st["mask_key"])
    return sorted(keys)


def build_masks():
    keys = all_mask_keys()
    offs = {}
    cols = []
    pos = 0
    for (W, r) in keys:
        kk = np.arange(128)[:, None]
        qq = np.arange(W)[None, :]
        m = (kk <= qq + r).astype(BF)
        offs[(W, r)] = (pos, W)
        cols.append(m)
        pos += W
    return np.concatenate(cols, axis=1), offs


def site_count():
    n = 0
    for gh, batches in attention_sites(0):
        for kb_list, sites in batches:
            n += len(sites)
    return n


def build_bias_table(core):
    nsites = site_count()
    tab = np.zeros((128, 2 * nsites), np.float32)
    for sp in range(2):
        a0, b0 = span_starts(core)
        q_glob0 = a0 if sp == 0 else b0
        blk0 = q_glob0 // 128
        idx = 0
        for gh, batches in attention_sites(sp):
            for kb_list, sites in batches:
                for st in sites:
                    h, kb, W, q0 = st["h"], st["kb"], st["W"], st["q0"]
                    gkb = blk0 + (kb - 64)
                    col = sp * nsites + idx
                    if gkb < 0:
                        tab[:, col] = -1e30
                    else:
                        slope = SLOPES[h]
                        kglob = 128 * gkb + np.arange(128)
                        qg0 = q_glob0 + q0
                        c = slope * (W / 2.0)
                        v = slope * (kglob - qg0) - c
                        v[kglob > qg0 + W - 1] = -1e30
                        tab[:, col] = v
                    idx += 1
    return tab



class Bank:
    def __init__(self, ap):
        self.ap = ap
        self.w = None
        self.r = None


def wms(e, ms):
    if ms is not None:
        e.wait_ge(ms[0].h, ms[1])


_M, _OFFS = None, None


def masks_and_offs():
    global _M, _OFFS
    if _M is None:
        _M, _OFFS = build_masks()
    return _M, _OFFS


def emit_ln2(ctx, sp, x_ready_ms):
    (p, banks, sems, X, XN, AR, WS, ONESC, ONESR, EPS1, STATR, ABrow, ABC, TMP,
     XBF2, SQ2, UPS, DSOS, UTSd, DTSd, mm_done, ws_last) = ctx['core']
    s_v, s_a, s_ts, ch_v, ch_a = sems['v'], sems['a'], sems['ts'], sems['chv'], sems['cha']
    s_wl = [sems['wl0'], sems['wl1']]
    s_wls = [sems['wls0'], sems['wls1']]
    t0 = sp * SPAN
    b_mu, b_sq, b_a, b_b = banks[4], banks[5], banks[6], banks[7]
    c0 = ch_v.n
    mu_rd, sq_rd = b_mu.r, b_sq.r
    tchk0 = s_ts.n

    def cast_chunks(e, t0=t0, x_ready_ms=x_ready_ms, c0=c0, tchk0=tchk0):
        wms(e, x_ready_ms)
        for i in range(NCH):
            if i >= 2:
                e.wait_ge(s_ts.h, tchk0 + i - 1)
            e.tensor_copy(XBF2[i % 2][:], X[:, i * TPC + t0:i * TPC + t0 + SPAN]
                          ).then_inc(ch_v.h, 1)
            e.wait_ge(ch_v.h, c0 + 2 * i + 1)
            e.tensor_tensor(SQ2[i % 2][:], XBF2[i % 2][:], XBF2[i % 2][:],
                            AluOpType.mult).then_inc(ch_v.h, 1)
    p.add("vector", cast_chunks)
    ch_v.bump(2 * NCH)

    def stats_mm(e, c0=c0, b_mu_ap=b_mu.ap, b_sq_ap=b_sq.ap, rd=(mu_rd, sq_rd)):
        wms(e, rd[0])
        wms(e, rd[1])
        for i in range(NCH):
            e.wait_ge(ch_v.h, c0 + 2 * i + 2)
            e.matmul(b_mu_ap[0:1, :], ONESC[:], XBF2[i % 2][:],
                     start=(i == 0), stop=(i == NCH - 1))
            inst = e.matmul(b_sq_ap[0:1, :], ONESC[:], SQ2[i % 2][:],
                            start=(i == 0), stop=(i == NCH - 1))
            inst.then_inc(s_ts.h, 1)
    p.add("tensor", stats_mm)
    stats_done = s_ts.bump(NCH)
    b_mu.w = b_sq.w = (s_ts, stats_done)
    c1, c2, c3 = ch_v.bump(1), ch_v.bump(1), ch_v.bump(1)

    def stats_fin(e, stats_done=stats_done, b_mu_ap=b_mu.ap, b_sq_ap=b_sq.ap,
                  c1=c1, c2=c2):
        e.wait_ge(s_ts.h, stats_done)
        e.tensor_scalar(STATR[0:1, 0:SPAN], b_mu_ap[0:1, :], 1.0 / D, None, AluOpType.mult)
        e.tensor_scalar(STATR[0:1, SPAN:2 * SPAN], b_sq_ap[0:1, :], 1.0 / D, None,
                        AluOpType.mult).then_inc(ch_v.h, 1)
        e.wait_ge(ch_v.h, c1)
        e.tensor_tensor(ABrow[0:1, SPAN:2 * SPAN], STATR[0:1, 0:SPAN], STATR[0:1, 0:SPAN],
                        AluOpType.mult).then_inc(ch_v.h, 1)
        e.wait_ge(ch_v.h, c2)
        e.tensor_tensor(ABrow[0:1, SPAN:2 * SPAN], STATR[0:1, SPAN:2 * SPAN],
                        ABrow[0:1, SPAN:2 * SPAN], AluOpType.subtract).then_inc(ch_v.h, 1)
    p.add("vector", stats_fin)
    var_done = (ch_v, ch_v.n)
    b_mu.r = b_sq.r = var_done
    ca = ch_a.bump(1)

    def rstd_f(e, var_done=var_done, ca=ca):
        wms(e, var_done)
        e.activation(ABrow[0:1, 0:SPAN], ABrow[0:1, SPAN:2 * SPAN], AF.Ln,
                     bias=EPS1[0:1, 0:1], scale=1.0).then_inc(ch_a.h, 1)
        e.wait_ge(ch_a.h, ca)
        e.activation(ABrow[0:1, 0:SPAN], ABrow[0:1, 0:SPAN], AF.Exp,
                     scale=-0.5).then_inc(s_a.h, 1)
    p.add("scalar", rstd_f)
    rstd_done = s_a.bump(1)
    c4 = ch_v.bump(1)

    def brow_f(e, rstd_done=rstd_done, c4=c4):
        e.wait_ge(s_a.h, rstd_done)
        e.tensor_tensor(ABrow[0:1, SPAN:2 * SPAN], STATR[0:1, 0:SPAN], ABrow[0:1, 0:SPAN],
                        AluOpType.mult).then_inc(ch_v.h, 1)
        e.wait_ge(ch_v.h, c4)
        e.tensor_scalar(ABrow[0:1, SPAN:2 * SPAN], ABrow[0:1, SPAN:2 * SPAN], -1.0, None,
                        AluOpType.mult).then_inc(s_v.h, 1)
    p.add("vector", brow_f)
    brow_done = s_v.bump(1)

    def bcast_mm(e, brow_done=brow_done, b_a_ap=b_a.ap, b_b_ap=b_b.ap,
                 rd=(b_a.r, b_b.r)):
        e.wait_ge(s_v.h, brow_done)
        wms(e, rd[0])
        wms(e, rd[1])
        e.matmul(b_a_ap[:], ONESR[:], ABrow[0:1, 0:SPAN], start=True, stop=True)
        e.matmul(b_b_ap[:], ONESR[:], ABrow[0:1, SPAN:2 * SPAN], start=True,
                 stop=True).then_inc(s_ts.h, 1)
    p.add("tensor", bcast_mm)
    bcast_done = s_ts.bump(1)
    b_a.w = b_b.w = (s_ts, bcast_done)
    c5 = ch_v.bump(1)

    def evac_bcast(e, bcast_done=bcast_done, b_a_ap=b_a.ap, b_b_ap=b_b.ap, c5=c5):
        e.wait_ge(s_ts.h, bcast_done)
        e.tensor_copy(ABC[:, 0:SPAN], b_a_ap[:]).then_inc(ch_v.h, 1)
        e.wait_ge(ch_v.h, c5)
        e.tensor_copy(ABC[:, SPAN:2 * SPAN], b_b_ap[:]).then_inc(s_v.h, 1)
    p.add("vector", evac_bcast)
    abc_done = s_v.bump(1)
    b_a.r = b_b.r = (s_v, abc_done)
    ap_c0 = ch_v.n

    def apply_f(e, abc_done=abc_done, t0=t0, ap_c0=ap_c0):
        e.wait_ge(s_v.h, abc_done)
        for i in range(NCH):
            if i:
                e.wait_ge(ch_v.h, ap_c0 + 2 * i)
            e.tensor_tensor(TMP[:, 0:SPAN], X[:, i * TPC + t0:i * TPC + t0 + SPAN],
                            ABC[:, 0:SPAN], AluOpType.mult).then_inc(ch_v.h, 1)
            e.wait_ge(ch_v.h, ap_c0 + 2 * i + 1)
            e.tensor_tensor(XN[:, i * SPAN:(i + 1) * SPAN], TMP[:, 0:SPAN],
                            ABC[:, SPAN:2 * SPAN], AluOpType.add).then_inc(ch_v.h, 1)
    p.add("vector", apply_f)
    ch_v.bump(2 * NCH)
    return (ch_v, ch_v.n)



def emit_ffn_span(ctx, sp, xn_ms, name):
    (p, banks, sems, X, XN, AR, WS, ONESC, ONESR, EPS1, STATR, ABrow, ABC, TMP,
     XBF2, SQ2, UPS, DSOS, UTSd, DTSd, mm_done, ws_last) = ctx['core']
    s_v, s_a, s_ts, ch_v, ch_a = sems['v'], sems['a'], sems['ts'], sems['chv'], sems['cha']
    s_wls = [sems['wls0'], sems['wls1']]
    WSF = ctx['WSF']
    wsf_last = ctx.setdefault('wsf_last', {0: None, 1: None})
    for hs in range(16):
        buf = hs % 2
        guard = wsf_last[buf]

        def wload(e, hs=hs, buf=buf, guard=guard):
            wms(e, guard)
            src = UTSd.rearrange("(c p) hh -> p c hh", c=NCH)[:, :, hs * 512:(hs + 1) * 512]
            e.dma_start(out=WSF[buf][:].rearrange("p (c hh) -> p c hh", c=NCH),
                        in_=src).then_inc(s_wls[buf].h, 16)
        p.add("sync", wload)
        set_loaded = s_wls[buf].bump(16)

        for hb4 in range(4):
            hb = hs * 4 + hb4
            bank = banks[hb % 2]
            my_mm = s_ts.n + 1

            def mms(e, buf=buf, hb4=hb4, hb=hb, bank_ap=bank.ap, set_loaded=set_loaded,
                    xn_ms=xn_ms, bank_r=bank.r):
                e.wait_ge(s_wls[buf].h, set_loaded)
                if hb == 0:
                    wms(e, xn_ms)
                wms(e, bank_r)
                for i in range(NCH):
                    inst = e.matmul(bank_ap[:],
                                    WSF[buf][:, i * 512 + hb4 * 128:i * 512 + (hb4 + 1) * 128],
                                    XN[:, i * SPAN:(i + 1) * SPAN],
                                    start=(i == 0), stop=(i == NCH - 1))
                inst.then_inc(s_ts.h, 1)
            p.add("tensor", mms)
            s_ts.bump(1)
            bank.w = (s_ts, my_mm)
            wsf_last[buf] = (s_ts, my_mm)

            def evac_up(e, bank_ap=bank.ap, hb=hb, my_mm=my_mm):
                e.wait_ge(s_ts.h, my_mm)
                e.activation(AR[:, hb * SPAN:(hb + 1) * SPAN], bank_ap[:], AF.Relu,
                             scale=UPS[:, hb:hb + 1]).then_inc(s_a.h, 1)
            p.add("scalar", evac_up)
            bank.r = (s_a, s_a.bump(1))
    up_all = (s_a, s_a.n)

    last = None
    for ddb in range(NCH):
        buf = ddb % 2
        guard = wsf_last[buf]

        def wload(e, ddb=ddb, buf=buf, guard=guard):
            wms(e, guard)
            src = DTSd.rearrange("(c p) dd -> p c dd", c=64)[:, :, ddb * 128:(ddb + 1) * 128]
            e.dma_start(out=WSF[buf][:].rearrange("p (c dd) -> p c dd", c=64),
                        in_=src).then_inc(s_wls[buf].h, 16)
        p.add("sync", wload)
        set_loaded = s_wls[buf].bump(16)
        bank = banks[2 + ddb % 2]
        my_mm = s_ts.n + 1

        def mms(e, buf=buf, bank_ap=bank.ap, set_loaded=set_loaded,
                up_all=up_all, ddb=ddb, bank_r=bank.r):
            e.wait_ge(s_wls[buf].h, set_loaded)
            if ddb == 0:
                wms(e, up_all)
            wms(e, bank_r)
            for i in range(64):
                inst = e.matmul(bank_ap[:],
                                WSF[buf][:, i * 128:(i + 1) * 128],
                                AR[:, i * SPAN:(i + 1) * SPAN],
                                start=(i == 0), stop=(i == 63))
            inst.then_inc(s_ts.h, 1)
        p.add("tensor", mms)
        s_ts.bump(1)
        bank.w = (s_ts, my_mm)
        wsf_last[buf] = (s_ts, my_mm)

        def evac_dn(e, bank_ap=bank.ap, ddb=ddb, sp=sp, my_mm=my_mm):
            e.wait_ge(s_ts.h, my_mm)
            e.scalar_tensor_tensor(
                X[:, ddb * TPC + sp * SPAN:ddb * TPC + (sp + 1) * SPAN],
                bank_ap[:], DSOS[:, ddb:ddb + 1],
                X[:, ddb * TPC + sp * SPAN:ddb * TPC + (sp + 1) * SPAN],
                AluOpType.mult, AluOpType.add).then_inc(s_v.h, 1)
        p.add("vector", evac_dn)
        last = (s_v, s_v.bump(1))
        bank.r = last
    return last


def emit_proj_to_x(ctx, src_off, Wd, bias_ap, ready_ms, name, residual=True):
    (p, banks, sems, X, XN, AR, WS, ONESC, ONESR, EPS1, STATR, ABrow, ABC, TMP,
     XBF2, SQ2, UPS, DSOS, UTSd, DTSd, mm_done, ws_last) = ctx['core']
    s_v, s_a, s_ts, ch_v, ch_a = sems['v'], sems['a'], sems['ts'], sems['chv'], sems['cha']
    s_wl = [sems['wl0'], sems['wl1']]
    s_wls = [sems['wls0'], sems['wls1']]
    WSF = ctx['WSF']
    ws_last = ctx.setdefault('wsf_last', {0: None, 1: None})
    last = None
    for ec in range(NCH):
        buf = ec % 2
        guard = ws_last[buf]

        def wload(e, Wd=Wd, ec=ec, buf=buf, guard=guard):
            wms(e, guard)
            src = Wd.rearrange("(c p) ee -> p c ee", c=NCH)[:, :, ec * 128:(ec + 1) * 128]
            e.dma_start(out=WSF[buf][:, 0:NCH * 128].rearrange("p (c ee) -> p c ee", c=NCH),
                        in_=src).then_inc(s_wl[buf].h, 16)
        p.add("gpsimd", wload)
        set_loaded = s_wl[buf].bump(16)

        for tsp in range(2):
            j = ec * 2 + tsp
            bank = banks[j % 2]
            my_mm = s_ts.n + 1

            def mms(e, buf=buf, tsp=tsp, bank_ap=bank.ap, src_off=src_off,
                    set_loaded=set_loaded, j=j, ready_ms=ready_ms, bank_r=bank.r):
                e.wait_ge(s_wl[buf].h, set_loaded)
                if j == 0:
                    wms(e, ready_ms)
                wms(e, bank_r)
                for i in range(NCH):
                    inst = e.matmul(bank_ap[:],
                                    WSF[buf][:, i * 128:(i + 1) * 128],
                                    AR[:, src_off + i * TPC + tsp * SPAN:
                                       src_off + i * TPC + (tsp + 1) * SPAN],
                                    start=(i == 0), stop=(i == NCH - 1))
                inst.then_inc(s_ts.h, 1)
            p.add("tensor", mms)
            s_ts.bump(1)
            bank.w = (s_ts, my_mm)
            mm_done[(name, ec)] = my_mm
            ws_last[buf] = (s_ts, my_mm)

            def evac(e, bank_ap=bank.ap, ec=ec, tsp=tsp, my_mm=my_mm, bias_ap=bias_ap):
                e.wait_ge(s_ts.h, my_mm)
                e.scalar_tensor_tensor(
                    X[:, ec * TPC + tsp * SPAN:ec * TPC + (tsp + 1) * SPAN],
                    bank_ap[:], bias_ap[:, ec:ec + 1],
                    X[:, ec * TPC + tsp * SPAN:ec * TPC + (tsp + 1) * SPAN],
                    AluOpType.add, AluOpType.add).then_inc(s_v.h, 1)
            p.add("vector", evac)
            last = (s_v, s_v.bump(1))
            bank.r = last
    return last



def build_l2(upto="full"):
    MASKARR, MASK_OFFS = masks_and_offs()
    MASK_COLS = MASKARR.shape[1]
    nsites = site_count()

    nc = bass.Bass("TRN2", target_bir_lowering=False)
    dram = lambda n, shape, dt, kind="ExternalInput": nc.dram_tensor(n, shape, dt, kind=kind).ap()
    XTd = dram("XT", [D, TPC], F32)
    QTd = dram("QT", [D, TPC], BF16)
    KWd = [dram("KWA", [D, KWIN * 128], BF16), dram("KWB", [D, KWIN * 128], BF16)]
    VWd = [dram("VWA", [KWIN * 128, D], BF16), dram("VWB", [KWIN * 128, D], BF16)]
    BTABd = dram("BTAB", [128, 2 * nsites], F32)
    MASKSd = dram("MASKS", [128, MASK_COLS], BF16)
    Wod = dram("Wo", [D, D], F32)
    boTd = dram("boT", [128, NCH], F32)
    UTSd = dram("UTS", [D, DH], BF16)
    DTSd = dram("DTS", [DH, D], BF16)
    upsTd = dram("upsT", [128, DH // 128], F32)
    dsTd = dram("dsT", [128, NCH], F32)
    oscd = dram("osc", [128, 1], F32)
    bWTd = [dram(n, [D, D], F32) for n in ("bWkT", "bWvT", "bWrT")]
    tdTd = dram("tdT", [128, NH], F32)
    tfTd = dram("tfT", [128, NH], F32)
    find_d = dram("FIRSTIND", [128, 1], F32)

    X2Td = dram("X2T", [D, TPC], F32, "ExternalOutput")
    NUMLd = dram("NUML", [D, TPC], F32, "ExternalOutput")
    DENLd = dram("DENL", [D, TPC], F32, "ExternalOutput")
    SIGd = dram("SIG", [D, TPC], F32, "ExternalOutput")
    SUMSd = dram("SUMS", [128, NCH * 4], F32, "ExternalOutput")

    es = ExitStack()
    sb = lambda name, shape, dt: es.enter_context(nc.sbuf_tensor(name, shape, dt))

    X = sb("X", [128, NCH * TPC], F32)
    XN = sb("XN", [128, NCH * SPAN], BF16)
    AR = sb("AR", [128, 2 * NCH * TPC], BF16)
    WS = None
    BIA2 = sb("BIA2", [128, NCH], F32)
    ONESC = sb("ONESC", [128, 1], BF16)
    ONESR = sb("ONESR", [1, 128], F32)
    EPS1 = sb("EPS1", [1, 1], F32)
    STATR = sb("STATR", [1, 2 * SPAN], F32)
    ABrow = sb("ABrow", [1, 2 * SPAN], F32)
    ABC = sb("ABC", [128, 2 * SPAN], F32)
    TMP = sb("TMP", [128, SPAN], F32)
    XBF2 = [sb(f"XBF2{i}", [128, SPAN], BF16) for i in range(2)]
    SQ2 = [sb(f"SQ2{i}", [128, SPAN], BF16) for i in range(2)]
    UPS = sb("UPS", [128, DH // 128], F32)
    DST = sb("DST", [128, NCH], F32)
    OSC = sb("OSC", [128, 1], F32)
    DSOS = sb("DSOS", [128, NCH], F32)
    TD = sb("TD", [128, NH], F32)
    TF = sb("TF", [128, NH], F32)
    IND = sb("IND", [128, 1], F32)
    E1 = sb("E1", [128, NH], F32)
    DEC = sb("DEC", [128, NH], F32)
    FEFF = sb("FEFF", [128, NH], F32)
    SUMS = sb("SUMSs", [128, NCH * 4], F32)

    es_attn = ExitStack()
    sba = lambda name, shape, dt: es_attn.enter_context(nc.sbuf_tensor(name, shape, dt))
    KB = [sba(f"KB{i}", [128, 3 * KBATCH * 128], BF16) for i in range(2)]
    VB = [sba(f"VB{i}", [128, 3 * KBATCH * 128], BF16) for i in range(2)]
    PT = [sba(f"PT{i}", [128, SPAN], BF16) for i in range(2)]
    BTAB = sba("BTABs", [128, nsites], F32)
    MASKS = sba("MASKSs", [128, MASK_COLS], BF16)
    RL = sba("RL", [1, SPAN], F32)
    RLB = sba("RLB", [128, SPAN], F32)

    PS = [es.enter_context(nc.psum_tensor(f"PS{i}", [128, SPAN], F32)) for i in range(8)]
    banks = [Bank(t) for t in PS]

    sems = {k: Sem(es.enter_context(nc.semaphore("s_" + k)))
            for k in ("ld", "ld2", "v", "a", "ts", "tp", "g", "kv0", "kv1", "wl0", "wl1",
                      "wls0", "wls1", "st0", "st1", "stx", "chv", "cha",
                      "w30", "w31", "w32", "w33", "w34", "w35")}
    s_ld, s_v, s_a, s_g = sems["ld"], sems["v"], sems["a"], sems["g"]
    s_ld2 = sems["ld2"]
    s_ts, s_tp = sems["ts"], sems["tp"]
    s_kv = [sems["kv0"], sems["kv1"]]
    ch_v, ch_a = sems["chv"], sems["cha"]
    s_wl = [sems["wl0"], sems["wl1"]]
    s_wls = [sems["wls0"], sems["wls1"]]
    s_st = [sems["st0"], sems["st1"]]
    s_stx = sems["stx"]
    s_w3 = [sems["w3" + str(i)] for i in range(6)]

    p = Plan()
    QT_OFF = 0
    OT_OFF = NCH * TPC

    def loads(e):
        e.dma_start(out=X[:].rearrange("p (c t) -> p c t", c=NCH),
                    in_=XTd.rearrange("(c p) t -> p c t", c=NCH)).then_inc(s_ld.h, 16)
        e.dma_start(out=AR[:, QT_OFF:QT_OFF + NCH * TPC].rearrange("p (c t) -> p c t", c=NCH),
                    in_=QTd.rearrange("(c p) t -> p c t", c=NCH)).then_inc(s_ld.h, 16)
        e.dma_start(out=BTAB[:], in_=BTABd[:, 0:nsites]).then_inc(s_ld.h, 16)
        e.dma_start(out=MASKS[:], in_=MASKSd).then_inc(s_ld.h, 16)
        e.dma_start(out=BIA2[:], in_=boTd).then_inc(s_ld.h, 16)
        e.dma_start(out=UPS[:], in_=upsTd).then_inc(s_ld.h, 16)
        e.dma_start(out=DST[:], in_=dsTd).then_inc(s_ld.h, 16)
        e.dma_start(out=OSC[:], in_=oscd).then_inc(s_ld.h, 16)
        e.dma_start(out=TD[:], in_=tdTd).then_inc(s_ld.h, 16)
        e.dma_start(out=TF[:], in_=tfTd).then_inc(s_ld.h, 16)
        e.dma_start(out=IND[:], in_=find_d).then_inc(s_ld.h, 16)
    p.add("sync", loads)
    loads_done = s_ld.bump(16 * 11)

    def consts(e):
        e.memset(ONESC[:], 1.0)
        e.memset(ONESR[:], 1.0)
        e.memset(EPS1[:], LN_EPS).then_inc(s_g.h, 1)
    p.add("gpsimd", consts)
    consts_done = s_g.bump()

    for eng in ("tensor", "scalar", "vector"):
        def gate(e, _ld=loads_done, _g=consts_done):
            e.wait_ge(s_ld.h, _ld)
            e.wait_ge(s_g.h, _g)
        p.add(eng, gate)

    site_idx = 0
    btab_reload_ms = None
    kv_parity = 0
    kv_guard = {0: 0, 1: 0}
    for sp in range(2):
        if sp == 1:
            exp_a_done = s_a.n

            def btab_reload(e, exp_a_done=exp_a_done):
                e.wait_ge(s_a.h, exp_a_done)
                e.dma_start(out=BTAB[:], in_=BTABd[:, nsites:2 * nsites]).then_inc(s_ld2.h, 16)
            p.add("sync", btab_reload)
            btab_reload_ms = (s_ld2, s_ld2.bump(16))
        for gi, (gh, batches) in enumerate(attention_sites(sp)):
            oh_banks = {h: banks[j] for j, h in enumerate(gh)}
            l_banks = {h: banks[3 + j] for j, h in enumerate(gh)}
            sping = [banks[6], banks[7]]
            first_pv = {h: True for h in gh}
            first_l = {h: True for h in gh}
            flat = []
            for bi, (kb_list, sites) in enumerate(batches):
                kb_lo = kb_list[0]
                nkb = kb_list[-1] - kb_lo + 1
                kbuf = kv_parity % 2
                kv_parity += 1
                flat.append(("BATCHLOAD", kb_lo, nkb, kbuf))
                kv_done_plan = s_kv[kbuf].n + 32
                for st in sites:
                    flat.append((st, kb_lo, nkb, kbuf, kv_done_plan))
                s_kv[kbuf].bump(32)
                flat.append(("ENDBATCH", kbuf))

            pend = None
            for item in flat:
                if item[0] == "ENDBATCH":
                    kv_guard[item[1]] = s_tp.n + 1
                    continue
                if item[0] == "BATCHLOAD":
                    _, kb_lo, nkb, kbuf = item
                    guard_tp = kv_guard[kbuf]

                    def kvload(e, sp=sp, gh=gh, kb_lo=kb_lo, nkb=nkb, guard_tp=guard_tp,
                               kbuf=kbuf):
                        if guard_tp:
                            e.wait_ge(s_tp.h, guard_tp)
                        h0, nh = gh[0], len(gh)
                        src_k = KWd[sp][128 * h0:128 * (h0 + nh), 128 * kb_lo:128 * (kb_lo + nkb)]
                        e.dma_start(out=KB[kbuf][:, 0:nh * nkb * 128]
                                    .rearrange("p (h k) -> p h k", h=nh),
                                    in_=src_k.rearrange("(h p) k -> p h k", h=nh)
                                    ).then_inc(s_kv[kbuf].h, 16)
                        src_v = VWd[sp][128 * kb_lo:(128 * (kb_lo + nkb)), 128 * h0:128 * (h0 + nh)]
                        e.dma_start(out=VB[kbuf][:, 0:nkb * nh * 128]
                                    .rearrange("p (k hd) -> p k hd", k=nkb),
                                    in_=src_v.rearrange("(k p) hd -> p k hd", k=nkb)
                                    ).then_inc(s_kv[kbuf].h, 16)
                    p.add("sync", kvload)
                    continue
                st, kb_lo, nkb, kbuf, kv_done = item
                h, q0, W, kb = st["h"], st["q0"], st["W"], st["kb"]
                hi = gh.index(h)
                kbi = kb - kb_lo
                pp = site_idx % 2
                ob, lb = oh_banks[h], l_banks[h]
                sb_ = sping[pp]
                col = site_idx % nsites
                my_score = s_ts.n + 1
                my_exp = s_a.n + 1
                is_first_pv = first_pv[h]; first_pv[h] = False
                is_first_l = first_l[h]; first_l[h] = False
                mask_key = st["mask_key"]

                def score(e, hi=hi, kbi=kbi, q0=q0, W=W, sp=sp, h=h, nkb=nkb,
                          kv_done=kv_done, sb_ap=sb_.ap, bank_r=sb_.r, kbuf=kbuf):
                    e.wait_ge(s_kv[kbuf].h, kv_done)
                    wms(e, bank_r)
                    e.matmul(sb_ap[:, 0:W],
                             KB[kbuf][:, (hi * nkb + kbi) * 128:(hi * nkb + kbi + 1) * 128],
                             AR[:, QT_OFF + h * TPC + sp * SPAN + q0:
                                QT_OFF + h * TPC + sp * SPAN + q0 + W],
                             start=True, stop=True).then_inc(s_ts.h, 1)
                p.add("tensor", score)
                s_ts.bump(1)
                sb_.w = (s_ts, my_score)

                if pend is not None:
                    p.add("tensor", pend)
                    s_tp.bump(1)
                    pend_banks[0].w = (s_tp, s_tp.n)
                    pend_banks[1].w = (s_tp, s_tp.n)

                pt_guard = (s_tp.n - 1) if site_idx >= 2 else 0

                def expf(e, pp=pp, W=W, col=col, my_score=my_score, sb_ap=sb_.ap,
                         pt_guard=pt_guard, btab_ms=btab_reload_ms if sp == 1 else None,
                         first_of_span=(site_idx % nsites == 0)):
                    if btab_ms is not None and first_of_span:
                        wms(e, btab_ms)
                    e.wait_ge(s_ts.h, my_score)
                    if pt_guard > 0:
                        e.wait_ge(s_tp.h, pt_guard)
                    e.activation(PT[pp][:, 0:W], sb_ap[:, 0:W], AF.Exp,
                                 bias=BTAB[:, col:col + 1], scale=1.0).then_inc(s_a.h, 1)
                p.add("scalar", expf)
                s_a.bump(1)
                sb_.r = (s_a, my_exp)

                mask_ms = None
                if mask_key is not None:
                    moff, mw = MASK_OFFS[mask_key]

                    def maskf(e, pp=pp, W=W, moff=moff, my_exp=my_exp):
                        e.wait_ge(s_a.h, my_exp)
                        e.tensor_tensor(PT[pp][:, 0:W], PT[pp][:, 0:W],
                                        MASKS[:, moff:moff + W],
                                        AluOpType.mult).then_inc(s_v.h, 1)
                    p.add("vector", maskf)
                    mask_ms = (s_v, s_v.bump(1))

                def pvl(e, pp=pp, hi=hi, kbi=kbi, q0=q0, W=W, nh=len(gh),
                        ob_ap=ob.ap, lb_ap=lb.ap, my_exp=my_exp, mask_ms=mask_ms,
                        is_first_pv=is_first_pv, is_first_l=is_first_l,
                        ob_r=ob.r, lb_r=lb.r, kbuf=kbuf):
                    if mask_ms is not None:
                        wms(e, mask_ms)
                    else:
                        e.wait_ge(s_a.h, my_exp)
                    if is_first_pv:
                        wms(e, ob_r)
                    if is_first_l:
                        wms(e, lb_r)
                    e.matmul(ob_ap[:, q0:q0 + W],
                             VB[kbuf][:, (kbi * nh + hi) * 128:(kbi * nh + hi + 1) * 128],
                             PT[pp][:, 0:W], start=is_first_pv, stop=False,
                             skip_group_check=True)
                    e.matmul(lb_ap[0:1, q0:q0 + W], ONESC[:], PT[pp][:, 0:W],
                             start=is_first_l, stop=False,
                             skip_group_check=True).then_inc(s_tp.h, 1)
                pend = pvl
                pend_banks = (ob, lb)
                site_idx += 1
            if pend is not None:
                p.add("tensor", pend)
                s_tp.bump(1)
                pend_banks[0].w = (s_tp, s_tp.n)
                pend_banks[1].w = (s_tp, s_tp.n)
                pend = None
            prev_bc = None
            for j, h in enumerate(gh):
                ob, lb = oh_banks[h], l_banks[h]
                bb = sping[0]
                c1 = ch_a.bump(1)

                def fin_rl(e, lb_ap=lb.ap, lb_w=lb.w, ob_w=ob.w, c1=c1, prev_bc=prev_bc):
                    wms(e, lb_w)
                    wms(e, ob_w)
                    wms(e, prev_bc)
                    e.activation(RL[0:1, :], lb_ap[0:1, :], AF.Ln).then_inc(ch_a.h, 1)
                    e.wait_ge(ch_a.h, c1)
                    e.activation(RL[0:1, :], RL[0:1, :], AF.Exp, scale=-1.0).then_inc(s_a.h, 1)
                p.add("scalar", fin_rl)
                rl_done = s_a.bump(1)
                lb.r = (s_a, rl_done)

                def fin_bc(e, bb_ap=bb.ap, rl_done=rl_done, bank_r=bb.r):
                    e.wait_ge(s_a.h, rl_done)
                    wms(e, bank_r)
                    e.matmul(bb_ap[:], ONESR[:], RL[0:1, :], start=True,
                             stop=True).then_inc(s_ts.h, 1)
                p.add("tensor", fin_bc)
                bc_done = s_ts.bump(1)
                bb.w = (s_ts, bc_done)
                prev_bc = (s_ts, bc_done)
                c2 = ch_v.bump(1)
                prev_ot = (s_v, s_v.n)

                def fin_ev(e, bb_ap=bb.ap, ob_ap=ob.ap, h=h, sp=sp, bc_done=bc_done,
                           c2=c2, prev_ot=prev_ot):
                    e.wait_ge(s_ts.h, bc_done)
                    wms(e, prev_ot)
                    e.tensor_copy(RLB[:], bb_ap[:]).then_inc(ch_v.h, 1)
                    e.wait_ge(ch_v.h, c2)
                    e.tensor_tensor(AR[:, OT_OFF + h * TPC + sp * SPAN:
                                       OT_OFF + h * TPC + (sp + 1) * SPAN],
                                    ob_ap[:], RLB[:], AluOpType.mult).then_inc(s_v.h, 1)
                p.add("vector", fin_ev)
                ot_done = s_v.bump(1)
                ob.r = (s_v, ot_done)
                bb.r = (s_v, ot_done)
    attn_done_ms = (s_v, s_v.n)
    es_attn.close()
    es_ffn = ExitStack()
    WSF = [es_ffn.enter_context(nc.sbuf_tensor(f"WSF{i}", [128, 8192], BF16)) for i in range(2)]

    mm_done = {}
    ws_last = {0: None, 1: None}
    ctx = {'core': (p, banks, sems, X, XN, AR, WS, ONESC, ONESR, EPS1, STATR, ABrow, ABC, TMP,
                    XBF2, SQ2, UPS, DSOS, UTSd, DTSd, mm_done, ws_last),
           'WSF': WSF,
           'wsf_last': {0: attn_done_ms, 1: attn_done_ms}}

    x1_ms = emit_proj_to_x(ctx, OT_OFF, Wod, BIA2, attn_done_ms, "woproj")

    cdsos = ch_v.bump(1)

    def dsos_prep(e, cdsos=cdsos):
        e.tensor_scalar(DSOS[:], DST[:], OSC[:, 0:1], None, AluOpType.mult).then_inc(ch_v.h, 1)
    p.add("vector", dsos_prep)
    dsos_ms = (ch_v, cdsos)

    if upto in ("ffn", "full"):
        xn_a = emit_ln2(ctx, 0, x1_ms)
        ffa = emit_ffn_span(ctx, 0, xn_a, "f0")
        xn_b = emit_ln2(ctx, 1, x1_ms)
        ffb = emit_ffn_span(ctx, 1, xn_b, "f1")
        x2_ms = ffb
    else:
        x2_ms = x1_ms

    def store_x(e, x2_ms=x2_ms):
        wms(e, x2_ms)
        e.dma_start(out=X2Td.rearrange("(c p) t -> p c t", c=NCH),
                    in_=X[:].rearrange("p (c t) -> p c t", c=NCH)).then_inc(s_st[0].h, 16)
    p.add("sync", store_x)
    s_st[0].bump(16)

    es_ffn.close()
    if upto == "full":
        ca1, ca2 = ch_a.bump(1), ch_a.bump(1)

        def decay_prep(e, ca1=ca1, ca2=ca2):
            e.activation(E1[:], TD[:], AF.Exp).then_inc(ch_a.h, 1)
            e.wait_ge(ch_a.h, ca1)
            e.activation(DEC[:], E1[:], AF.Exp, scale=-1.0).then_inc(ch_a.h, 1)
            e.wait_ge(ch_a.h, ca2)
            e.activation(FEFF[:], TF[:], AF.Exp, scale=IND[:, 0:1]).then_inc(s_a.h, 1)
        p.add("scalar", decay_prep)
        decay_ms = (s_a, s_a.bump(1))

        es_b = ExitStack()
        sbb = lambda name, shape, dt: es_b.enter_context(nc.sbuf_tensor(name, shape, dt))
        WT = sbb("WT", [128, SPAN], F32)
        AA = sbb("AA", [128, SPAN], F32)
        NUMT = sbb("NUMT", [128, SPAN], F32)
        DENT = sbb("DENT", [128, SPAN], F32)
        SIGT = sbb("SIGT", [128, SPAN], F32)
        DECB = sbb("DECB", [128, SPAN], F32)
        ONES32 = sbb("ONES32", [128, SPAN], F32)
        WB3 = [sbb(f"WB3{i}", [128, NCH * 128], BF16) for i in range(6)]

        def consts2(e, wl0=s_wl[0].n, wl1=s_wl[1].n, wls0=s_wls[0].n, wls1=s_wls[1].n,
                    ffn_mm=s_ts.n):
            e.wait_ge(s_wl[0].h, wl0)
            e.wait_ge(s_wl[1].h, wl1)
            e.wait_ge(s_wls[0].h, wls0)
            e.wait_ge(s_wls[1].h, wls1)
            e.wait_ge(s_ts.h, ffn_mm)
            e.memset(ONES32[:], 1.0).then_inc(s_g.h, 1)
        p.add("gpsimd", consts2)
        consts2_done = s_g.bump()

        for sp in range(2):
            xnb_ms = emit_ln2(ctx, sp, x2_ms)
            for cc in range(NCH):
                kbank = banks[0 + 3 * (cc % 2)]
                vbank = banks[1 + 3 * (cc % 2)]
                rbank = banks[2 + 3 * (cc % 2)]
                step = sp * NCH + cc
                my_mm = s_ts.n + 3
                for wj in range(3):
                    bslot = wj * 2 + (cc % 2)
                    guard = mm_done.get(("b3", step - 2, wj))

                    def wload(e, wj=wj, cc=cc, bslot=bslot, guard=guard, step=step,
                              c2=consts2_done):
                        if step < 2:
                            e.wait_ge(s_g.h, c2)
                        if guard:
                            e.wait_ge(s_ts.h, guard)
                        src = bWTd[wj].rearrange("(c p) cd -> p c cd", c=NCH)[
                            :, :, cc * 128:(cc + 1) * 128]
                        e.dma_start(out=WB3[bslot][:].rearrange("p (c cd) -> p c cd", c=NCH),
                                    in_=src).then_inc(s_w3[bslot].h, 16)
                    p.add("gpsimd", wload)
                    sl = (s_w3[bslot], s_w3[bslot].bump(16))

                    def mmh(e, wj=wj, cc=cc, bslot=bslot, sl=sl, xnb_ms=xnb_ms,
                            ap_=(kbank.ap, vbank.ap, rbank.ap)[wj],
                            rd=(kbank.r, vbank.r, rbank.r)[wj]):
                        e.wait_ge(sl[0].h, sl[1])
                        if wj == 0 and cc == 0:
                            wms(e, xnb_ms)
                        wms(e, rd)
                        for i in range(NCH):
                            inst = e.matmul(ap_[:],
                                            WB3[bslot][:, i * 128:(i + 1) * 128],
                                            XN[:, i * SPAN:(i + 1) * SPAN],
                                            start=(i == 0), stop=(i == NCH - 1))
                        inst.then_inc(s_ts.h, 1)
                    p.add("tensor", mmh)
                    s_ts.bump(1)
                    mm_done[("b3", step, wj)] = s_ts.n
                kbank.w = vbank.w = rbank.w = (s_ts, my_mm)

                def sig_f(e, rap=rbank.ap, my_mm=my_mm, step=step):
                    e.wait_ge(s_ts.h, my_mm)
                    if step:
                        e.wait_ge(s_stx.h, 48 * step)
                    e.activation(SIGT[:], rap[:], AF.Sigmoid).then_inc(s_a.h, 1)
                p.add("scalar", sig_f)
                sig_ms = (s_a, s_a.bump(1))
                rbank.r = sig_ms

                clip_v = s_v.bump(1)
                cbase = ch_v.n
                exp_done_val = s_a.n + 1

                def kvchain(e, kap=kbank.ap, vap=vbank.ap, my_mm=my_mm, cc=cc, sp=sp,
                            cbase=cbase, decay_ms=decay_ms, step=step,
                            exp_done=exp_done_val, c2_done=consts2_done):
                    e.wait_ge(s_ts.h, my_mm)
                    wms(e, decay_ms)
                    e.wait_ge(s_g.h, c2_done)
                    if step:
                        e.wait_ge(s_stx.h, 48 * step)
                    e.tensor_scalar(WT[:], kap[:], 10.0, -10.0, AluOpType.min,
                                    AluOpType.max).then_inc(s_v.h, 1)
                    e.wait_ge(s_a.h, exp_done)
                    e.tensor_tensor(AA[:], WT[:], vap[:], AluOpType.mult).then_inc(ch_v.h, 1)
                    e.tensor_scalar(DECB[:], ONES32[:], DEC[:, cc:cc + 1], None,
                                    AluOpType.mult).then_inc(ch_v.h, 1)
                    e.wait_ge(ch_v.h, cbase + 2)
                    e.tensor_tensor_scan(NUMT[:], DECB[:], AA[:], 0.0, AluOpType.mult,
                                         AluOpType.add).then_inc(ch_v.h, 1)
                    e.tensor_tensor_scan(DENT[:], DECB[:], WT[:], 0.0, AluOpType.mult,
                                         AluOpType.add).then_inc(ch_v.h, 1)
                    e.wait_ge(ch_v.h, cbase + 4)
                    e.tensor_copy(SUMS[:, cc * 4 + 2 * sp:cc * 4 + 2 * sp + 1],
                                  NUMT[:, SPAN - 1:SPAN]).then_inc(ch_v.h, 1)
                    e.tensor_copy(SUMS[:, cc * 4 + 2 * sp + 1:cc * 4 + 2 * sp + 2],
                                  DENT[:, SPAN - 1:SPAN]).then_inc(s_v.h, 1)
                p.add("vector", kvchain)
                ch_v.bump(5)
                scan_ms = (s_v, s_v.bump(1))
                kbank.r = scan_ms
                vbank.r = scan_ms

                ca = ch_a.bump(1)

                def exp_w(e, clip_v=clip_v, cc=cc, sp=sp, ca=ca):
                    e.wait_ge(s_v.h, clip_v)
                    e.activation(WT[:], WT[:], AF.Exp).then_inc(ch_a.h, 1)
                    e.wait_ge(ch_a.h, ca)
                    e.activation(WT[:, 0:1], WT[:, 0:1], AF.Copy,
                                 scale=FEFF[:, cc:cc + 1] if sp == 0 else 1.0).then_inc(s_a.h, 1)
                p.add("scalar", exp_w)
                s_a.bump(1)

                def stores(e, cc=cc, sp=sp, scan_ms=scan_ms, sig_ms=sig_ms):
                    wms(e, scan_ms)
                    wms(e, sig_ms)
                    e.dma_start(out=NUMLd[cc * 128:(cc + 1) * 128, sp * SPAN:(sp + 1) * SPAN],
                                in_=NUMT[:]).then_inc(s_stx.h, 16)
                    e.dma_start(out=DENLd[cc * 128:(cc + 1) * 128, sp * SPAN:(sp + 1) * SPAN],
                                in_=DENT[:]).then_inc(s_stx.h, 16)
                    e.dma_start(out=SIGd[cc * 128:(cc + 1) * 128, sp * SPAN:(sp + 1) * SPAN],
                                in_=SIGT[:]).then_inc(s_stx.h, 16)
                p.add("sync", stores)
                s_stx.bump(48)

        def store_sums(e, n=s_v.n):
            e.wait_ge(s_v.h, n)
            e.dma_start(out=SUMSd, in_=SUMS[:]).then_inc(s_st[1].h, 16)
        p.add("sync", store_sums)
        s_st[1].bump(16)
        es_b.close()

    def fin(e, a=s_st[0].n, b=s_st[1].n, c=s_stx.n):
        e.wait_ge(s_st[0].h, a)
        if b:
            e.wait_ge(s_st[1].h, b)
        if c:
            e.wait_ge(s_stx.h, c)
    p.add("sync", fin)

    p.emit(es.enter_context(nc.Block()))
    es.close()
    return nc


def stage_l2(inputs, l1_outs, core):
    a0, b0 = span_starts(core)
    toks = np.r_[a0:a0 + SPAN, b0:b0 + SPAN]
    x = np.asarray(inputs["L0_attn_a_x"])[0]
    xTc = np.ascontiguousarray(x[toks, :].T)

    kT_full = np.zeros((D, T), BF)
    vT_full = np.zeros((D, T), BF)
    for c in range(NCORES):
        ca, cb = span_starts(c)
        kT_full[:, ca:ca + SPAN] = l1_outs[c]["kT"][:, 0:SPAN]
        kT_full[:, cb:cb + SPAN] = l1_outs[c]["kT"][:, SPAN:2 * SPAN]
        vT_full[:, ca:ca + SPAN] = l1_outs[c]["vT"][:, 0:SPAN]
        vT_full[:, cb:cb + SPAN] = l1_outs[c]["vT"][:, SPAN:2 * SPAN]
    v_full = np.ascontiguousarray(vT_full.T)

    def kwin(sp):
        q0 = a0 if sp == 0 else b0
        blk0 = q0 // 128
        KW = np.zeros((D, KWIN * 128), BF)
        VWm = np.zeros((KWIN * 128, D), BF)
        for i in range(KWIN):
            g = blk0 + (i - 64)
            if 0 <= g < T // 128:
                KW[:, 128 * i:128 * (i + 1)] = kT_full[:, 128 * g:128 * (g + 1)]
                VWm[128 * i:128 * (i + 1), :] = v_full[128 * g:128 * (g + 1), :]
        return KW, VWm

    KWA, VWA = kwin(0)
    KWB, VWB = kwin(1)
    maskarr, _ = masks_and_offs()

    def bT(b):
        return np.ascontiguousarray(np.asarray(b).reshape(NCH, 128).T.astype(np.float32))

    U = np.asarray(inputs["L0_ffn_up_weight"])
    Dw = np.asarray(inputs["L0_ffn_down_weight"])
    return {
        "XT": xTc,
        "QT": l1_outs[core]["qT"],
        "KWA": KWA, "KWB": KWB, "VWA": VWA, "VWB": VWB,
        "BTAB": build_bias_table(core),
        "MASKS": maskarr,
        "Wo": np.asarray(inputs["L0_attn_a_Wo"]),
        "boT": bT(inputs["L0_attn_a_bo"]),
        "UTS": l1_outs[core]["UTS"],
        "DTS": l1_outs[core]["DTS"],
        "upsT": np.ascontiguousarray(np.asarray(inputs["L0_ffn_up_scale"])
                                     .reshape(DH // 128, 128).T.astype(np.float32)),
        "dsT": bT(inputs["L0_ffn_down_scale"]),
        "osc": np.broadcast_to(np.asarray(inputs["L0_ffn_output_scale"]).reshape(1, 1),
                               (128, 1)).copy(),
        "bWkT": np.ascontiguousarray(np.asarray(inputs["L0_attn_b_Wk"]).T),
        "bWvT": np.ascontiguousarray(np.asarray(inputs["L0_attn_b_Wv"]).T),
        "bWrT": np.ascontiguousarray(np.asarray(inputs["L0_attn_b_Wr"]).T),
        "tdT": np.ascontiguousarray(np.asarray(inputs["L0_attn_b_time_decay"]).T.astype(np.float32)),
        "tfT": np.ascontiguousarray(np.asarray(inputs["L0_attn_b_time_first"]).T.astype(np.float32)),
        "FIRSTIND": np.full((128, 1), 1.0 if core == 0 else 0.0, np.float32),
    }



def build_l3():
    nc = bass.Bass("TRN2", target_bir_lowering=False)
    dram = lambda n, shape, dt, kind="ExternalInput": nc.dram_tensor(n, shape, dt, kind=kind).ap()
    X2Td = dram("X2T", [D, TPC], F32)
    NUMLd = dram("NUML", [D, TPC], F32)
    DENLd = dram("DENL", [D, TPC], F32)
    SIGd = dram("SIG", [D, TPC], F32)
    SUMNd = dram("SUMN", [128, NCH * 16], F32)
    SUMDd = dram("SUMD", [128, NCH * 16], F32)
    SELAd = dram("SELA", [128, 16], F32)
    SELBd = dram("SELB", [128, 16], F32)
    bWoTd = dram("bWoT", [D, D], F32)
    UTSd = dram("UTS", [D, DH], BF16)
    DTSd = dram("DTS", [DH, D], BF16)
    upsTd = dram("upsT", [128, DH // 128], F32)
    dsTd = dram("dsT", [128, NCH], F32)
    oscd = dram("osc", [128, 1], F32)
    tdTd = dram("tdT", [128, NH], F32)
    XOUTd = dram("XOUT", [D, TPC], F32, "ExternalOutput")

    es = ExitStack()
    sb = lambda name, shape, dt: es.enter_context(nc.sbuf_tensor(name, shape, dt))

    X = sb("X", [128, NCH * TPC], F32)
    XN = sb("XN", [128, NCH * SPAN], BF16)
    AR = sb("AR", [128, 2 * NCH * TPC], BF16)
    WS = None
    ONESC = sb("ONESC", [128, 1], BF16)
    ONESR = sb("ONESR", [1, 128], F32)
    EPS1 = sb("EPS1", [1, 1], F32)
    EPS6 = sb("EPS6", [128, 1], F32)
    STATR = sb("STATR", [1, 2 * SPAN], F32)
    ABrow = sb("ABrow", [1, 2 * SPAN], F32)
    ABC = sb("ABC", [128, 2 * SPAN], F32)
    TMP = sb("TMP", [128, SPAN], F32)
    XBF2 = [sb(f"XBF2{i}", [128, SPAN], BF16) for i in range(2)]
    SQ2 = [sb(f"SQ2{i}", [128, SPAN], BF16) for i in range(2)]
    UPS = sb("UPS", [128, DH // 128], F32)
    DST = sb("DST", [128, NCH], F32)
    OSC = sb("OSC", [128, 1], F32)
    DSOS = sb("DSOS", [128, NCH], F32)
    ZB = sb("ZB", [128, NCH], F32)
    TD = sb("TD", [128, NH], F32)
    E1 = sb("E1", [128, NH], F32)
    NE1 = sb("NE1", [128, NH], F32)
    D512 = sb("D512", [128, NH], F32)
    es_w = ExitStack()
    sbw = lambda name, shape, dt: es_w.enter_context(nc.sbuf_tensor(name, shape, dt))
    SUMN = sbw("SUMNs", [128, NCH * 16], F32)
    SUMD = sbw("SUMDs", [128, NCH * 16], F32)
    SELA = sbw("SELAs", [128, 16], F32)
    SELB = sbw("SELBs", [128, 16], F32)
    CARN = sbw("CARN", [128, NCH * 16], F32)
    CARD = sbw("CARD", [128, NCH * 16], F32)
    CAN = [sbw(f"CAN{i}", [128, NCH], F32) for i in range(2)]
    CAD = [sbw(f"CAD{i}", [128, NCH], F32) for i in range(2)]
    SELT = sbw("SELT", [128, 16], F32)
    IOTA1 = sbw("IOTA1", [128, SPAN], F32)
    POW = sbw("POW", [128, SPAN], F32)
    TMPP = sbw("TMPP", [128, SPAN], F32)
    NUMT = sbw("NUMT", [128, SPAN], F32)
    DENT = sbw("DENT", [128, SPAN], F32)
    SGT = sbw("SGT", [128, SPAN], F32)
    RECIP = sbw("RECIP", [128, SPAN], F32)
    WKVF = sbw("WKVF", [128, SPAN], F32)

    PS = [es.enter_context(nc.psum_tensor(f"PS{i}", [128, SPAN], F32)) for i in range(8)]
    banks = [Bank(t) for t in PS]

    sems = {k: Sem(es.enter_context(nc.semaphore("s_" + k)))
            for k in ("ld", "ldx", "v", "a", "ts", "g", "wl0", "wl1", "wls0", "wls1",
                      "st0", "chv", "cha")}
    s_ld, s_v, s_a, s_g = sems["ld"], sems["v"], sems["a"], sems["g"]
    s_ldx, s_ts = sems["ldx"], sems["ts"]
    ch_v, ch_a = sems["chv"], sems["cha"]
    s_st = sems["st0"]

    p = Plan()
    mm_done = {}
    ws_last = {0: None, 1: None}
    ctx = {'core': (p, banks, sems, X, XN, AR, WS, ONESC, ONESR, EPS1, STATR, ABrow, ABC, TMP,
                    XBF2, SQ2, UPS, DSOS, UTSd, DTSd, mm_done, ws_last)}

    def loads(e):
        e.dma_start(out=X[:].rearrange("p (c t) -> p c t", c=NCH),
                    in_=X2Td.rearrange("(c p) t -> p c t", c=NCH)).then_inc(s_ld.h, 16)
        e.dma_start(out=SUMN[:], in_=SUMNd).then_inc(s_ld.h, 16)
        e.dma_start(out=SUMD[:], in_=SUMDd).then_inc(s_ld.h, 16)
        e.dma_start(out=SELA[:], in_=SELAd).then_inc(s_ld.h, 16)
        e.dma_start(out=SELB[:], in_=SELBd).then_inc(s_ld.h, 16)
        e.dma_start(out=UPS[:], in_=upsTd).then_inc(s_ld.h, 16)
        e.dma_start(out=DST[:], in_=dsTd).then_inc(s_ld.h, 16)
        e.dma_start(out=OSC[:], in_=oscd).then_inc(s_ld.h, 16)
        e.dma_start(out=TD[:], in_=tdTd).then_inc(s_ld.h, 16)
    p.add("sync", loads)
    loads_done = s_ld.bump(16 * 9)

    def consts(e):
        e.memset(ONESC[:], 1.0)
        e.memset(ONESR[:], 1.0)
        e.memset(EPS1[:], LN_EPS)
        e.memset(EPS6[:], 1e-6)
        e.memset(ZB[:], 0.0)
        e.memset(CARN[:], 0.0)
        e.memset(CARD[:], 0.0)
        e.iota(IOTA1[:], [[1, SPAN]], base=1, channel_multiplier=0,
               allow_small_or_imprecise_dtypes=True).then_inc(s_g.h, 1)
    p.add("gpsimd", consts)
    consts_done = s_g.bump()

    for eng in ("tensor", "scalar", "vector"):
        def gate(e, _ld=loads_done, _g=consts_done):
            e.wait_ge(s_ld.h, _ld)
            e.wait_ge(s_g.h, _g)
        p.add(eng, gate)

    ca1 = ch_a.bump(1)

    def decay_f(e, ca1=ca1):
        e.activation(E1[:], TD[:], AF.Exp).then_inc(ch_a.h, 1)
        e.wait_ge(ch_a.h, ca1)
        e.activation(D512[:], E1[:], AF.Exp, scale=-512.0).then_inc(s_a.h, 1)
    p.add("scalar", decay_f)
    e1_done = s_a.bump(1)
    cv1 = ch_v.bump(1)

    def ne1_f(e, e1_done=e1_done, cv1=cv1):
        e.wait_ge(s_a.h, e1_done)
        e.tensor_scalar(NE1[:], E1[:], -1.0, None, AluOpType.mult).then_inc(ch_v.h, 1)
    p.add("vector", ne1_f)
    ne1_done = (ch_v, cv1)

    carry_c0 = ch_v.n

    def carry_f(e, e1_done=e1_done, c0=carry_c0):
        e.wait_ge(s_a.h, e1_done)
        step = 0
        for s_ in range(1, 16):
            if s_ > 1:
                e.wait_ge(ch_v.h, c0 + step)
            for cc in range(NCH):
                i0 = cc * 16 + s_
                e.scalar_tensor_tensor(CARN[:, i0:i0 + 1], CARN[:, i0 - 1:i0], D512[:, cc:cc + 1],
                                       SUMN[:, i0 - 1:i0], AluOpType.mult, AluOpType.add)
                inst = e.scalar_tensor_tensor(CARD[:, i0:i0 + 1], CARD[:, i0 - 1:i0], D512[:, cc:cc + 1],
                                              SUMD[:, i0 - 1:i0], AluOpType.mult, AluOpType.add)
            step += 1
            inst.then_inc(ch_v.h, 1)
        e.wait_ge(ch_v.h, c0 + 15)
        k = c0 + 15
        for spi, SEL in ((0, SELA), (1, SELB)):
            for car, dst in ((CARN, CAN[spi]), (CARD, CAD[spi])):
                for cc in range(NCH):
                    e.wait_ge(ch_v.h, k)
                    e.tensor_tensor(SELT[:], car[:, cc * 16:(cc + 1) * 16], SEL[:],
                                    AluOpType.mult).then_inc(ch_v.h, 1)
                    k += 1
                    e.wait_ge(ch_v.h, k)
                    e.reduce_sum(dst[:, cc:cc + 1], SELT[:], axis=mybir.AxisListType.X
                                 ).then_inc(ch_v.h, 1)
                    k += 1
        e.nop().then_inc(s_v.h, 1)
    p.add("vector", carry_f)
    ch_v.bump(15 + 4 * NCH * 2)
    carry_done = s_v.bump(1)

    cdsos = ch_v.bump(1)

    def dsos_prep(e, cdsos=cdsos):
        e.tensor_scalar(DSOS[:], DST[:], OSC[:, 0:1], None, AluOpType.mult).then_inc(ch_v.h, 1)
    p.add("vector", dsos_prep)

    rw_ms = None
    prev_pow_done = 0
    prev_corr1 = None
    for cc in range(NCH):
        def pow_ts(e, cc=cc, ne1_done=ne1_done, prev_pow_done=prev_pow_done):
            wms(e, ne1_done)
            if prev_pow_done:
                e.wait_ge(s_a.h, prev_pow_done)
            e.tensor_scalar(TMPP[:], IOTA1[:], NE1[:, cc:cc + 1], None,
                            AluOpType.mult).then_inc(s_v.h, 1)
        p.add("vector", pow_ts)
        tspw = s_v.bump(1)

        def exp_pow(e, tspw=tspw, prev_corr1=prev_corr1):
            e.wait_ge(s_v.h, tspw)
            wms(e, prev_corr1)
            e.activation(POW[:], TMPP[:], AF.Exp).then_inc(s_a.h, 1)
        p.add("scalar", exp_pow)
        pow_done = s_a.bump(1)
        prev_pow_done = pow_done

        for sp in range(2):
            ld0 = s_ldx.n
            prev_use = rw_ms

            def tload(e, cc=cc, sp=sp, prev_use=prev_use):
                wms(e, prev_use)
                e.dma_start(out=NUMT[:], in_=NUMLd[cc * 128:(cc + 1) * 128,
                                               sp * SPAN:(sp + 1) * SPAN]).then_inc(s_ldx.h, 16)
                e.dma_start(out=DENT[:], in_=DENLd[cc * 128:(cc + 1) * 128,
                                               sp * SPAN:(sp + 1) * SPAN]).then_inc(s_ldx.h, 16)
                e.dma_start(out=SGT[:], in_=SIGd[cc * 128:(cc + 1) * 128,
                                              sp * SPAN:(sp + 1) * SPAN]).then_inc(s_ldx.h, 16)
            p.add("sync", tload)
            tl_done = s_ldx.bump(48)
            cfix = ch_v.bump(1)

            def correct(e, cc=cc, sp=sp, tl_done=tl_done, pow_done=pow_done,
                        carry_done=carry_done, cfix=cfix):
                e.wait_ge(s_ldx.h, tl_done)
                e.wait_ge(s_a.h, pow_done)
                e.wait_ge(s_v.h, carry_done)
                e.scalar_tensor_tensor(NUMT[:], POW[:], CAN[sp][:, cc:cc + 1], NUMT[:],
                                       AluOpType.mult, AluOpType.add).then_inc(ch_v.h, 1)
                e.scalar_tensor_tensor(DENT[:], POW[:], CAD[sp][:, cc:cc + 1], DENT[:],
                                       AluOpType.mult, AluOpType.add).then_inc(s_v.h, 1)
            p.add("vector", correct)
            corr_done = s_v.bump(1)
            ca2 = ch_a.bump(1)

            def recip_f(e, corr_done=corr_done, ca2=ca2):
                e.wait_ge(s_v.h, corr_done)
                e.activation(RECIP[:], DENT[:], AF.Ln, bias=EPS6[:, 0:1]).then_inc(ch_a.h, 1)
                e.wait_ge(ch_a.h, ca2)
                e.activation(RECIP[:], RECIP[:], AF.Exp, scale=-1.0).then_inc(s_a.h, 1)
            p.add("scalar", recip_f)
            recip_done = s_a.bump(1)
            cwv = ch_v.bump(1)

            def rwkv_f(e, cc=cc, sp=sp, recip_done=recip_done, cwv=cwv):
                e.wait_ge(s_a.h, recip_done)
                e.tensor_tensor(WKVF[:], NUMT[:], RECIP[:], AluOpType.mult).then_inc(ch_v.h, 1)
                e.wait_ge(ch_v.h, cwv)
                e.tensor_tensor(AR[:, cc * TPC + sp * SPAN:cc * TPC + (sp + 1) * SPAN],
                                WKVF[:], SGT[:], AluOpType.mult).then_inc(s_v.h, 1)
            p.add("vector", rwkv_f)
            rw_ms = (s_v, s_v.bump(1))
            if sp == 1:
                prev_corr1 = (s_v, corr_done)
    rw_all = rw_ms
    es_w.close()
    WSF = [es.enter_context(nc.sbuf_tensor(f"WSF{i}", [128, 8192], BF16)) for i in range(2)]
    ctx['WSF'] = WSF
    ctx['wsf_last'] = {0: rw_all, 1: rw_all}

    x3_ms = emit_proj_to_x(ctx, 0, bWoTd, ZB, rw_all, "bwo")
    xn_a = emit_ln2(ctx, 0, x3_ms)
    ffa = emit_ffn_span(ctx, 0, xn_a, "g0")
    xn_b = emit_ln2(ctx, 1, x3_ms)
    ffb = emit_ffn_span(ctx, 1, xn_b, "g1")

    def store_x(e, ffb=ffb):
        wms(e, ffb)
        e.dma_start(out=XOUTd.rearrange("(c p) t -> p c t", c=NCH),
                    in_=X[:].rearrange("p (c t) -> p c t", c=NCH)).then_inc(s_st.h, 16)
    p.add("sync", store_x)
    s_st.bump(16)

    def fin(e, a=s_st.n):
        e.wait_ge(s_st.h, a)
    p.add("sync", fin)

    p.emit(es.enter_context(nc.Block()))
    es.close()
    return nc


def stage_l3(inputs, l1_outs, l2_outs, core):
    a0, b0 = span_starts(core)
    SUMN = np.zeros((128, NCH * 16), np.float32)
    SUMD = np.zeros((128, NCH * 16), np.float32)
    for s_ in range(16):
        if s_ < 8:
            oc, osp = s_, 0
        else:
            oc, osp = 15 - s_, 1
        sm = l2_outs[oc]["SUMS"]
        for cc in range(NCH):
            SUMN[:, cc * 16 + s_] = sm[:, cc * 4 + 2 * osp]
            SUMD[:, cc * 16 + s_] = sm[:, cc * 4 + 2 * osp + 1]
    SELA = np.zeros((128, 16), np.float32)
    SELB = np.zeros((128, 16), np.float32)
    SELA[:, core] = 1.0
    SELB[:, 15 - core] = 1.0

    def bT(b):
        return np.ascontiguousarray(np.asarray(b).reshape(NCH, 128).T.astype(np.float32))

    return {
        "X2T": l2_outs[core]["X2T"],
        "NUML": l2_outs[core]["NUML"],
        "DENL": l2_outs[core]["DENL"],
        "SIG": l2_outs[core]["SIG"],
        "SUMN": SUMN, "SUMD": SUMD, "SELA": SELA, "SELB": SELB,
        "bWoT": np.ascontiguousarray(np.asarray(inputs["L0_attn_b_Wo"]).T),
        "UTS": l1_outs[core]["UTS"],
        "DTS": l1_outs[core]["DTS"],
        "upsT": np.ascontiguousarray(np.asarray(inputs["L0_ffn_up_scale"])
                                     .reshape(DH // 128, 128).T.astype(np.float32)),
        "dsT": bT(inputs["L0_ffn_down_scale"]),
        "osc": np.broadcast_to(np.asarray(inputs["L0_ffn_output_scale"]).reshape(1, 1),
                               (128, 1)).copy(),
        "tdT": np.ascontiguousarray(np.asarray(inputs["L0_attn_b_time_decay"]).T.astype(np.float32)),
    }


_NC_CACHE = {}


def kernel(**inputs):
    if "l1" not in _NC_CACHE:
        _NC_CACHE["l1"] = build_l1()
        _NC_CACHE["l2"] = build_l2("full")
        _NC_CACHE["l3"] = build_l3()
    cores = list(range(NCORES))
    r1 = run_bass_kernel_spmd(_NC_CACHE["l1"], stage_l1(inputs), core_ids=cores)
    l1_outs = r1.results
    r2 = run_bass_kernel_spmd(_NC_CACHE["l2"],
                              [stage_l2(inputs, l1_outs, c) for c in cores], core_ids=cores)
    l2_outs = r2.results
    r3 = run_bass_kernel_spmd(_NC_CACHE["l3"],
                              [stage_l3(inputs, l1_outs, l2_outs, c) for c in cores],
                              core_ids=cores)
    out = np.zeros((T, D), np.float32)
    for c in cores:
        a0, b0 = span_starts(c)
        xo = r3.results[c]["XOUT"]
        out[a0:a0 + SPAN] = xo[:, 0:SPAN].T
        out[b0:b0 + SPAN] = xo[:, SPAN:2 * SPAN].T
    return out[None, :, :]


# version 07 (final)
# speedup vs baseline: 1.6183x; 1.0663x over previous
import math
import numpy as np
import ml_dtypes
from contextlib import ExitStack

import concourse.bass as bass
import concourse.mybir as mybir
from concourse.alu_op_type import AluOpType
from concourse.bass_utils import run_bass_kernel_spmd

F32 = mybir.dt.float32
BF16 = mybir.dt.bfloat16
AF = mybir.ActivationFunctionType
BF = ml_dtypes.bfloat16

NCORES = 8
T = 8192
D = 2048
NH = 16
HDIM = 128
DH = 8192
NCH = 16
TPC = 1024
SPAN = 512
QSCALE = 1.0 / math.sqrt(HDIM)
LN_EPS = 1e-5

SLOPES = [2.0 ** (-0.5 * (i + 1)) for i in range(NH)]


def span_starts(c):
    return 512 * c, 7680 - 512 * c


class Sem:
    def __init__(self, handle):
        self.h = handle
        self.n = 0

    def bump(self, k=1):
        self.n += k
        return self.n


class Plan:
    ENGS = ("tensor", "vector", "scalar", "gpsimd", "sync")

    def __init__(self):
        self.ops = {e: [] for e in self.ENGS}

    def add(self, eng, fn):
        self.ops[eng].append(fn)

    def emit(self, block):
        plan = self

        @block.sync
        def _(e):
            for fn in plan.ops["sync"]:
                fn(e)

        @block.tensor
        def _(e):
            for fn in plan.ops["tensor"]:
                fn(e)

        @block.vector
        def _(e):
            for fn in plan.ops["vector"]:
                fn(e)

        @block.scalar
        def _(e):
            for fn in plan.ops["scalar"]:
                fn(e)

        @block.gpsimd
        def _(e):
            for fn in plan.ops["gpsimd"]:
                fn(e)


def chunked_colmajor(ap, nch, per):
    return ap.rearrange("(c p) t -> p (c t)", c=nch)



def emit_layernorm(p, sems, bufs, nch, t_stride, t0, span, src_X, dst_XN, dst_dtype_note=""):
    s_v, s_a, s_t, s_g, s_gx = sems["v"], sems["a"], sems["t"], sems["g"], sems["gx"]
    XBF2, SQ2 = bufs["XBF2"], bufs["SQ2"]
    ONESC, ONESR, EPS1 = bufs["ONESC"], bufs["ONESR"], bufs["EPS1"]
    STATR, AB, ABC, TMP = bufs["STATR"], bufs["AB"], bufs["ABC"], bufs["TMP"]
    PS_MU, PS_SQ, PS_AB = bufs["PS_MU"], bufs["PS_SQ"], bufs["PS_AB"]
    consts_ready = bufs["consts_ready"]
    x_ready = bufs["x_ready"]
    ch_v = sems["chv"]

    c0 = ch_v.n
    tchk0 = s_t.n

    def cast_chunks(e, c0=c0, tchk0=tchk0):
        for i in range(nch):
            if i >= 2:
                e.wait_ge(s_t.h, tchk0 + i - 1)
            e.tensor_copy(XBF2[i % 2][:], src_X[:, i * t_stride + t0:i * t_stride + t0 + span]
                          ).then_inc(ch_v.h, 1)
            e.wait_ge(ch_v.h, c0 + 2 * i + 1)
            e.tensor_tensor(SQ2[i % 2][:], XBF2[i % 2][:], XBF2[i % 2][:],
                            AluOpType.mult).then_inc(ch_v.h, 1)
    p.add("vector", cast_chunks)
    ch_v.bump(2 * nch)

    def stats_mm(e, c0=c0):
        e.wait_ge(s_g.h, consts_ready)
        for i in range(nch):
            e.wait_ge(ch_v.h, c0 + 2 * i + 2)
            e.matmul(PS_MU[0:1, :], ONESC[:], XBF2[i % 2][:],
                     start=(i == 0), stop=(i == nch - 1))
            inst = e.matmul(PS_SQ[0:1, :], ONESC[:], SQ2[i % 2][:],
                            start=(i == 0), stop=(i == nch - 1))
            inst.then_inc(s_t.h, 1)
    p.add("tensor", stats_mm)
    stats_done = s_t.bump(nch)
    bufs["stats_done_out"] = stats_done

    def stats_fin(e):
        e.wait_ge(s_t.h, stats_done)
        e.tensor_scalar(STATR[0:1, 0:span], PS_MU[0:1, :], 1.0 / D, None, AluOpType.mult)
        e.tensor_scalar(STATR[0:1, span:2 * span], PS_SQ[0:1, :], 1.0 / D, None,
                        AluOpType.mult).then_inc(ch_v.h, 1)
        e.wait_ge(ch_v.h, stats_fin.c1)
        e.tensor_tensor(AB[0:1, span:2 * span], STATR[0:1, 0:span], STATR[0:1, 0:span],
                        AluOpType.mult).then_inc(ch_v.h, 1)
        e.wait_ge(ch_v.h, stats_fin.c2)
        e.tensor_tensor(AB[0:1, span:2 * span], STATR[0:1, span:2 * span], AB[0:1, span:2 * span],
                        AluOpType.subtract).then_inc(s_v.h, 1)
    stats_fin.c1 = ch_v.bump()
    stats_fin.c2 = ch_v.bump()
    p.add("vector", stats_fin)
    var_done = s_v.bump()

    ch_a = sems["cha"]

    def rstd_f(e):
        e.wait_ge(s_v.h, var_done)
        e.activation(AB[0:1, 0:span], AB[0:1, span:2 * span], AF.Ln, bias=EPS1[0:1, 0:1],
                     scale=1.0).then_inc(ch_a.h, 1)
        e.wait_ge(ch_a.h, rstd_f.c1)
        e.activation(AB[0:1, 0:span], AB[0:1, 0:span], AF.Exp, scale=-0.5).then_inc(s_a.h, 1)
    rstd_f.c1 = ch_a.bump()
    p.add("scalar", rstd_f)
    rstd_done = s_a.bump()

    def brow_f(e):
        e.wait_ge(s_a.h, rstd_done)
        e.tensor_tensor(AB[0:1, span:2 * span], STATR[0:1, 0:span], AB[0:1, 0:span],
                        AluOpType.mult).then_inc(ch_v.h, 1)
        e.wait_ge(ch_v.h, brow_f.c1)
        e.tensor_scalar(AB[0:1, span:2 * span], AB[0:1, span:2 * span], -1.0, None,
                        AluOpType.mult).then_inc(s_v.h, 1)
    brow_f.c1 = ch_v.bump()
    p.add("vector", brow_f)
    brow_done = s_v.bump()

    def bcast_mm(e):
        e.wait_ge(s_v.h, brow_done)
        e.matmul(PS_AB[:, 0:span], ONESR[:], AB[0:1, 0:span], start=True, stop=True)
        e.matmul(PS_AB[:, span:2 * span], ONESR[:], AB[0:1, span:2 * span],
                 start=True, stop=True).then_inc(s_t.h, 1)
    p.add("tensor", bcast_mm)
    bcast_done = s_t.bump()

    def evac_bcast(e):
        e.wait_ge(s_t.h, bcast_done)
        e.tensor_copy(ABC[:, 0:span], PS_AB[:, 0:span])
        e.tensor_copy(ABC[:, span:2 * span], PS_AB[:, span:2 * span]).then_inc(s_v.h, 1)
    p.add("vector", evac_bcast)
    abc_done = s_v.bump()

    def apply_f(e):
        e.wait_ge(s_v.h, abc_done)
        for i in range(nch):
            if i:
                e.wait_ge(ch_v.h, apply_f.c0 + 2 * i)
            e.tensor_tensor(TMP[:, 0:span], src_X[:, i * t_stride + t0:i * t_stride + t0 + span],
                            ABC[:, 0:span], AluOpType.mult).then_inc(ch_v.h, 1)
            e.wait_ge(ch_v.h, apply_f.c0 + 2 * i + 1)
            e.tensor_tensor(dst_XN[:, i * t_stride + t0:i * t_stride + t0 + span], TMP[:, 0:span],
                            ABC[:, span:2 * span], AluOpType.add).then_inc(ch_v.h, 1)
    apply_f.c0 = ch_v.n
    done = ch_v.bump(2 * nch)
    p.add("vector", apply_f)
    return done



def build_l1():
    nc = bass.Bass("TRN2", target_bir_lowering=False)
    xT = nc.dram_tensor("xT", [D, TPC], F32, kind="ExternalInput").ap()
    Ws = {n: nc.dram_tensor(n, [D, D], F32, kind="ExternalInput").ap() for n in ("Wq", "Wk", "Wv")}
    bTs = {n: nc.dram_tensor(n, [128, NCH], F32, kind="ExternalInput").ap()
           for n in ("bqT", "bkT", "bvT")}
    OUTs = {n: nc.dram_tensor(n, [D, TPC], BF16, kind="ExternalOutput").ap()
            for n in ("qT", "kT", "vT")}
    UTd = nc.dram_tensor("UT", [D, DH], F32, kind="ExternalInput").ap()
    DTd = nc.dram_tensor("DT", [DH, D], F32, kind="ExternalInput").ap()
    UTSd = nc.dram_tensor("UTS", [D, DH], BF16, kind="ExternalOutput").ap()
    DTSd = nc.dram_tensor("DTS", [DH, D], BF16, kind="ExternalOutput").ap()

    es = ExitStack()
    sb = lambda name, shape, dt: es.enter_context(nc.sbuf_tensor(name, shape, dt))
    ps_ = lambda name, shape, dt=F32: es.enter_context(nc.psum_tensor(name, shape, dt))

    X = sb("X", [128, NCH * TPC], F32)
    XN = sb("XN", [128, NCH * TPC], BF16)
    XBF2 = [sb(f"XBF2{i}", [128, SPAN], BF16) for i in range(2)]
    SQ2 = [sb(f"SQ2{i}", [128, SPAN], BF16) for i in range(2)]
    ONESC = sb("ONESC", [128, 1], BF16)
    EPS1 = sb("EPS1", [1, 1], F32)
    ONESR = sb("ONESR", [1, 128], F32)
    STATR = sb("STATR", [1, 2 * SPAN], F32)
    AB = sb("AB", [1, 2 * SPAN], F32)
    ABC = sb("ABC", [128, 2 * SPAN], F32)
    TMP = sb("TMP", [128, SPAN], F32)
    WBF = [sb(f"WBF{i}", [128, NCH * 512], BF16) for i in range(2)]
    BIA = sb("BIA", [128, 3 * NCH], F32)
    OUTB = [sb(f"OUTB{i}", [128, SPAN], BF16) for i in range(2)]
    SGB = [sb(f"SGB{i}", [128, 4096], BF16) for i in range(2)]
    SGF = [sb(f"SGF{i}", [128, 4096], F32) for i in range(2)]

    PS_MU = ps_("PS_MU", [1, SPAN])
    PS_SQ = ps_("PS_SQ", [1, SPAN])
    PS_AB = ps_("PS_AB", [128, 2 * SPAN])
    PS_O = [ps_(f"PS_O{i}", [128, SPAN]) for i in range(2)]

    sems = {k: Sem(es.enter_context(nc.semaphore("s_" + k)))
            for k in ("ld", "v", "a", "t", "g", "gx", "st0", "st1", "wl0", "wl1", "ev",
                      "chv", "cha", "sg0", "sg1", "so0", "so1", "sa")}
    s_ld, s_v, s_a, s_t, s_g = sems["ld"], sems["v"], sems["a"], sems["t"], sems["g"]
    s_st = [sems["st0"], sems["st1"]]
    s_wl = [sems["wl0"], sems["wl1"]]
    s_ev, s_gx = sems["ev"], sems["gx"]
    s_sg = [sems["sg0"], sems["sg1"]]
    s_so = [sems["so0"], sems["so1"]]
    s_sa = sems["sa"]

    p = Plan()

    def loads(e):
        e.dma_start(out=X[:].rearrange("p (c t) -> p c t", c=NCH),
                    in_=xT.rearrange("(c p) t -> p c t", c=NCH)).then_inc(s_ld.h, 16)
        for j, n in enumerate(("bqT", "bkT", "bvT")):
            e.dma_start(out=BIA[:, j * NCH:(j + 1) * NCH], in_=bTs[n]).then_inc(s_ld.h, 16)
    p.add("sync", loads)
    s_ld.bump(16 * 4)
    x_loaded = 16
    bias_loaded = s_ld.n

    def consts(e):
        e.memset(ONESC[:], 1.0)
        e.memset(ONESR[:], 1.0)
        e.memset(EPS1[:], LN_EPS).then_inc(s_g.h, 1)
    p.add("gpsimd", consts)
    consts_ready = s_g.bump()

    def bias_prep(e):
        e.wait_ge(s_ld.h, bias_loaded)
        e.tensor_scalar(BIA[:, 0:NCH], BIA[:, 0:NCH], QSCALE, None, AluOpType.mult).then_inc(s_v.h, 1)
    p.add("vector", bias_prep)
    bias_ready = s_v.bump()

    ln_done = []
    for sp in range(2):
        t0 = sp * SPAN
        bufs = dict(XBF2=XBF2, SQ2=SQ2, ONESC=ONESC, ONESR=ONESR, STATR=STATR, AB=AB, ABC=ABC,
                    TMP=TMP, PS_MU=PS_MU, PS_SQ=PS_SQ, PS_AB=PS_AB, EPS1=EPS1,
                    consts_ready=consts_ready, x_ready=0)

        def wait_x(e):
            e.wait_ge(s_ld.h, s_ld.n)
            e.nop().then_inc(s_v.h, 1)
        p.add("vector", wait_x)
        s_v.bump()

        done = emit_layernorm(p, {"v": s_v, "a": s_a, "t": s_t, "g": s_g, "gx": s_gx,
                                  "chv": sems["chv"], "cha": sems["cha"]}, bufs,
                              NCH, TPC, t0, SPAN, X, XN)
        ln_done.append(done)

    evac_cnt = 0
    mm_groups = 0
    wl_sets = 0
    projs = [("Wq", "qT", 0, QSCALE), ("Wk", "kT", 1, 1.0), ("Wv", "vT", 2, 1.0)]
    set_mm_done = {}
    for oi, (wn, on, bidx, scal) in enumerate(projs):
        W, OUT = Ws[wn], OUTs[on]
        for espan in range(4):
            step = oi * 4 + espan
            buf = step % 2
            guard = set_mm_done.get(step - 2, 0)

            def wload(e, W=W, espan=espan, buf=buf, guard=guard):
                if guard:
                    e.wait_ge(s_t.h, guard)
                src = W.rearrange("(c p) e -> p c e", c=NCH)[:, :, espan * 512:(espan + 1) * 512]
                e.dma_start(out=WBF[buf][:].rearrange("p (c e) -> p c e", c=NCH),
                            in_=src).then_inc(s_wl[buf].h, 16)
            p.add("gpsimd", wload)
            set_loaded = s_wl[buf].bump(16)

            chv_h = sems["chv"].h

            def mms(e, buf=buf, set_loaded=set_loaded, step=step, ln_all=tuple(ln_done),
                    chv_h=chv_h):
                e.wait_ge(s_wl[buf].h, set_loaded)
                if step == 0:
                    e.wait_ge(chv_h, max(ln_all))
                for eb in range(4):
                    for tsp in range(2):
                        j = (step * 8 + eb * 2 + tsp)
                        bank = j % 2
                        if j >= 2:
                            e.wait_ge(s_ev.h, j - 1)
                        for i in range(NCH):
                            inst = e.matmul(PS_O[bank][:],
                                            WBF[buf][:, i * 512 + eb * 128:i * 512 + (eb + 1) * 128],
                                            XN[:, i * TPC + tsp * SPAN:i * TPC + (tsp + 1) * SPAN],
                                            start=(i == 0), stop=(i == NCH - 1))
                        inst.then_inc(s_t.h, 1)
            p.add("tensor", mms)
            first_tile = s_t.n + 1
            set_mm_done[step] = s_t.bump(8)

            def evacs(e, bidx=bidx, scal=scal, first_tile=first_tile, step=step):
                for eb in range(4):
                    for tsp in range(2):
                        j = step * 8 + eb * 2 + tsp
                        bank = j % 2
                        e.wait_ge(s_t.h, first_tile + eb * 2 + tsp)
                        if j >= 2:
                            e.wait_ge(s_st[j % 2].h, 16 * (j // 2))
                        col = bidx * NCH + (step % 4) * 4 + eb
                        e.tensor_scalar(OUTB[bank][:], PS_O[bank][:], scal,
                                        BIA[:, col:col + 1], AluOpType.mult, AluOpType.add
                                        ).then_inc(s_ev.h, 1)
            p.add("vector", evacs)
            first_ev = s_ev.n + 1
            s_ev.bump(8)

            def stores(e, OUT=OUT, espan=espan, first_ev=first_ev, step=step):
                for eb in range(4):
                    for tsp in range(2):
                        j = step * 8 + eb * 2 + tsp
                        bank = j % 2
                        e.wait_ge(s_ev.h, first_ev + eb * 2 + tsp)
                        r0 = espan * 512 + eb * 128
                        e.dma_start(out=OUT[r0:r0 + 128, tsp * SPAN:(tsp + 1) * SPAN],
                                    in_=OUTB[bank][:]).then_inc(s_st[bank].h, 16)
            p.add("sync", stores)
            s_st[0].bump(16 * 4)
            s_st[1].bump(16 * 4)

    sign_units = []
    for rc in range(16):
        for half in range(2):
            sign_units.append((UTd[rc * 128:(rc + 1) * 128, half * 4096:(half + 1) * 4096],
                               UTSd[rc * 128:(rc + 1) * 128, half * 4096:(half + 1) * 4096]))
    for u in range(32):
        sign_units.append((DTd[u * 256:(u + 1) * 256, :].rearrange("(c p) dd -> p c dd", c=2),
                           DTSd[u * 256:(u + 1) * 256, :].rearrange("(c p) dd -> p c dd", c=2)))

    def emit_sgload(ui):
        sap = sign_units[ui][0]
        par = ui % 2

        def sg_load(e, sap=sap, par=par, ui=ui, sa_guard=(ui - 1) if ui >= 2 else 0):
            if sa_guard:
                e.wait_ge(s_sa.h, sa_guard)
            dst = SGF[par][:]
            if len(sap.shape) == 3:
                dst = dst.rearrange("p (c dd) -> p c dd", c=2)
            e.dma_start(out=dst, in_=sap).then_inc(s_sg[par].h, 16)
        p.add("scalar", sg_load)
        s_sg[par].bump(16)
        return s_sg[par].n

    ldvals = {}
    ldvals[0] = emit_sgload(0)
    ldvals[1] = emit_sgload(1)
    for ui in range(len(sign_units)):
        par = ui % 2
        ld_val = ldvals[ui]

        def sg_sign(e, par=par, ld_val=ld_val, ui=ui):
            e.wait_ge(s_sg[par].h, ld_val)
            if ui >= 2:
                e.wait_ge(s_so[par].h, 16 * (ui // 2))
            e.activation(SGB[par][:], SGF[par][:], AF.Sign).then_inc(s_sa.h, 1)
        p.add("scalar", sg_sign)
        sa_val = s_sa.bump(1)
        if ui + 2 < len(sign_units):
            ldvals[ui + 2] = emit_sgload(ui + 2)

        def sg_store(e, dap=sign_units[ui][1], par=par, sa_val=sa_val):
            e.wait_ge(s_sa.h, sa_val)
            srcb = SGB[par][:]
            if len(dap.shape) == 3:
                srcb = srcb.rearrange("p (c dd) -> p c dd", c=2)
            e.dma_start(out=dap, in_=srcb).then_inc(s_so[par].h, 16)
        p.add("sync", sg_store)
        s_so[par].bump(16)

    def fin(e):
        e.wait_ge(s_st[0].h, s_st[0].n)
        e.wait_ge(s_st[1].h, s_st[1].n)
        e.wait_ge(s_so[0].h, s_so[0].n)
        e.wait_ge(s_so[1].h, s_so[1].n)
    p.add("sync", fin)

    p.emit(es.enter_context(nc.Block()))
    es.close()
    return nc



def stage_l1(inputs):
    x = np.asarray(inputs["L0_attn_a_x"])[0]
    Wq = np.asarray(inputs["L0_attn_a_Wq"])
    Wk = np.asarray(inputs["L0_attn_a_Wk"])
    Wv = np.asarray(inputs["L0_attn_a_Wv"])
    bq = np.asarray(inputs["L0_attn_a_bq"])
    bk = np.asarray(inputs["L0_attn_a_bk"])
    bv = np.asarray(inputs["L0_attn_a_bv"])
    UT_T = np.ascontiguousarray(np.asarray(inputs["L0_ffn_up_weight"]).T)
    DT_T = np.ascontiguousarray(np.asarray(inputs["L0_ffn_down_weight"]).T)

    def bT(b):
        return np.ascontiguousarray(b.reshape(NCH, 128).T)

    in_maps = []
    for c in range(NCORES):
        a0, b0 = span_starts(c)
        toks = np.r_[a0:a0 + SPAN, b0:b0 + SPAN]
        xTc = np.ascontiguousarray(x[toks, :].T)
        in_maps.append({
            "xT": xTc, "Wq": Wq, "Wk": Wk, "Wv": Wv,
            "bqT": bT(bq), "bkT": bT(bk), "bvT": bT(bv),
            "UT": UT_T, "DT": DT_T,
        })
    return in_maps


def kernel(**inputs):
    raise NotImplementedError("under construction: L2/L3 pending")



SUBW = [32, 64, 64, 128, 128, 256, 256, 512] + [512] * 8
WBLK = [2, 3, 4, 5, 7, 10, 14, 19, 27, 38, 54, 64, 64, 64, 64, 64]
KWIN = 68
HGROUPS = [(0, 1, 2), (3, 4, 5), (6, 7, 8), (9, 10, 11), (12, 13, 14), (15,)]
KBATCH = 4


def attention_sites(span):
    groups = []
    for gh in HGROUPS:
        persite = []
        kbs_needed = set()
        for h in gh:
            W = SUBW[h]
            nsub = 512 // W
            w = WBLK[h]
            for s in range(nsub):
                q0 = W * s
                jlast = (W * (s + 1) - 1) // 128
                kb_hi = 64 + jlast
                kb_lo = kb_hi - (w - 1)
                kb_lo = max(kb_lo, 0)
                for kb in range(kb_lo, kb_hi + 1):
                    kbs_needed.add(kb)
                persite.append((h, s, q0, W, kb_lo, kb_hi))
        kbs_sorted = sorted(kbs_needed)
        batches = []
        for b0 in range(0, len(kbs_sorted), KBATCH):
            kb_list = kbs_sorted[b0:b0 + KBATCH]
            site_list = []
            for (h, s, q0, W, kb_lo, kb_hi) in persite:
                for kb in kb_list:
                    if kb_lo <= kb <= kb_hi:
                        r = q0 - 128 * (kb - 64)
                        jq_lo = q0 // 128
                        jq_hi = (q0 + W - 1) // 128
                        overlaps = (64 + jq_lo) <= kb <= (64 + jq_hi)
                        mask_key = (W, r) if overlaps else None
                        site_list.append(dict(h=h, sub=s, q0=q0, W=W, kb=kb,
                                              first=(kb == kb_lo), last=(kb == kb_hi),
                                              mask_key=mask_key))
            batches.append((kb_list, site_list))
        groups.append((gh, batches))
    return groups


def all_mask_keys():
    keys = []
    for span in range(1):
        for gh, batches in attention_sites(0):
            for kb_list, sites in batches:
                for st in sites:
                    if st["mask_key"] and st["mask_key"] not in keys:
                        keys.append(st["mask_key"])
    return sorted(keys)


def build_masks():
    keys = all_mask_keys()
    offs = {}
    cols = []
    pos = 0
    for (W, r) in keys:
        kk = np.arange(128)[:, None]
        qq = np.arange(W)[None, :]
        m = (kk <= qq + r).astype(BF)
        offs[(W, r)] = (pos, W)
        cols.append(m)
        pos += W
    return np.concatenate(cols, axis=1), offs


def site_count():
    n = 0
    for gh, batches in attention_sites(0):
        for kb_list, sites in batches:
            n += len(sites)
    return n


def build_bias_table(core):
    nsites = site_count()
    tab = np.zeros((128, 2 * nsites), np.float32)
    for sp in range(2):
        a0, b0 = span_starts(core)
        q_glob0 = a0 if sp == 0 else b0
        blk0 = q_glob0 // 128
        idx = 0
        for gh, batches in attention_sites(sp):
            for kb_list, sites in batches:
                for st in sites:
                    h, kb, W, q0 = st["h"], st["kb"], st["W"], st["q0"]
                    gkb = blk0 + (kb - 64)
                    col = sp * nsites + idx
                    if gkb < 0:
                        tab[:, col] = -1e30
                    else:
                        slope = SLOPES[h]
                        kglob = 128 * gkb + np.arange(128)
                        qg0 = q_glob0 + q0
                        c = slope * (W / 2.0)
                        v = slope * (kglob - qg0) - c
                        v[kglob > qg0 + W - 1] = -1e30
                        tab[:, col] = v
                    idx += 1
    return tab



class Bank:
    def __init__(self, ap):
        self.ap = ap
        self.w = None
        self.r = None


def wms(e, ms):
    if ms is not None:
        e.wait_ge(ms[0].h, ms[1])


_M, _OFFS = None, None


def masks_and_offs():
    global _M, _OFFS
    if _M is None:
        _M, _OFFS = build_masks()
    return _M, _OFFS


def emit_ln2(ctx, sp, x_ready_ms):
    (p, banks, sems, X, XN, AR, WS, ONESC, ONESR, EPS1, STATR, ABrow, ABC, TMP,
     XBF2, SQ2, UPS, DSOS, UTSd, DTSd, mm_done, ws_last) = ctx['core']
    s_v, s_a, s_ts, ch_v, ch_a = sems['v'], sems['a'], sems['ts'], sems['chv'], sems['cha']
    s_wl = [sems['wl0'], sems['wl1']]
    s_wls = [sems['wls0'], sems['wls1']]
    t0 = sp * SPAN
    b_mu, b_sq, b_a, b_b = banks[4], banks[5], banks[6], banks[7]
    c0 = ch_v.n
    mu_rd, sq_rd = b_mu.r, b_sq.r
    tchk0 = s_ts.n

    def cast_chunks(e, t0=t0, x_ready_ms=x_ready_ms, c0=c0, tchk0=tchk0):
        wms(e, x_ready_ms)
        for i in range(NCH):
            if i >= 2:
                e.wait_ge(s_ts.h, tchk0 + i - 1)
            e.tensor_copy(XBF2[i % 2][:], X[:, i * TPC + t0:i * TPC + t0 + SPAN]
                          ).then_inc(ch_v.h, 1)
            e.wait_ge(ch_v.h, c0 + 2 * i + 1)
            e.tensor_tensor(SQ2[i % 2][:], XBF2[i % 2][:], XBF2[i % 2][:],
                            AluOpType.mult).then_inc(ch_v.h, 1)
    p.add("vector", cast_chunks)
    ch_v.bump(2 * NCH)

    def stats_mm(e, c0=c0, b_mu_ap=b_mu.ap, b_sq_ap=b_sq.ap, rd=(mu_rd, sq_rd)):
        wms(e, rd[0])
        wms(e, rd[1])
        for i in range(NCH):
            e.wait_ge(ch_v.h, c0 + 2 * i + 2)
            e.matmul(b_mu_ap[0:1, :], ONESC[:], XBF2[i % 2][:],
                     start=(i == 0), stop=(i == NCH - 1))
            inst = e.matmul(b_sq_ap[0:1, :], ONESC[:], SQ2[i % 2][:],
                            start=(i == 0), stop=(i == NCH - 1))
            inst.then_inc(s_ts.h, 1)
    p.add("tensor", stats_mm)
    stats_done = s_ts.bump(NCH)
    b_mu.w = b_sq.w = (s_ts, stats_done)
    c1, c2, c3 = ch_v.bump(1), ch_v.bump(1), ch_v.bump(1)

    def stats_fin(e, stats_done=stats_done, b_mu_ap=b_mu.ap, b_sq_ap=b_sq.ap,
                  c1=c1, c2=c2):
        e.wait_ge(s_ts.h, stats_done)
        e.tensor_scalar(STATR[0:1, 0:SPAN], b_mu_ap[0:1, :], 1.0 / D, None, AluOpType.mult)
        e.tensor_scalar(STATR[0:1, SPAN:2 * SPAN], b_sq_ap[0:1, :], 1.0 / D, None,
                        AluOpType.mult).then_inc(ch_v.h, 1)
        e.wait_ge(ch_v.h, c1)
        e.tensor_tensor(ABrow[0:1, SPAN:2 * SPAN], STATR[0:1, 0:SPAN], STATR[0:1, 0:SPAN],
                        AluOpType.mult).then_inc(ch_v.h, 1)
        e.wait_ge(ch_v.h, c2)
        e.tensor_tensor(ABrow[0:1, SPAN:2 * SPAN], STATR[0:1, SPAN:2 * SPAN],
                        ABrow[0:1, SPAN:2 * SPAN], AluOpType.subtract).then_inc(ch_v.h, 1)
    p.add("vector", stats_fin)
    var_done = (ch_v, ch_v.n)
    b_mu.r = b_sq.r = var_done
    ca = ch_a.bump(1)

    def rstd_f(e, var_done=var_done, ca=ca):
        wms(e, var_done)
        e.activation(ABrow[0:1, 0:SPAN], ABrow[0:1, SPAN:2 * SPAN], AF.Ln,
                     bias=EPS1[0:1, 0:1], scale=1.0).then_inc(ch_a.h, 1)
        e.wait_ge(ch_a.h, ca)
        e.activation(ABrow[0:1, 0:SPAN], ABrow[0:1, 0:SPAN], AF.Exp,
                     scale=-0.5).then_inc(s_a.h, 1)
    p.add("scalar", rstd_f)
    rstd_done = s_a.bump(1)
    c4 = ch_v.bump(1)

    def brow_f(e, rstd_done=rstd_done, c4=c4):
        e.wait_ge(s_a.h, rstd_done)
        e.tensor_tensor(ABrow[0:1, SPAN:2 * SPAN], STATR[0:1, 0:SPAN], ABrow[0:1, 0:SPAN],
                        AluOpType.mult).then_inc(ch_v.h, 1)
        e.wait_ge(ch_v.h, c4)
        e.tensor_scalar(ABrow[0:1, SPAN:2 * SPAN], ABrow[0:1, SPAN:2 * SPAN], -1.0, None,
                        AluOpType.mult).then_inc(s_v.h, 1)
    p.add("vector", brow_f)
    brow_done = s_v.bump(1)

    def bcast_mm(e, brow_done=brow_done, b_a_ap=b_a.ap, b_b_ap=b_b.ap,
                 rd=(b_a.r, b_b.r)):
        e.wait_ge(s_v.h, brow_done)
        wms(e, rd[0])
        wms(e, rd[1])
        e.matmul(b_a_ap[:], ONESR[:], ABrow[0:1, 0:SPAN], start=True, stop=True)
        e.matmul(b_b_ap[:], ONESR[:], ABrow[0:1, SPAN:2 * SPAN], start=True,
                 stop=True).then_inc(s_ts.h, 1)
    p.add("tensor", bcast_mm)
    bcast_done = s_ts.bump(1)
    b_a.w = b_b.w = (s_ts, bcast_done)
    c5 = ch_v.bump(1)

    def evac_bcast(e, bcast_done=bcast_done, b_a_ap=b_a.ap, b_b_ap=b_b.ap, c5=c5):
        e.wait_ge(s_ts.h, bcast_done)
        e.tensor_copy(ABC[:, 0:SPAN], b_a_ap[:]).then_inc(ch_v.h, 1)
        e.wait_ge(ch_v.h, c5)
        e.tensor_copy(ABC[:, SPAN:2 * SPAN], b_b_ap[:]).then_inc(s_v.h, 1)
    p.add("vector", evac_bcast)
    abc_done = s_v.bump(1)
    b_a.r = b_b.r = (s_v, abc_done)
    ap_c0 = ch_v.n

    def apply_f(e, abc_done=abc_done, t0=t0, ap_c0=ap_c0):
        e.wait_ge(s_v.h, abc_done)
        for i in range(NCH):
            if i:
                e.wait_ge(ch_v.h, ap_c0 + 2 * i)
            e.tensor_tensor(TMP[:, 0:SPAN], X[:, i * TPC + t0:i * TPC + t0 + SPAN],
                            ABC[:, 0:SPAN], AluOpType.mult).then_inc(ch_v.h, 1)
            e.wait_ge(ch_v.h, ap_c0 + 2 * i + 1)
            e.tensor_tensor(XN[:, i * SPAN:(i + 1) * SPAN], TMP[:, 0:SPAN],
                            ABC[:, SPAN:2 * SPAN], AluOpType.add).then_inc(ch_v.h, 1)
    p.add("vector", apply_f)
    ch_v.bump(2 * NCH)
    return (ch_v, ch_v.n)



def emit_ffn_span(ctx, sp, xn_ms, name):
    (p, banks, sems, X, XN, AR, WS, ONESC, ONESR, EPS1, STATR, ABrow, ABC, TMP,
     XBF2, SQ2, UPS, DSOS, UTSd, DTSd, mm_done, ws_last) = ctx['core']
    s_v, s_a, s_ts, ch_v, ch_a = sems['v'], sems['a'], sems['ts'], sems['chv'], sems['cha']
    s_wls = [sems['wls0'], sems['wls1']]
    WSF = ctx['WSF']
    wsf_last = ctx.setdefault('wsf_last', {0: None, 1: None})
    for hs in range(16):
        buf = hs % 2
        guard = wsf_last[buf]

        def wload(e, hs=hs, buf=buf, guard=guard):
            wms(e, guard)
            src = UTSd.rearrange("(c p) hh -> p c hh", c=NCH)[:, :, hs * 512:(hs + 1) * 512]
            e.dma_start(out=WSF[buf][:].rearrange("p (c hh) -> p c hh", c=NCH),
                        in_=src).then_inc(s_wls[buf].h, 16)
        p.add("sync", wload)
        set_loaded = s_wls[buf].bump(16)

        for hb4 in range(4):
            hb = hs * 4 + hb4
            bank = banks[hb % 2]
            my_mm = s_ts.n + 1

            def mms(e, buf=buf, hb4=hb4, hb=hb, bank_ap=bank.ap, set_loaded=set_loaded,
                    xn_ms=xn_ms, bank_r=bank.r):
                e.wait_ge(s_wls[buf].h, set_loaded)
                if hb == 0:
                    wms(e, xn_ms)
                wms(e, bank_r)
                for i in range(NCH):
                    inst = e.matmul(bank_ap[:],
                                    WSF[buf][:, i * 512 + hb4 * 128:i * 512 + (hb4 + 1) * 128],
                                    XN[:, i * SPAN:(i + 1) * SPAN],
                                    start=(i == 0), stop=(i == NCH - 1))
                inst.then_inc(s_ts.h, 1)
            p.add("tensor", mms)
            s_ts.bump(1)
            bank.w = (s_ts, my_mm)
            wsf_last[buf] = (s_ts, my_mm)

            def evac_up(e, bank_ap=bank.ap, hb=hb, my_mm=my_mm):
                e.wait_ge(s_ts.h, my_mm)
                e.activation(AR[:, hb * SPAN:(hb + 1) * SPAN], bank_ap[:], AF.Relu,
                             scale=UPS[:, hb:hb + 1]).then_inc(s_a.h, 1)
            p.add("scalar", evac_up)
            bank.r = (s_a, s_a.bump(1))
    up_all = (s_a, s_a.n)

    last = None
    for ddb in range(NCH):
        buf = ddb % 2
        guard = wsf_last[buf]

        def wload(e, ddb=ddb, buf=buf, guard=guard):
            wms(e, guard)
            src = DTSd.rearrange("(c p) dd -> p c dd", c=64)[:, :, ddb * 128:(ddb + 1) * 128]
            e.dma_start(out=WSF[buf][:].rearrange("p (c dd) -> p c dd", c=64),
                        in_=src).then_inc(s_wls[buf].h, 16)
        p.add("sync", wload)
        set_loaded = s_wls[buf].bump(16)
        bank = banks[2 + ddb % 2]
        my_mm = s_ts.n + 1

        def mms(e, buf=buf, bank_ap=bank.ap, set_loaded=set_loaded,
                up_all=up_all, ddb=ddb, bank_r=bank.r):
            e.wait_ge(s_wls[buf].h, set_loaded)
            if ddb == 0:
                wms(e, up_all)
            wms(e, bank_r)
            for i in range(64):
                inst = e.matmul(bank_ap[:],
                                WSF[buf][:, i * 128:(i + 1) * 128],
                                AR[:, i * SPAN:(i + 1) * SPAN],
                                start=(i == 0), stop=(i == 63))
            inst.then_inc(s_ts.h, 1)
        p.add("tensor", mms)
        s_ts.bump(1)
        bank.w = (s_ts, my_mm)
        wsf_last[buf] = (s_ts, my_mm)

        def evac_dn(e, bank_ap=bank.ap, ddb=ddb, sp=sp, my_mm=my_mm):
            e.wait_ge(s_ts.h, my_mm)
            e.scalar_tensor_tensor(
                X[:, ddb * TPC + sp * SPAN:ddb * TPC + (sp + 1) * SPAN],
                bank_ap[:], DSOS[:, ddb:ddb + 1],
                X[:, ddb * TPC + sp * SPAN:ddb * TPC + (sp + 1) * SPAN],
                AluOpType.mult, AluOpType.add).then_inc(s_v.h, 1)
        p.add("vector", evac_dn)
        last = (s_v, s_v.bump(1))
        bank.r = last
    return last


def emit_proj_to_x(ctx, src_off, Wd, bias_ap, ready_ms, name, residual=True):
    (p, banks, sems, X, XN, AR, WS, ONESC, ONESR, EPS1, STATR, ABrow, ABC, TMP,
     XBF2, SQ2, UPS, DSOS, UTSd, DTSd, mm_done, ws_last) = ctx['core']
    s_v, s_a, s_ts, ch_v, ch_a = sems['v'], sems['a'], sems['ts'], sems['chv'], sems['cha']
    s_wl = [sems['wl0'], sems['wl1']]
    s_wls = [sems['wls0'], sems['wls1']]
    WSF = ctx['WSF']
    ws_last = ctx.setdefault('wsf_last', {0: None, 1: None})
    last = None
    for ec in range(NCH):
        buf = ec % 2
        guard = ws_last[buf]

        def wload(e, Wd=Wd, ec=ec, buf=buf, guard=guard):
            wms(e, guard)
            src = Wd.rearrange("(c p) ee -> p c ee", c=NCH)[:, :, ec * 128:(ec + 1) * 128]
            e.dma_start(out=WSF[buf][:, 0:NCH * 128].rearrange("p (c ee) -> p c ee", c=NCH),
                        in_=src).then_inc(s_wl[buf].h, 16)
        p.add("gpsimd", wload)
        set_loaded = s_wl[buf].bump(16)

        for tsp in range(2):
            j = ec * 2 + tsp
            bank = banks[j % 2]
            my_mm = s_ts.n + 1

            def mms(e, buf=buf, tsp=tsp, bank_ap=bank.ap, src_off=src_off,
                    set_loaded=set_loaded, j=j, ready_ms=ready_ms, bank_r=bank.r):
                e.wait_ge(s_wl[buf].h, set_loaded)
                if j == 0:
                    wms(e, ready_ms)
                wms(e, bank_r)
                for i in range(NCH):
                    inst = e.matmul(bank_ap[:],
                                    WSF[buf][:, i * 128:(i + 1) * 128],
                                    AR[:, src_off + i * TPC + tsp * SPAN:
                                       src_off + i * TPC + (tsp + 1) * SPAN],
                                    start=(i == 0), stop=(i == NCH - 1))
                inst.then_inc(s_ts.h, 1)
            p.add("tensor", mms)
            s_ts.bump(1)
            bank.w = (s_ts, my_mm)
            mm_done[(name, ec)] = my_mm
            ws_last[buf] = (s_ts, my_mm)

            def evac(e, bank_ap=bank.ap, ec=ec, tsp=tsp, my_mm=my_mm, bias_ap=bias_ap):
                e.wait_ge(s_ts.h, my_mm)
                e.scalar_tensor_tensor(
                    X[:, ec * TPC + tsp * SPAN:ec * TPC + (tsp + 1) * SPAN],
                    bank_ap[:], bias_ap[:, ec:ec + 1],
                    X[:, ec * TPC + tsp * SPAN:ec * TPC + (tsp + 1) * SPAN],
                    AluOpType.add, AluOpType.add).then_inc(s_v.h, 1)
            p.add("vector", evac)
            last = (s_v, s_v.bump(1))
            bank.r = last
    return last



def build_l2(upto="full"):
    MASKARR, MASK_OFFS = masks_and_offs()
    MASK_COLS = MASKARR.shape[1]
    nsites = site_count()

    nc = bass.Bass("TRN2", target_bir_lowering=False)
    dram = lambda n, shape, dt, kind="ExternalInput": nc.dram_tensor(n, shape, dt, kind=kind).ap()
    XTd = dram("XT", [D, TPC], F32)
    QTd = dram("QT", [D, TPC], BF16)
    KWd = [dram("KWA", [D, KWIN * 128], BF16), dram("KWB", [D, KWIN * 128], BF16)]
    VWd = [dram("VWA", [KWIN * 128, D], BF16), dram("VWB", [KWIN * 128, D], BF16)]
    BTABd = dram("BTAB", [128, 2 * nsites], F32)
    MASKSd = dram("MASKS", [128, MASK_COLS], BF16)
    Wod = dram("Wo", [D, D], F32)
    boTd = dram("boT", [128, NCH], F32)
    UTSd = dram("UTS", [D, DH], BF16)
    DTSd = dram("DTS", [DH, D], BF16)
    upsTd = dram("upsT", [128, DH // 128], F32)
    dsTd = dram("dsT", [128, NCH], F32)
    oscd = dram("osc", [128, 1], F32)
    bWTd = [dram(n, [D, D], F32) for n in ("bWkT", "bWvT", "bWrT")]
    tdTd = dram("tdT", [128, NH], F32)
    tfTd = dram("tfT", [128, NH], F32)
    find_d = dram("FIRSTIND", [128, 1], F32)

    X2Td = dram("X2T", [D, TPC], F32, "ExternalOutput")
    NUMLd = dram("NUML", [D, TPC], F32, "ExternalOutput")
    DENLd = dram("DENL", [D, TPC], F32, "ExternalOutput")
    SIGd = dram("SIG", [D, TPC], F32, "ExternalOutput")
    SUMSd = dram("SUMS", [128, NCH * 4], F32, "ExternalOutput")

    es = ExitStack()
    sb = lambda name, shape, dt: es.enter_context(nc.sbuf_tensor(name, shape, dt))

    X = sb("X", [128, NCH * TPC], F32)
    XN = sb("XN", [128, NCH * SPAN], BF16)
    AR = sb("AR", [128, 2 * NCH * TPC], BF16)
    WS = None
    BIA2 = sb("BIA2", [128, NCH], F32)
    ONESC = sb("ONESC", [128, 1], BF16)
    ONESR = sb("ONESR", [1, 128], F32)
    EPS1 = sb("EPS1", [1, 1], F32)
    STATR = sb("STATR", [1, 2 * SPAN], F32)
    ABrow = sb("ABrow", [1, 2 * SPAN], F32)
    ABC = sb("ABC", [128, 2 * SPAN], F32)
    TMP = sb("TMP", [128, SPAN], F32)
    XBF2 = [sb(f"XBF2{i}", [128, SPAN], BF16) for i in range(2)]
    SQ2 = [sb(f"SQ2{i}", [128, SPAN], BF16) for i in range(2)]
    UPS = sb("UPS", [128, DH // 128], F32)
    DST = sb("DST", [128, NCH], F32)
    OSC = sb("OSC", [128, 1], F32)
    DSOS = sb("DSOS", [128, NCH], F32)
    TD = sb("TD", [128, NH], F32)
    TF = sb("TF", [128, NH], F32)
    IND = sb("IND", [128, 1], F32)
    E1 = sb("E1", [128, NH], F32)
    DEC = sb("DEC", [128, NH], F32)
    FEFF = sb("FEFF", [128, NH], F32)
    SUMS = sb("SUMSs", [128, NCH * 4], F32)

    es_attn = ExitStack()
    sba = lambda name, shape, dt: es_attn.enter_context(nc.sbuf_tensor(name, shape, dt))
    KB = [sba(f"KB{i}", [128, 3 * KBATCH * 128], BF16) for i in range(2)]
    VB = [sba(f"VB{i}", [128, 3 * KBATCH * 128], BF16) for i in range(2)]
    PT = [sba(f"PT{i}", [128, SPAN], BF16) for i in range(4)]
    BTAB = sba("BTABs", [128, nsites], F32)
    MASKS = sba("MASKSs", [128, MASK_COLS], BF16)
    RL = sba("RL", [1, SPAN], F32)
    RLB = sba("RLB", [128, SPAN], F32)

    PS = [es.enter_context(nc.psum_tensor(f"PS{i}", [128, SPAN], F32)) for i in range(8)]
    banks = [Bank(t) for t in PS]

    sems = {k: Sem(es.enter_context(nc.semaphore("s_" + k)))
            for k in ("ld", "ld2", "v", "a", "ts", "tp", "g", "kv0", "kv1", "wl0", "wl1",
                      "wls0", "wls1", "st0", "st1", "stx", "chv", "cha",
                      "w30", "w31", "w32", "w33", "w34", "w35")}
    s_ld, s_v, s_a, s_g = sems["ld"], sems["v"], sems["a"], sems["g"]
    s_ld2 = sems["ld2"]
    s_ts, s_tp = sems["ts"], sems["tp"]
    s_kv = [sems["kv0"], sems["kv1"]]
    ch_v, ch_a = sems["chv"], sems["cha"]
    s_wl = [sems["wl0"], sems["wl1"]]
    s_wls = [sems["wls0"], sems["wls1"]]
    s_st = [sems["st0"], sems["st1"]]
    s_stx = sems["stx"]
    s_w3 = [sems["w3" + str(i)] for i in range(6)]

    p = Plan()
    QT_OFF = 0
    OT_OFF = NCH * TPC

    def loads(e):
        e.dma_start(out=X[:].rearrange("p (c t) -> p c t", c=NCH),
                    in_=XTd.rearrange("(c p) t -> p c t", c=NCH)).then_inc(s_ld.h, 16)
        e.dma_start(out=AR[:, QT_OFF:QT_OFF + NCH * TPC].rearrange("p (c t) -> p c t", c=NCH),
                    in_=QTd.rearrange("(c p) t -> p c t", c=NCH)).then_inc(s_ld.h, 16)
        e.dma_start(out=BTAB[:], in_=BTABd[:, 0:nsites]).then_inc(s_ld.h, 16)
        e.dma_start(out=MASKS[:], in_=MASKSd).then_inc(s_ld.h, 16)
        e.dma_start(out=BIA2[:], in_=boTd).then_inc(s_ld.h, 16)
        e.dma_start(out=UPS[:], in_=upsTd).then_inc(s_ld.h, 16)
        e.dma_start(out=DST[:], in_=dsTd).then_inc(s_ld.h, 16)
        e.dma_start(out=OSC[:], in_=oscd).then_inc(s_ld.h, 16)
        e.dma_start(out=TD[:], in_=tdTd).then_inc(s_ld.h, 16)
        e.dma_start(out=TF[:], in_=tfTd).then_inc(s_ld.h, 16)
        e.dma_start(out=IND[:], in_=find_d).then_inc(s_ld.h, 16)
    p.add("sync", loads)
    loads_done = s_ld.bump(16 * 11)

    def consts(e):
        e.memset(ONESC[:], 1.0)
        e.memset(ONESR[:], 1.0)
        e.memset(EPS1[:], LN_EPS).then_inc(s_g.h, 1)
    p.add("gpsimd", consts)
    consts_done = s_g.bump()

    for eng in ("tensor", "scalar", "vector"):
        def gate(e, _ld=loads_done, _g=consts_done):
            e.wait_ge(s_ld.h, _ld)
            e.wait_ge(s_g.h, _g)
        p.add(eng, gate)

    site_idx = 0
    btab_reload_ms = None
    kv_parity = 0
    kv_guard = {0: 0, 1: 0}
    for sp in range(2):
        if sp == 1:
            exp_a_done = s_a.n

            def btab_reload(e, exp_a_done=exp_a_done):
                e.wait_ge(s_a.h, exp_a_done)
                e.dma_start(out=BTAB[:], in_=BTABd[:, nsites:2 * nsites]).then_inc(s_ld2.h, 16)
            p.add("sync", btab_reload)
            btab_reload_ms = (s_ld2, s_ld2.bump(16))
        for gi, (gh, batches) in enumerate(attention_sites(sp)):
            oh_banks = {h: banks[j] for j, h in enumerate(gh)}
            l_banks = {h: banks[3 + j] for j, h in enumerate(gh)}
            sping = [banks[6], banks[7]]
            first_pv = {h: True for h in gh}
            first_l = {h: True for h in gh}
            flat = []
            for bi, (kb_list, sites) in enumerate(batches):
                kb_lo = kb_list[0]
                nkb = kb_list[-1] - kb_lo + 1
                kbuf = kv_parity % 2
                kv_parity += 1
                flat.append(("BATCHLOAD", kb_lo, nkb, kbuf))
                kv_done_plan = s_kv[kbuf].n + 32
                for st in sites:
                    flat.append((st, kb_lo, nkb, kbuf, kv_done_plan))
                s_kv[kbuf].bump(32)
                flat.append(("ENDBATCH", kbuf))

            pendq = []
            for item in flat:
                if item[0] == "ENDBATCH":
                    kv_guard[item[1]] = s_tp.n + len(pendq)
                    continue
                if item[0] == "BATCHLOAD":
                    _, kb_lo, nkb, kbuf = item
                    guard_tp = kv_guard[kbuf]

                    def kvload(e, sp=sp, gh=gh, kb_lo=kb_lo, nkb=nkb, guard_tp=guard_tp,
                               kbuf=kbuf):
                        if guard_tp:
                            e.wait_ge(s_tp.h, guard_tp)
                        h0, nh = gh[0], len(gh)
                        src_k = KWd[sp][128 * h0:128 * (h0 + nh), 128 * kb_lo:128 * (kb_lo + nkb)]
                        e.dma_start(out=KB[kbuf][:, 0:nh * nkb * 128]
                                    .rearrange("p (h k) -> p h k", h=nh),
                                    in_=src_k.rearrange("(h p) k -> p h k", h=nh)
                                    ).then_inc(s_kv[kbuf].h, 16)
                        src_v = VWd[sp][128 * kb_lo:(128 * (kb_lo + nkb)), 128 * h0:128 * (h0 + nh)]
                        e.dma_start(out=VB[kbuf][:, 0:nkb * nh * 128]
                                    .rearrange("p (k hd) -> p k hd", k=nkb),
                                    in_=src_v.rearrange("(k p) hd -> p k hd", k=nkb)
                                    ).then_inc(s_kv[kbuf].h, 16)
                    p.add("sync", kvload)
                    continue
                st, kb_lo, nkb, kbuf, kv_done = item
                h, q0, W, kb = st["h"], st["q0"], st["W"], st["kb"]
                hi = gh.index(h)
                kbi = kb - kb_lo
                pp = site_idx % 2
                ppt = site_idx % 4
                ob, lb = oh_banks[h], l_banks[h]
                sb_ = sping[pp]
                col = site_idx % nsites
                my_score = s_ts.n + 1
                my_exp = s_a.n + 1
                is_first_pv = first_pv[h]; first_pv[h] = False
                is_first_l = first_l[h]; first_l[h] = False
                mask_key = st["mask_key"]

                def score(e, hi=hi, kbi=kbi, q0=q0, W=W, sp=sp, h=h, nkb=nkb,
                          kv_done=kv_done, sb_ap=sb_.ap, bank_r=sb_.r, kbuf=kbuf):
                    e.wait_ge(s_kv[kbuf].h, kv_done)
                    wms(e, bank_r)
                    e.matmul(sb_ap[:, 0:W],
                             KB[kbuf][:, (hi * nkb + kbi) * 128:(hi * nkb + kbi + 1) * 128],
                             AR[:, QT_OFF + h * TPC + sp * SPAN + q0:
                                QT_OFF + h * TPC + sp * SPAN + q0 + W],
                             start=True, stop=True).then_inc(s_ts.h, 1)
                p.add("tensor", score)
                s_ts.bump(1)
                sb_.w = (s_ts, my_score)

                if len(pendq) >= 2:
                    fn, bks = pendq.pop(0)
                    p.add("tensor", fn)
                    s_tp.bump(1)
                    bks[0].w = (s_tp, s_tp.n)
                    bks[1].w = (s_tp, s_tp.n)

                pt_guard = max(0, site_idx - 3)

                def expf(e, ppt=ppt, W=W, col=col, my_score=my_score, sb_ap=sb_.ap,
                         pt_guard=pt_guard, btab_ms=btab_reload_ms if sp == 1 else None,
                         first_of_span=(site_idx % nsites == 0)):
                    if btab_ms is not None and first_of_span:
                        wms(e, btab_ms)
                    e.wait_ge(s_ts.h, my_score)
                    if pt_guard > 0:
                        e.wait_ge(s_tp.h, pt_guard)
                    e.activation(PT[ppt][:, 0:W], sb_ap[:, 0:W], AF.Exp,
                                 bias=BTAB[:, col:col + 1], scale=1.0).then_inc(s_a.h, 1)
                p.add("scalar", expf)
                s_a.bump(1)
                sb_.r = (s_a, my_exp)

                mask_ms = None
                if mask_key is not None:
                    moff, mw = MASK_OFFS[mask_key]

                    def maskf(e, ppt=ppt, W=W, moff=moff, my_exp=my_exp):
                        e.wait_ge(s_a.h, my_exp)
                        e.tensor_tensor(PT[ppt][:, 0:W], PT[ppt][:, 0:W],
                                        MASKS[:, moff:moff + W],
                                        AluOpType.mult).then_inc(s_v.h, 1)
                    p.add("vector", maskf)
                    mask_ms = (s_v, s_v.bump(1))

                def pvl(e, ppt=ppt, hi=hi, kbi=kbi, q0=q0, W=W, nh=len(gh),
                        ob_ap=ob.ap, lb_ap=lb.ap, my_exp=my_exp, mask_ms=mask_ms,
                        is_first_pv=is_first_pv, is_first_l=is_first_l,
                        ob_r=ob.r, lb_r=lb.r, kbuf=kbuf):
                    if mask_ms is not None:
                        wms(e, mask_ms)
                    else:
                        e.wait_ge(s_a.h, my_exp)
                    if is_first_pv:
                        wms(e, ob_r)
                    if is_first_l:
                        wms(e, lb_r)
                    e.matmul(ob_ap[:, q0:q0 + W],
                             VB[kbuf][:, (kbi * nh + hi) * 128:(kbi * nh + hi + 1) * 128],
                             PT[ppt][:, 0:W], start=is_first_pv, stop=False,
                             skip_group_check=True)
                    e.matmul(lb_ap[0:1, q0:q0 + W], ONESC[:], PT[ppt][:, 0:W],
                             start=is_first_l, stop=False,
                             skip_group_check=True).then_inc(s_tp.h, 1)
                pendq.append((pvl, (ob, lb)))
                site_idx += 1
            while pendq:
                fn, bks = pendq.pop(0)
                p.add("tensor", fn)
                s_tp.bump(1)
                bks[0].w = (s_tp, s_tp.n)
                bks[1].w = (s_tp, s_tp.n)
            prev_bc = None
            for j, h in enumerate(gh):
                ob, lb = oh_banks[h], l_banks[h]
                bb = sping[0]
                c1 = ch_a.bump(1)

                def fin_rl(e, lb_ap=lb.ap, lb_w=lb.w, ob_w=ob.w, c1=c1, prev_bc=prev_bc):
                    wms(e, lb_w)
                    wms(e, ob_w)
                    wms(e, prev_bc)
                    e.activation(RL[0:1, :], lb_ap[0:1, :], AF.Ln).then_inc(ch_a.h, 1)
                    e.wait_ge(ch_a.h, c1)
                    e.activation(RL[0:1, :], RL[0:1, :], AF.Exp, scale=-1.0).then_inc(s_a.h, 1)
                p.add("scalar", fin_rl)
                rl_done = s_a.bump(1)
                lb.r = (s_a, rl_done)

                def fin_bc(e, bb_ap=bb.ap, rl_done=rl_done, bank_r=bb.r):
                    e.wait_ge(s_a.h, rl_done)
                    wms(e, bank_r)
                    e.matmul(bb_ap[:], ONESR[:], RL[0:1, :], start=True,
                             stop=True).then_inc(s_ts.h, 1)
                p.add("tensor", fin_bc)
                bc_done = s_ts.bump(1)
                bb.w = (s_ts, bc_done)
                prev_bc = (s_ts, bc_done)
                c2 = ch_v.bump(1)
                prev_ot = (s_v, s_v.n)

                def fin_ev(e, bb_ap=bb.ap, ob_ap=ob.ap, h=h, sp=sp, bc_done=bc_done,
                           c2=c2, prev_ot=prev_ot):
                    e.wait_ge(s_ts.h, bc_done)
                    wms(e, prev_ot)
                    e.tensor_copy(RLB[:], bb_ap[:]).then_inc(ch_v.h, 1)
                    e.wait_ge(ch_v.h, c2)
                    e.tensor_tensor(AR[:, OT_OFF + h * TPC + sp * SPAN:
                                       OT_OFF + h * TPC + (sp + 1) * SPAN],
                                    ob_ap[:], RLB[:], AluOpType.mult).then_inc(s_v.h, 1)
                p.add("vector", fin_ev)
                ot_done = s_v.bump(1)
                ob.r = (s_v, ot_done)
                bb.r = (s_v, ot_done)
    attn_done_ms = (s_v, s_v.n)
    es_attn.close()
    es_ffn = ExitStack()
    WSF = [es_ffn.enter_context(nc.sbuf_tensor(f"WSF{i}", [128, 8192], BF16)) for i in range(2)]

    mm_done = {}
    ws_last = {0: None, 1: None}
    ctx = {'core': (p, banks, sems, X, XN, AR, WS, ONESC, ONESR, EPS1, STATR, ABrow, ABC, TMP,
                    XBF2, SQ2, UPS, DSOS, UTSd, DTSd, mm_done, ws_last),
           'WSF': WSF,
           'wsf_last': {0: attn_done_ms, 1: attn_done_ms}}

    x1_ms = emit_proj_to_x(ctx, OT_OFF, Wod, BIA2, attn_done_ms, "woproj")

    cdsos = ch_v.bump(1)

    def dsos_prep(e, cdsos=cdsos):
        e.tensor_scalar(DSOS[:], DST[:], OSC[:, 0:1], None, AluOpType.mult).then_inc(ch_v.h, 1)
    p.add("vector", dsos_prep)
    dsos_ms = (ch_v, cdsos)

    if upto in ("ffn", "full"):
        xn_a = emit_ln2(ctx, 0, x1_ms)
        ffa = emit_ffn_span(ctx, 0, xn_a, "f0")
        xn_b = emit_ln2(ctx, 1, x1_ms)
        ffb = emit_ffn_span(ctx, 1, xn_b, "f1")
        x2_ms = ffb
    else:
        x2_ms = x1_ms

    def store_x(e, x2_ms=x2_ms):
        wms(e, x2_ms)
        e.dma_start(out=X2Td.rearrange("(c p) t -> p c t", c=NCH),
                    in_=X[:].rearrange("p (c t) -> p c t", c=NCH)).then_inc(s_st[0].h, 16)
    p.add("sync", store_x)
    s_st[0].bump(16)

    es_ffn.close()
    if upto == "full":
        ca1, ca2 = ch_a.bump(1), ch_a.bump(1)

        def decay_prep(e, ca1=ca1, ca2=ca2):
            e.activation(E1[:], TD[:], AF.Exp).then_inc(ch_a.h, 1)
            e.wait_ge(ch_a.h, ca1)
            e.activation(DEC[:], E1[:], AF.Exp, scale=-1.0).then_inc(ch_a.h, 1)
            e.wait_ge(ch_a.h, ca2)
            e.activation(FEFF[:], TF[:], AF.Exp, scale=IND[:, 0:1]).then_inc(s_a.h, 1)
        p.add("scalar", decay_prep)
        decay_ms = (s_a, s_a.bump(1))

        es_b = ExitStack()
        sbb = lambda name, shape, dt: es_b.enter_context(nc.sbuf_tensor(name, shape, dt))
        WT = sbb("WT", [128, SPAN], F32)
        AA = sbb("AA", [128, SPAN], F32)
        NUMT = sbb("NUMT", [128, SPAN], F32)
        DENT = sbb("DENT", [128, SPAN], F32)
        SIGT = sbb("SIGT", [128, SPAN], F32)
        DECB = sbb("DECB", [128, SPAN], F32)
        ONES32 = sbb("ONES32", [128, SPAN], F32)
        WB3 = [sbb(f"WB3{i}", [128, NCH * 128], BF16) for i in range(6)]

        def consts2(e, wl0=s_wl[0].n, wl1=s_wl[1].n, wls0=s_wls[0].n, wls1=s_wls[1].n,
                    ffn_mm=s_ts.n):
            e.wait_ge(s_wl[0].h, wl0)
            e.wait_ge(s_wl[1].h, wl1)
            e.wait_ge(s_wls[0].h, wls0)
            e.wait_ge(s_wls[1].h, wls1)
            e.wait_ge(s_ts.h, ffn_mm)
            e.memset(ONES32[:], 1.0).then_inc(s_g.h, 1)
        p.add("gpsimd", consts2)
        consts2_done = s_g.bump()

        for sp in range(2):
            xnb_ms = emit_ln2(ctx, sp, x2_ms)
            for cc in range(NCH):
                kbank = banks[0 + 3 * (cc % 2)]
                vbank = banks[1 + 3 * (cc % 2)]
                rbank = banks[2 + 3 * (cc % 2)]
                step = sp * NCH + cc
                my_mm = s_ts.n + 3
                for wj in range(3):
                    bslot = wj * 2 + (cc % 2)
                    guard = mm_done.get(("b3", step - 2, wj))

                    def wload(e, wj=wj, cc=cc, bslot=bslot, guard=guard, step=step,
                              c2=consts2_done):
                        if step < 2:
                            e.wait_ge(s_g.h, c2)
                        if guard:
                            e.wait_ge(s_ts.h, guard)
                        src = bWTd[wj].rearrange("(c p) cd -> p c cd", c=NCH)[
                            :, :, cc * 128:(cc + 1) * 128]
                        e.dma_start(out=WB3[bslot][:].rearrange("p (c cd) -> p c cd", c=NCH),
                                    in_=src).then_inc(s_w3[bslot].h, 16)
                    p.add("gpsimd", wload)
                    sl = (s_w3[bslot], s_w3[bslot].bump(16))

                    def mmh(e, wj=wj, cc=cc, bslot=bslot, sl=sl, xnb_ms=xnb_ms,
                            ap_=(kbank.ap, vbank.ap, rbank.ap)[wj],
                            rd=(kbank.r, vbank.r, rbank.r)[wj]):
                        e.wait_ge(sl[0].h, sl[1])
                        if wj == 0 and cc == 0:
                            wms(e, xnb_ms)
                        wms(e, rd)
                        for i in range(NCH):
                            inst = e.matmul(ap_[:],
                                            WB3[bslot][:, i * 128:(i + 1) * 128],
                                            XN[:, i * SPAN:(i + 1) * SPAN],
                                            start=(i == 0), stop=(i == NCH - 1))
                        inst.then_inc(s_ts.h, 1)
                    p.add("tensor", mmh)
                    s_ts.bump(1)
                    mm_done[("b3", step, wj)] = s_ts.n
                kbank.w = vbank.w = rbank.w = (s_ts, my_mm)

                def sig_f(e, rap=rbank.ap, my_mm=my_mm, step=step):
                    e.wait_ge(s_ts.h, my_mm)
                    if step:
                        e.wait_ge(s_stx.h, 48 * step)
                    e.activation(SIGT[:], rap[:], AF.Sigmoid).then_inc(s_a.h, 1)
                p.add("scalar", sig_f)
                sig_ms = (s_a, s_a.bump(1))
                rbank.r = sig_ms

                clip_v = s_v.bump(1)
                cbase = ch_v.n
                exp_done_val = s_a.n + 1

                def kvchain(e, kap=kbank.ap, vap=vbank.ap, my_mm=my_mm, cc=cc, sp=sp,
                            cbase=cbase, decay_ms=decay_ms, step=step,
                            exp_done=exp_done_val, c2_done=consts2_done):
                    e.wait_ge(s_ts.h, my_mm)
                    wms(e, decay_ms)
                    e.wait_ge(s_g.h, c2_done)
                    if step:
                        e.wait_ge(s_stx.h, 48 * step)
                    e.tensor_scalar(WT[:], kap[:], 10.0, -10.0, AluOpType.min,
                                    AluOpType.max).then_inc(s_v.h, 1)
                    e.wait_ge(s_a.h, exp_done)
                    e.tensor_tensor(AA[:], WT[:], vap[:], AluOpType.mult).then_inc(ch_v.h, 1)
                    e.tensor_scalar(DECB[:], ONES32[:], DEC[:, cc:cc + 1], None,
                                    AluOpType.mult).then_inc(ch_v.h, 1)
                    e.wait_ge(ch_v.h, cbase + 2)
                    e.tensor_tensor_scan(NUMT[:], DECB[:], AA[:], 0.0, AluOpType.mult,
                                         AluOpType.add).then_inc(ch_v.h, 1)
                    e.tensor_tensor_scan(DENT[:], DECB[:], WT[:], 0.0, AluOpType.mult,
                                         AluOpType.add).then_inc(ch_v.h, 1)
                    e.wait_ge(ch_v.h, cbase + 4)
                    e.tensor_copy(SUMS[:, cc * 4 + 2 * sp:cc * 4 + 2 * sp + 1],
                                  NUMT[:, SPAN - 1:SPAN]).then_inc(ch_v.h, 1)
                    e.tensor_copy(SUMS[:, cc * 4 + 2 * sp + 1:cc * 4 + 2 * sp + 2],
                                  DENT[:, SPAN - 1:SPAN]).then_inc(s_v.h, 1)
                p.add("vector", kvchain)
                ch_v.bump(5)
                scan_ms = (s_v, s_v.bump(1))
                kbank.r = scan_ms
                vbank.r = scan_ms

                ca = ch_a.bump(1)

                def exp_w(e, clip_v=clip_v, cc=cc, sp=sp, ca=ca):
                    e.wait_ge(s_v.h, clip_v)
                    e.activation(WT[:], WT[:], AF.Exp).then_inc(ch_a.h, 1)
                    e.wait_ge(ch_a.h, ca)
                    e.activation(WT[:, 0:1], WT[:, 0:1], AF.Copy,
                                 scale=FEFF[:, cc:cc + 1] if sp == 0 else 1.0).then_inc(s_a.h, 1)
                p.add("scalar", exp_w)
                s_a.bump(1)

                def stores(e, cc=cc, sp=sp, scan_ms=scan_ms, sig_ms=sig_ms):
                    wms(e, scan_ms)
                    wms(e, sig_ms)
                    e.dma_start(out=NUMLd[cc * 128:(cc + 1) * 128, sp * SPAN:(sp + 1) * SPAN],
                                in_=NUMT[:]).then_inc(s_stx.h, 16)
                    e.dma_start(out=DENLd[cc * 128:(cc + 1) * 128, sp * SPAN:(sp + 1) * SPAN],
                                in_=DENT[:]).then_inc(s_stx.h, 16)
                    e.dma_start(out=SIGd[cc * 128:(cc + 1) * 128, sp * SPAN:(sp + 1) * SPAN],
                                in_=SIGT[:]).then_inc(s_stx.h, 16)
                p.add("sync", stores)
                s_stx.bump(48)

        def store_sums(e, n=s_v.n):
            e.wait_ge(s_v.h, n)
            e.dma_start(out=SUMSd, in_=SUMS[:]).then_inc(s_st[1].h, 16)
        p.add("sync", store_sums)
        s_st[1].bump(16)
        es_b.close()

    def fin(e, a=s_st[0].n, b=s_st[1].n, c=s_stx.n):
        e.wait_ge(s_st[0].h, a)
        if b:
            e.wait_ge(s_st[1].h, b)
        if c:
            e.wait_ge(s_stx.h, c)
    p.add("sync", fin)

    p.emit(es.enter_context(nc.Block()))
    es.close()
    return nc


def stage_l2(inputs, l1_outs, core):
    a0, b0 = span_starts(core)
    toks = np.r_[a0:a0 + SPAN, b0:b0 + SPAN]
    x = np.asarray(inputs["L0_attn_a_x"])[0]
    xTc = np.ascontiguousarray(x[toks, :].T)

    kT_full = np.zeros((D, T), BF)
    vT_full = np.zeros((D, T), BF)
    for c in range(NCORES):
        ca, cb = span_starts(c)
        kT_full[:, ca:ca + SPAN] = l1_outs[c]["kT"][:, 0:SPAN]
        kT_full[:, cb:cb + SPAN] = l1_outs[c]["kT"][:, SPAN:2 * SPAN]
        vT_full[:, ca:ca + SPAN] = l1_outs[c]["vT"][:, 0:SPAN]
        vT_full[:, cb:cb + SPAN] = l1_outs[c]["vT"][:, SPAN:2 * SPAN]
    v_full = np.ascontiguousarray(vT_full.T)

    def kwin(sp):
        q0 = a0 if sp == 0 else b0
        blk0 = q0 // 128
        KW = np.zeros((D, KWIN * 128), BF)
        VWm = np.zeros((KWIN * 128, D), BF)
        for i in range(KWIN):
            g = blk0 + (i - 64)
            if 0 <= g < T // 128:
                KW[:, 128 * i:128 * (i + 1)] = kT_full[:, 128 * g:128 * (g + 1)]
                VWm[128 * i:128 * (i + 1), :] = v_full[128 * g:128 * (g + 1), :]
        return KW, VWm

    KWA, VWA = kwin(0)
    KWB, VWB = kwin(1)
    maskarr, _ = masks_and_offs()

    def bT(b):
        return np.ascontiguousarray(np.asarray(b).reshape(NCH, 128).T.astype(np.float32))

    U = np.asarray(inputs["L0_ffn_up_weight"])
    Dw = np.asarray(inputs["L0_ffn_down_weight"])
    return {
        "XT": xTc,
        "QT": l1_outs[core]["qT"],
        "KWA": KWA, "KWB": KWB, "VWA": VWA, "VWB": VWB,
        "BTAB": build_bias_table(core),
        "MASKS": maskarr,
        "Wo": np.asarray(inputs["L0_attn_a_Wo"]),
        "boT": bT(inputs["L0_attn_a_bo"]),
        "UTS": l1_outs[core]["UTS"],
        "DTS": l1_outs[core]["DTS"],
        "upsT": np.ascontiguousarray(np.asarray(inputs["L0_ffn_up_scale"])
                                     .reshape(DH // 128, 128).T.astype(np.float32)),
        "dsT": bT(inputs["L0_ffn_down_scale"]),
        "osc": np.broadcast_to(np.asarray(inputs["L0_ffn_output_scale"]).reshape(1, 1),
                               (128, 1)).copy(),
        "bWkT": np.ascontiguousarray(np.asarray(inputs["L0_attn_b_Wk"]).T),
        "bWvT": np.ascontiguousarray(np.asarray(inputs["L0_attn_b_Wv"]).T),
        "bWrT": np.ascontiguousarray(np.asarray(inputs["L0_attn_b_Wr"]).T),
        "tdT": np.ascontiguousarray(np.asarray(inputs["L0_attn_b_time_decay"]).T.astype(np.float32)),
        "tfT": np.ascontiguousarray(np.asarray(inputs["L0_attn_b_time_first"]).T.astype(np.float32)),
        "FIRSTIND": np.full((128, 1), 1.0 if core == 0 else 0.0, np.float32),
    }



def build_l3():
    nc = bass.Bass("TRN2", target_bir_lowering=False)
    dram = lambda n, shape, dt, kind="ExternalInput": nc.dram_tensor(n, shape, dt, kind=kind).ap()
    X2Td = dram("X2T", [D, TPC], F32)
    NUMLd = dram("NUML", [D, TPC], F32)
    DENLd = dram("DENL", [D, TPC], F32)
    SIGd = dram("SIG", [D, TPC], F32)
    SUMNd = dram("SUMN", [128, NCH * 16], F32)
    SUMDd = dram("SUMD", [128, NCH * 16], F32)
    SELAd = dram("SELA", [128, 16], F32)
    SELBd = dram("SELB", [128, 16], F32)
    bWoTd = dram("bWoT", [D, D], F32)
    UTSd = dram("UTS", [D, DH], BF16)
    DTSd = dram("DTS", [DH, D], BF16)
    upsTd = dram("upsT", [128, DH // 128], F32)
    dsTd = dram("dsT", [128, NCH], F32)
    oscd = dram("osc", [128, 1], F32)
    tdTd = dram("tdT", [128, NH], F32)
    XOUTd = dram("XOUT", [D, TPC], F32, "ExternalOutput")

    es = ExitStack()
    sb = lambda name, shape, dt: es.enter_context(nc.sbuf_tensor(name, shape, dt))

    X = sb("X", [128, NCH * TPC], F32)
    XN = sb("XN", [128, NCH * SPAN], BF16)
    AR = sb("AR", [128, 2 * NCH * TPC], BF16)
    WS = None
    ONESC = sb("ONESC", [128, 1], BF16)
    ONESR = sb("ONESR", [1, 128], F32)
    EPS1 = sb("EPS1", [1, 1], F32)
    EPS6 = sb("EPS6", [128, 1], F32)
    STATR = sb("STATR", [1, 2 * SPAN], F32)
    ABrow = sb("ABrow", [1, 2 * SPAN], F32)
    ABC = sb("ABC", [128, 2 * SPAN], F32)
    TMP = sb("TMP", [128, SPAN], F32)
    XBF2 = [sb(f"XBF2{i}", [128, SPAN], BF16) for i in range(2)]
    SQ2 = [sb(f"SQ2{i}", [128, SPAN], BF16) for i in range(2)]
    UPS = sb("UPS", [128, DH // 128], F32)
    DST = sb("DST", [128, NCH], F32)
    OSC = sb("OSC", [128, 1], F32)
    DSOS = sb("DSOS", [128, NCH], F32)
    ZB = sb("ZB", [128, NCH], F32)
    TD = sb("TD", [128, NH], F32)
    E1 = sb("E1", [128, NH], F32)
    NE1 = sb("NE1", [128, NH], F32)
    D512 = sb("D512", [128, NH], F32)
    es_w = ExitStack()
    sbw = lambda name, shape, dt: es_w.enter_context(nc.sbuf_tensor(name, shape, dt))
    SUMN = sbw("SUMNs", [128, NCH * 16], F32)
    SUMD = sbw("SUMDs", [128, NCH * 16], F32)
    SELA = sbw("SELAs", [128, 16], F32)
    SELB = sbw("SELBs", [128, 16], F32)
    CARN = sbw("CARN", [128, NCH * 16], F32)
    CARD = sbw("CARD", [128, NCH * 16], F32)
    CAN = [sbw(f"CAN{i}", [128, NCH], F32) for i in range(2)]
    CAD = [sbw(f"CAD{i}", [128, NCH], F32) for i in range(2)]
    SELT = sbw("SELT", [128, 16], F32)
    IOTA1 = sbw("IOTA1", [128, SPAN], F32)
    POW = sbw("POW", [128, SPAN], F32)
    TMPP = sbw("TMPP", [128, SPAN], F32)
    NUMT = [sbw(f"NUMT{i}", [128, SPAN], F32) for i in range(2)]
    DENT = [sbw(f"DENT{i}", [128, SPAN], F32) for i in range(2)]
    SGT = [sbw(f"SGT{i}", [128, SPAN], F32) for i in range(2)]
    RECIP = [sbw(f"RECIP{i}", [128, SPAN], F32) for i in range(2)]
    WKVF = [sbw(f"WKVF{i}", [128, SPAN], F32) for i in range(2)]

    PS = [es.enter_context(nc.psum_tensor(f"PS{i}", [128, SPAN], F32)) for i in range(8)]
    banks = [Bank(t) for t in PS]

    sems = {k: Sem(es.enter_context(nc.semaphore("s_" + k)))
            for k in ("ld", "ldx0", "ldx1", "v", "a", "ts", "g", "wl0", "wl1", "wls0", "wls1",
                      "st0", "chv", "cha")}
    s_ld, s_v, s_a, s_g = sems["ld"], sems["v"], sems["a"], sems["g"]
    s_ldx = [sems["ldx0"], sems["ldx1"]]
    s_ts = sems["ts"]
    ch_v, ch_a = sems["chv"], sems["cha"]
    s_st = sems["st0"]

    p = Plan()
    mm_done = {}
    ws_last = {0: None, 1: None}
    ctx = {'core': (p, banks, sems, X, XN, AR, WS, ONESC, ONESR, EPS1, STATR, ABrow, ABC, TMP,
                    XBF2, SQ2, UPS, DSOS, UTSd, DTSd, mm_done, ws_last)}

    def loads(e):
        e.dma_start(out=X[:].rearrange("p (c t) -> p c t", c=NCH),
                    in_=X2Td.rearrange("(c p) t -> p c t", c=NCH)).then_inc(s_ld.h, 16)
        e.dma_start(out=SUMN[:], in_=SUMNd).then_inc(s_ld.h, 16)
        e.dma_start(out=SUMD[:], in_=SUMDd).then_inc(s_ld.h, 16)
        e.dma_start(out=SELA[:], in_=SELAd).then_inc(s_ld.h, 16)
        e.dma_start(out=SELB[:], in_=SELBd).then_inc(s_ld.h, 16)
        e.dma_start(out=UPS[:], in_=upsTd).then_inc(s_ld.h, 16)
        e.dma_start(out=DST[:], in_=dsTd).then_inc(s_ld.h, 16)
        e.dma_start(out=OSC[:], in_=oscd).then_inc(s_ld.h, 16)
        e.dma_start(out=TD[:], in_=tdTd).then_inc(s_ld.h, 16)
    p.add("sync", loads)
    loads_done = s_ld.bump(16 * 9)

    def consts(e):
        e.memset(ONESC[:], 1.0)
        e.memset(ONESR[:], 1.0)
        e.memset(EPS1[:], LN_EPS)
        e.memset(EPS6[:], 1e-6)
        e.memset(ZB[:], 0.0)
        e.memset(CARN[:], 0.0)
        e.memset(CARD[:], 0.0)
        e.iota(IOTA1[:], [[1, SPAN]], base=1, channel_multiplier=0,
               allow_small_or_imprecise_dtypes=True).then_inc(s_g.h, 1)
    p.add("gpsimd", consts)
    consts_done = s_g.bump()

    for eng in ("tensor", "scalar", "vector"):
        def gate(e, _ld=loads_done, _g=consts_done):
            e.wait_ge(s_ld.h, _ld)
            e.wait_ge(s_g.h, _g)
        p.add(eng, gate)

    ca1 = ch_a.bump(1)

    def decay_f(e, ca1=ca1):
        e.activation(E1[:], TD[:], AF.Exp).then_inc(ch_a.h, 1)
        e.wait_ge(ch_a.h, ca1)
        e.activation(D512[:], E1[:], AF.Exp, scale=-512.0).then_inc(s_a.h, 1)
    p.add("scalar", decay_f)
    e1_done = s_a.bump(1)
    cv1 = ch_v.bump(1)

    def ne1_f(e, e1_done=e1_done, cv1=cv1):
        e.wait_ge(s_a.h, e1_done)
        e.tensor_scalar(NE1[:], E1[:], -1.0, None, AluOpType.mult).then_inc(ch_v.h, 1)
    p.add("vector", ne1_f)
    ne1_done = (ch_v, cv1)

    carry_c0 = ch_v.n

    def carry_f(e, e1_done=e1_done, c0=carry_c0):
        e.wait_ge(s_a.h, e1_done)
        step = 0
        for s_ in range(1, 16):
            if s_ > 1:
                e.wait_ge(ch_v.h, c0 + step)
            for cc in range(NCH):
                i0 = cc * 16 + s_
                e.scalar_tensor_tensor(CARN[:, i0:i0 + 1], CARN[:, i0 - 1:i0], D512[:, cc:cc + 1],
                                       SUMN[:, i0 - 1:i0], AluOpType.mult, AluOpType.add)
                inst = e.scalar_tensor_tensor(CARD[:, i0:i0 + 1], CARD[:, i0 - 1:i0], D512[:, cc:cc + 1],
                                              SUMD[:, i0 - 1:i0], AluOpType.mult, AluOpType.add)
            step += 1
            inst.then_inc(ch_v.h, 1)
        e.wait_ge(ch_v.h, c0 + 15)
        k = c0 + 15
        for spi, SEL in ((0, SELA), (1, SELB)):
            for car, dst in ((CARN, CAN[spi]), (CARD, CAD[spi])):
                for cc in range(NCH):
                    e.wait_ge(ch_v.h, k)
                    e.tensor_tensor(SELT[:], car[:, cc * 16:(cc + 1) * 16], SEL[:],
                                    AluOpType.mult).then_inc(ch_v.h, 1)
                    k += 1
                    e.wait_ge(ch_v.h, k)
                    e.reduce_sum(dst[:, cc:cc + 1], SELT[:], axis=mybir.AxisListType.X
                                 ).then_inc(ch_v.h, 1)
                    k += 1
        e.nop().then_inc(s_v.h, 1)
    p.add("vector", carry_f)
    ch_v.bump(15 + 4 * NCH * 2)
    carry_done = s_v.bump(1)

    cdsos = ch_v.bump(1)

    def dsos_prep(e, cdsos=cdsos):
        e.tensor_scalar(DSOS[:], DST[:], OSC[:, 0:1], None, AluOpType.mult).then_inc(ch_v.h, 1)
    p.add("vector", dsos_prep)

    rw_ms = None
    rw_hist = {0: None, 1: None}
    prev_pow_done = 0
    prev_corr1 = None
    for cc in range(NCH):
        def pow_ts(e, cc=cc, ne1_done=ne1_done, prev_pow_done=prev_pow_done):
            wms(e, ne1_done)
            if prev_pow_done:
                e.wait_ge(s_a.h, prev_pow_done)
            e.tensor_scalar(TMPP[:], IOTA1[:], NE1[:, cc:cc + 1], None,
                            AluOpType.mult).then_inc(s_v.h, 1)
        p.add("vector", pow_ts)
        tspw = s_v.bump(1)

        def exp_pow(e, tspw=tspw, prev_corr1=prev_corr1):
            e.wait_ge(s_v.h, tspw)
            wms(e, prev_corr1)
            e.activation(POW[:], TMPP[:], AF.Exp).then_inc(s_a.h, 1)
        p.add("scalar", exp_pow)
        pow_done = s_a.bump(1)
        prev_pow_done = pow_done

        for sp in range(2):
            wpar = (cc * 2 + sp) % 2
            prev_use = rw_hist[wpar]

            def tload(e, cc=cc, sp=sp, prev_use=prev_use, wpar=wpar):
                wms(e, prev_use)
                e.dma_start(out=NUMT[wpar][:], in_=NUMLd[cc * 128:(cc + 1) * 128,
                                               sp * SPAN:(sp + 1) * SPAN]).then_inc(s_ldx[wpar].h, 16)
                e.dma_start(out=DENT[wpar][:], in_=DENLd[cc * 128:(cc + 1) * 128,
                                               sp * SPAN:(sp + 1) * SPAN]).then_inc(s_ldx[wpar].h, 16)
                e.dma_start(out=SGT[wpar][:], in_=SIGd[cc * 128:(cc + 1) * 128,
                                              sp * SPAN:(sp + 1) * SPAN]).then_inc(s_ldx[wpar].h, 16)
            p.add("sync", tload)
            tl_done = s_ldx[wpar].bump(48)
            cfix = ch_v.bump(1)

            def correct(e, cc=cc, sp=sp, tl_done=tl_done, pow_done=pow_done,
                        carry_done=carry_done, cfix=cfix, wpar=wpar):
                e.wait_ge(s_ldx[wpar].h, tl_done)
                e.wait_ge(s_a.h, pow_done)
                e.wait_ge(s_v.h, carry_done)
                e.scalar_tensor_tensor(NUMT[wpar][:], POW[:], CAN[sp][:, cc:cc + 1], NUMT[wpar][:],
                                       AluOpType.mult, AluOpType.add).then_inc(ch_v.h, 1)
                e.scalar_tensor_tensor(DENT[wpar][:], POW[:], CAD[sp][:, cc:cc + 1], DENT[wpar][:],
                                       AluOpType.mult, AluOpType.add).then_inc(s_v.h, 1)
            p.add("vector", correct)
            corr_done = s_v.bump(1)
            ca2 = ch_a.bump(1)

            def recip_f(e, corr_done=corr_done, ca2=ca2, wpar=wpar):
                e.wait_ge(s_v.h, corr_done)
                e.activation(RECIP[wpar][:], DENT[wpar][:], AF.Ln,
                             bias=EPS6[:, 0:1]).then_inc(ch_a.h, 1)
                e.wait_ge(ch_a.h, ca2)
                e.activation(RECIP[wpar][:], RECIP[wpar][:], AF.Exp, scale=-1.0).then_inc(s_a.h, 1)
            p.add("scalar", recip_f)
            recip_done = s_a.bump(1)
            cwv = ch_v.bump(1)

            def rwkv_f(e, cc=cc, sp=sp, recip_done=recip_done, cwv=cwv, wpar=wpar):
                e.wait_ge(s_a.h, recip_done)
                e.tensor_tensor(WKVF[wpar][:], NUMT[wpar][:], RECIP[wpar][:],
                                AluOpType.mult).then_inc(ch_v.h, 1)
                e.wait_ge(ch_v.h, cwv)
                e.tensor_tensor(AR[:, cc * TPC + sp * SPAN:cc * TPC + (sp + 1) * SPAN],
                                WKVF[wpar][:], SGT[wpar][:], AluOpType.mult).then_inc(s_v.h, 1)
            p.add("vector", rwkv_f)
            rw_ms = (s_v, s_v.bump(1))
            rw_hist[wpar] = rw_ms
            if sp == 1:
                prev_corr1 = (s_v, corr_done)
    rw_all = rw_ms
    es_w.close()
    WSF = [es.enter_context(nc.sbuf_tensor(f"WSF{i}", [128, 8192], BF16)) for i in range(2)]
    ctx['WSF'] = WSF
    ctx['wsf_last'] = {0: rw_all, 1: rw_all}

    x3_ms = emit_proj_to_x(ctx, 0, bWoTd, ZB, rw_all, "bwo")
    xn_a = emit_ln2(ctx, 0, x3_ms)
    ffa = emit_ffn_span(ctx, 0, xn_a, "g0")
    xn_b = emit_ln2(ctx, 1, x3_ms)
    ffb = emit_ffn_span(ctx, 1, xn_b, "g1")

    def store_x(e, ffb=ffb):
        wms(e, ffb)
        e.dma_start(out=XOUTd.rearrange("(c p) t -> p c t", c=NCH),
                    in_=X[:].rearrange("p (c t) -> p c t", c=NCH)).then_inc(s_st.h, 16)
    p.add("sync", store_x)
    s_st.bump(16)

    def fin(e, a=s_st.n):
        e.wait_ge(s_st.h, a)
    p.add("sync", fin)

    p.emit(es.enter_context(nc.Block()))
    es.close()
    return nc


def stage_l3(inputs, l1_outs, l2_outs, core):
    a0, b0 = span_starts(core)
    SUMN = np.zeros((128, NCH * 16), np.float32)
    SUMD = np.zeros((128, NCH * 16), np.float32)
    for s_ in range(16):
        if s_ < 8:
            oc, osp = s_, 0
        else:
            oc, osp = 15 - s_, 1
        sm = l2_outs[oc]["SUMS"]
        for cc in range(NCH):
            SUMN[:, cc * 16 + s_] = sm[:, cc * 4 + 2 * osp]
            SUMD[:, cc * 16 + s_] = sm[:, cc * 4 + 2 * osp + 1]
    SELA = np.zeros((128, 16), np.float32)
    SELB = np.zeros((128, 16), np.float32)
    SELA[:, core] = 1.0
    SELB[:, 15 - core] = 1.0

    def bT(b):
        return np.ascontiguousarray(np.asarray(b).reshape(NCH, 128).T.astype(np.float32))

    return {
        "X2T": l2_outs[core]["X2T"],
        "NUML": l2_outs[core]["NUML"],
        "DENL": l2_outs[core]["DENL"],
        "SIG": l2_outs[core]["SIG"],
        "SUMN": SUMN, "SUMD": SUMD, "SELA": SELA, "SELB": SELB,
        "bWoT": np.ascontiguousarray(np.asarray(inputs["L0_attn_b_Wo"]).T),
        "UTS": l1_outs[core]["UTS"],
        "DTS": l1_outs[core]["DTS"],
        "upsT": np.ascontiguousarray(np.asarray(inputs["L0_ffn_up_scale"])
                                     .reshape(DH // 128, 128).T.astype(np.float32)),
        "dsT": bT(inputs["L0_ffn_down_scale"]),
        "osc": np.broadcast_to(np.asarray(inputs["L0_ffn_output_scale"]).reshape(1, 1),
                               (128, 1)).copy(),
        "tdT": np.ascontiguousarray(np.asarray(inputs["L0_attn_b_time_decay"]).T.astype(np.float32)),
    }


_NC_CACHE = {}


def kernel(**inputs):
    if "l1" not in _NC_CACHE:
        _NC_CACHE["l1"] = build_l1()
        _NC_CACHE["l2"] = build_l2("full")
        _NC_CACHE["l3"] = build_l3()
    cores = list(range(NCORES))
    r1 = run_bass_kernel_spmd(_NC_CACHE["l1"], stage_l1(inputs), core_ids=cores)
    l1_outs = r1.results
    r2 = run_bass_kernel_spmd(_NC_CACHE["l2"],
                              [stage_l2(inputs, l1_outs, c) for c in cores], core_ids=cores)
    l2_outs = r2.results
    r3 = run_bass_kernel_spmd(_NC_CACHE["l3"],
                              [stage_l3(inputs, l1_outs, l2_outs, c) for c in cores],
                              core_ids=cores)
    out = np.zeros((T, D), np.float32)
    for c in cores:
        a0, b0 = span_starts(c)
        xo = r3.results[c]["XOUT"]
        out[a0:a0 + SPAN] = xo[:, 0:SPAN].T
        out[b0:b0 + SPAN] = xo[:, SPAN:2 * SPAN].T
    return out[None, :, :]
